# version 06
# baseline (speedup 1.0000x reference)
import numpy as np
import ml_dtypes
from contextlib import ExitStack
import concourse.bass as bass
import concourse.mybir as mybir
from concourse.bass_utils import run_bass_kernel_spmd

F32 = mybir.dt.float32
BF16 = mybir.dt.bfloat16
AF = mybir.ActivationFunctionType
ALU = mybir.AluOpType
AX = mybir.AxisListType
BF = ml_dtypes.bfloat16

T = 2048
D = 1024
NT = 16
NF = 8
FH = 8
FDH = 64
WIN = 512
HID = 2730
NH = 22
QK_SCALE = 8.0
LH = 16

STAGE = "full"


def rot_tables():
    half = np.arange(0, FDH, 2, dtype=np.float64)
    inv_freq = 1.0 / 10000.0 ** (half / FDH)
    t = np.arange(T, dtype=np.float64)
    freqs = t[:, None] * inv_freq[None, :]
    freqs64 = np.concatenate([freqs, freqs], axis=-1)
    base = (half + 0.4 * FDH) / (1.4 * FDH)
    power = (t - T // 2) / float(WIN // 2)
    scale = base[None, :] ** power[:, None]
    scale64 = np.concatenate([scale, scale], axis=-1)
    cos, sin = np.cos(freqs64), np.sin(freqs64)
    sign = np.where(np.arange(FDH) < FDH // 2, -1.0, 1.0)[None, :]
    tabs = [QK_SCALE * cos * scale64, QK_SCALE * sign * sin * scale64,
            cos / scale64, sign * sin / scale64]
    out = []
    for a in tabs:
        out.append(np.ascontiguousarray(
            a.reshape(NT, 128, FDH).transpose(1, 0, 2).reshape(128, NT * FDH)
        ).astype(BF))
    return out

def mask_t_ffn():
    out = np.zeros((128, 5 * 128), np.float32)
    u = np.arange(128)[:, None]
    rr = np.arange(128)[None, :]
    for c in range(5):
        qk = 128 * (4 - c) + rr - u
        out[:, c * 128:(c + 1) * 128] = ((qk >= 0) & (qk <= WIN)).astype(np.float32)
    return out.astype(BF)

def mask_t_lin():
    k = np.arange(128)[:, None]
    q = np.arange(128)[None, :]
    return (k <= q).astype(np.float32).astype(BF)


class Buf:
    __slots__ = ("name", "w", "r")
    def __init__(self, name, parents=()):
        self.name = name
        self.w = None
        self.r = []
        for p in parents:
            if p.w is not None:
                self.r.append(p.w)
            self.r.extend(p.r)

ENGS = ("sync", "tensor", "vector", "scalar", "gpsimd")
QUEUES = {"qs": "sync", "qp": "gpsimd", "qa": "scalar"}

class Rec:
    def __init__(self, nc, es):
        self.nc = nc
        self.ops = {e: [] for e in ENGS}
        self.need_inc = {e: set() for e in ENGS}
        self.dmacnt = {q: 0 for q in QUEUES}
        self.emitted = {}
        self.sems = {}
        for e in ENGS:
            self.sems[e] = es.enter_context(nc.semaphore("sem_" + e))
        for q in QUEUES:
            self.sems[q] = es.enter_context(nc.semaphore("sem_" + q))

    def _record(self, eng, fn, reads, writes, queue):
        import sys
        fr = sys._getframe(2)
        label = f"{fr.f_code.co_name}:{fr.f_lineno}"
        deps = []
        for b in reads:
            if b.w is not None:
                deps.append(b.w)
        for b in writes:
            if b.w is not None:
                deps.append(b.w)
            deps.extend(b.r)
        fdeps = {}
        for (d0, d1) in deps:
            if d0 == eng and eng == "tensor":
                continue
            fdeps[d0] = max(fdeps.get(d0, 0), d1)
        fdeps = sorted(fdeps.items())
        idx = len(self.ops[eng])
        self.ops[eng].append((fn, fdeps, queue, label))
        if queue is None:
            tok = (eng, idx)
        else:
            self.dmacnt[queue] += 16
            tok = (queue, self.dmacnt[queue])
        for d0, d1 in fdeps:
            if d0 not in QUEUES:
                self.need_inc[d0].add(d1)
        for b in reads:
            b.r.append(tok)
        for b in writes:
            b.w = tok
            b.r = []
        return tok

    def op(self, eng, fn, reads=(), writes=()):
        return self._record(eng, fn, list(reads), list(writes), None)

    def dma(self, queue, fn, reads=(), writes=()):
        return self._record(QUEUES[queue], fn, list(reads), list(writes), queue)

    def finalize_and_emit(self):
        mil = {}
        for e in ENGS:
            mil[e] = {idx: k + 1 for k, idx in enumerate(sorted(self.need_inc[e]))}
        rec = self

        def mkreplay(eng):
            def replay(e):
                hw = {}
                qcnt = {q: 0 for q in QUEUES}
                for idx, (fn, fdeps, queue, label) in enumerate(rec.ops[eng]):
                    if queue is not None:
                        prev = qcnt[queue] * 16
                        qcnt[queue] += 1
                        if prev > 0 and hw.get(queue, 0) < prev:
                            hw[queue] = prev
                            e.wait_ge(rec.sems[queue], prev)
                    for d0, d1 in fdeps:
                        val = d1 if d0 in QUEUES else mil[d0][d1]
                        if hw.get(d0, 0) >= val:
                            continue
                        hw[d0] = val
                        e.wait_ge(rec.sems[d0], val)
                    ins = fn(e)
                    rec.emitted.setdefault(eng, []).append((ins.ins.name, idx))
                    if queue is not None:
                        ins.then_inc(rec.sems[queue], 16)
                    elif idx in mil[eng]:
                        ins.then_inc(rec.sems[eng], 1)
                if eng == "sync":
                    for q in QUEUES:
                        if rec.dmacnt[q] > 0 and hw.get(q, 0) < rec.dmacnt[q]:
                            e.wait_ge(rec.sems[q], rec.dmacnt[q])
            return replay

        with self.nc.Block() as blk:
            blk.sync(mkreplay("sync"))
            blk.tensor(mkreplay("tensor"))
            blk.vector(mkreplay("vector"))
            blk.scalar(mkreplay("scalar"))
            blk.gpsimd(mkreplay("gpsimd"))

    def nops(self):
        return {e: len(self.ops[e]) for e in ENGS}

class Arena:
    def __init__(self, nc, base, top):
        self.nc = nc
        self.base = base
        self.top = top
        self.free_iv = [(base, top)]
        self.live = {}
        self.hist = []
        self.count = 0

    def alloc(self, name, shape, dtype, nbufs=1):
        itemsize = np.dtype(mybir.dt.np(dtype)).itemsize
        size = int(np.prod(shape[1:])) * itemsize
        size = (size + 31) & ~31
        off = None
        for i, (s, e) in enumerate(self.free_iv):
            if e - s >= size:
                off = s
                if s + size < e:
                    self.free_iv[i] = (s + size, e)
                else:
                    del self.free_iv[i]
                break
        if off is None:
            raise MemoryError(f"arena overflow: {name} needs {size}; free={self.free_iv}")
        parents = []
        for (fo, fs, fb) in self.hist:
            if fo < off + size and off < fo + fs:
                parents.extend(fb)
        self.count += 1
        t = self.nc.alloc_sbuf_tensor_at(f"ar{self.count}_{name}", list(shape), dtype, offset=off)
        bufs = [Buf(f"{name}[{i}]", parents) for i in range(nbufs)]
        self.live[t.name] = (off, size, bufs)
        self.log = getattr(self, "log", [])
        self.log.append(("A", t.name, off, size))
        return t, bufs

    def free(self, *tensors):
        for t in tensors:
            off, size, bufs = self.live.pop(t.name)
            self.log = getattr(self, "log", [])
            self.log.append(("F", t.name, off, size))
            self.hist.append((off, size, bufs))
            self.free_iv.append((off, off + size))
        self._merge()

    def _merge(self):
        self.free_iv.sort()
        merged = []
        for s, e in self.free_iv:
            if merged and s <= merged[-1][1]:
                merged[-1] = (merged[-1][0], max(merged[-1][1], e))
            else:
                merged.append((s, e))
        self.free_iv = merged

    def lend(self, off, size, seed_bufs):
        self.hist.append((off, size, list(seed_bufs)))
        self.free_iv.append((off, off + size))
        self._merge()

    def reclaim(self, off, size):
        out = []
        niv = []
        for s, e in self.free_iv:
            if s < off + size and off < e:
                assert s <= off and e >= off + size, \
                    f"reclaim range busy: {(off, size)} vs free {(s, e)}"
                if s < off:
                    niv.append((s, off))
                if e > off + size:
                    niv.append((off + size, e))
            else:
                niv.append((s, e))
        self.free_iv = niv
        keep = []
        for (fo, fs, fb) in self.hist:
            if fo < off + size and off < fo + fs:
                out.extend(fb)
            else:
                keep.append((fo, fs, fb))
        self.hist = keep
        return out


class MK:
    def __init__(self, stage=STAGE):
        self.stage = stage
        self.es = ExitStack()
        self.nc = bass.Bass("TRN2", target_bir_lowering=False)
        self.input_names = []
        self.build()

    def psum_phase(self, es2, specs):
        tensors, bufs = {}, {}
        for name, shape in specs.items():
            self._psum_uid += 1
            t = es2.enter_context(self.nc.psum_tensor(f"pp{self._psum_uid}_{name}", list(shape), F32))
            tensors[name] = t
            bufs[name] = Buf(name + str(self._psum_uid), self.psum_prev)
        return tensors, bufs

    def psum_phase_end(self, bufs):
        self.psum_prev = list(bufs.values())

    def din(self, name, shape, dtype):
        self.input_names.append(name)
        return self.nc.dram_tensor(name, list(shape), dtype, kind="ExternalInput").ap()

    def sb(self, name, shape, dtype, nbufs=1):
        t = self.nc.alloc_sbuf_tensor(name, list(shape), dtype)
        return t, [Buf(f"{name}[{i}]") for i in range(nbufs)]

    def build(self):
        nc, es = self.nc, self.es
        r = self.rec = Rec(nc, es)
        st = self.stage

        x_in = self.din("x", [T, D], F32)
        aw_qkv = self.din("aw_qkv", [3072, 1024], F32)
        ab_qkv = self.din("ab_qkv", [3072], F32)
        aw_out = self.din("aw_out", [1024, 1024], F32)
        ab_out = self.din("ab_out", [1024], F32)
        fw_qkv = self.din("fw_qkv", [2, 1024, 1536], F32)
        fqs = self.din("fqs", [2, 64], F32)
        fks = self.din("fks", [2, 64], F32)
        fwo = self.din("fwo", [2, 512, 1024], F32)
        flng = self.din("flng", [2, 1024], F32)
        flnb = self.din("flnb", [2, 1024], F32)
        fw1 = self.din("fw1", [2, 1024, 5460], F32)
        fb1 = self.din("fb1", [2, 5460], F32)
        fw2 = self.din("fw2", [2, 2730, 1024], F32)
        fb2 = self.din("fb2", [2, 1024], F32)
        c_ident = self.din("c_ident", [128, 128], BF16)
        c_identf = self.din("c_identf", [128, 128], F32)
        c_maskf = self.din("c_maskf", [128, 640], BF16)
        c_maskl = self.din("c_maskl", [128, 128], BF16)
        c_csq = self.din("c_csq", [128, NT * 64], BF16)
        c_sxq = self.din("c_sxq", [128, NT * 64], BF16)
        c_csk = self.din("c_csk", [128, NT * 64], BF16)
        c_sxk = self.din("c_sxk", [128, NT * 64], BF16)
        y_out = nc.dram_tensor("y", [T, D], F32, kind="ExternalOutput").ap()
        self._y = y_out
        x2_spill = nc.dram_tensor("x2spill", [T, D], F32, kind="Internal").ap()

        slotA, bufA = self.sb("slotA", [128, NT * D], F32, NT)
        slotB, bufB = self.sb("slotB", [128, NT * D], F32, NT)
        ident, bident = self.sb("ident", [128, 128], BF16)
        identf, bidentf = self.sb("identf", [128, 128], F32)
        maskf, bmaskf = self.sb("maskf", [128, 640], BF16)
        maskl, bmaskl = self.sb("maskl", [128, 128], BF16)
        onesb, bonesb = self.sb("onesb", [1, 128], BF16)
        abqkv_pp, b_abqkv = self.sb("abqkv_pp", [128, 24], F32)
        lg_pp, b_lg = self.sb("lg_pp", [128, 16], F32)
        lb_pp, b_lb = self.sb("lb_pp", [128, 16], F32)
        b1a_pp, b_b1a = self.sb("b1a_pp", [128, 2 * NH], F32)
        b1g_pp, b_b1g = self.sb("b1g_pp", [128, 2 * NH], F32)
        b2_pp, b_b2 = self.sb("b2_pp", [128, 16], F32)
        qsrow, b_qsrow = self.sb("qsrow", [1, 128], F32)
        ksrow, b_ksrow = self.sb("ksrow", [1, 128], F32)
        stat, b_stat = self.sb("stat", [128, 128], F32, 48)
        scr, b_scr = self.sb("scr", [128, 1024], F32)

        PT = es.enter_context(nc.psum_tensor("PsT", [128, 1024], BF16))
        pb = {"PT": Buf("PT")}
        self.pb = pb
        self.P = dict(PT=PT)
        self.psum_prev = []
        self._psum_uid = 0
        self.done = False

        ccol_t, b_ccol = self.sb("ccol", [128, 8], F32)
        self._ccols = {}
        def constcol(val):
            if val not in self._ccols:
                i = len(self._ccols)
                assert i < 8
                ap = ccol_t[:, i:i + 1]
                r.op("vector", lambda e, a=ap, v=float(val): e.memset(a, v),
                     writes=[b_ccol[0]])
                self._ccols[val] = ap
            return self._ccols[val]
        self.constcol = constcol
        self.b_ccol = b_ccol[0]

        def d(q, out_ap, in_ap, wbufs, rbufs=(), slow=False):
            def fn(e, o=out_ap, i=in_ap, s=slow):
                if s:
                    with nc.allow_non_contiguous_dma("small"):
                        return e.dma_start(out=o, in_=i, allow_slow_non_contiguous=True)
                return e.dma_start(out=o, in_=i)
            return r.dma(q, fn, reads=list(rbufs), writes=list(wbufs))

        ROP = r.op

        d("qs", ident[:], c_ident, [bident[0]])
        d("qs", identf[:], c_identf, [bidentf[0]])
        d("qs", maskf[:], c_maskf, [bmaskf[0]])
        d("qs", maskl[:], c_maskl, [bmaskl[0]])
        ROP("vector", lambda e: e.memset(onesb[:], 1.0), writes=[bonesb[0]])
        ROP("vector", lambda e: e.memset(b1a_pp[:], 0.0), writes=[b_b1a[0]])
        ROP("vector", lambda e: e.memset(b1g_pp[:], 0.0), writes=[b_b1g[0]])
        d("qp", abqkv_pp[:], ab_qkv.rearrange("(c p) -> p c", p=128), [b_abqkv[0]], slow=True)
        for l in range(2):
            d("qp", lg_pp[:, l * 8:(l + 1) * 8], flng[l].rearrange("(c p) -> p c", p=128), [b_lg[0]], slow=True)
            d("qp", lb_pp[:, l * 8:(l + 1) * 8], flnb[l].rearrange("(c p) -> p c", p=128), [b_lb[0]], slow=True)
            d("qp", b2_pp[:, l * 8:(l + 1) * 8], fb2[l].rearrange("(c p) -> p c", p=128), [b_b2[0]], slow=True)
            d("qp", b1a_pp[:, l * NH:l * NH + 21], fb1[l, 0:2688].rearrange("(c p) -> p c", p=128), [b_b1a[0]], slow=True)
            d("qp", b1a_pp[0:42, l * NH + 21:l * NH + 22], fb1[l, 2688:2730].unsqueeze(1), [b_b1a[0]], slow=True)
            d("qp", b1g_pp[:, l * NH:l * NH + 21], fb1[l, 2730:5418].rearrange("(c p) -> p c", p=128), [b_b1g[0]], slow=True)
            d("qp", b1g_pp[0:42, l * NH + 21:l * NH + 22], fb1[l, 5418:5460].unsqueeze(1), [b_b1g[0]], slow=True)
            d("qs", qsrow[:, l * 64:(l + 1) * 64], fqs[l].unsqueeze(0), [b_qsrow[0]])
            d("qs", ksrow[:, l * 64:(l + 1) * 64], fks[l].unsqueeze(0), [b_ksrow[0]])

        def ln_full_tile(src_ap, src_bufs, dst_ap, dst_bufs, tslot, eps=1e-5):
            sc = stat[:, tslot * 4 + 0:tslot * 4 + 1]
            qc = stat[:, tslot * 4 + 1:tslot * 4 + 2]
            nm = stat[:, tslot * 4 + 2:tslot * 4 + 3]
            rs = stat[:, tslot * 4 + 3:tslot * 4 + 4]
            sb_ = b_stat[tslot]
            ROP("scalar", lambda e: e.activation(scr[:], src_ap, AF.Square),
                reads=src_bufs, writes=[b_scr[0]])
            ROP("vector", lambda e: e.reduce_sum(qc, scr[:], axis=AX.X),
                reads=[b_scr[0]], writes=[sb_])
            ROP("vector", lambda e: e.reduce_sum(sc, src_ap, axis=AX.X),
                reads=src_bufs, writes=[sb_])
            ROP("vector", lambda e: e.tensor_scalar(nm, sc, -1.0 / D, None, ALU.mult),
                reads=[sb_], writes=[sb_])
            ROP("vector", lambda e: e.tensor_mul(rs, nm, nm), reads=[sb_], writes=[sb_])
            ROP("vector", lambda e: e.scalar_tensor_tensor(qc, qc, 1.0 / D, rs, ALU.mult, ALU.subtract),
                reads=[sb_], writes=[sb_])
            epsc = self.constcol(eps)
            nhalf = self.constcol(-0.5)
            ROP("scalar", lambda e: e.activation(sc, qc, AF.Ln, bias=epsc),
                reads=[sb_, b_ccol[0]], writes=[sb_])
            ROP("scalar", lambda e: e.activation(rs, sc, AF.Exp, scale=nhalf),
                reads=[sb_, b_ccol[0]], writes=[sb_])
            ROP("vector", lambda e: e.tensor_scalar(dst_ap, src_ap, nm, rs, ALU.add, ALU.mult),
                reads=list(src_bufs) + [sb_], writes=dst_bufs)

        self.arena = Arena(nc, nc.sbuf_base, nc.sbuf_top)
        xio, b_xio = self.arena.alloc("xio0", [128, 3, D], F32, 3)
        for t in range(NT):
            xb = b_xio[t % 3]
            xap = xio[:, t % 3, :]
            d("qs", xap, x_in[t * 128:(t + 1) * 128, :], [xb])
            ln_full_tile(xap, [xb], slotA[:, t * D:(t + 1) * D], [bufA[t]], t % 16)
        self.arena.free(xio)

        self.ctx = dict(locals())

        if st == "ln1":
            self.emit_out(slotA, bufA)
            r.finalize_and_emit()
            return

        if st in ("lin_dbg", "linq_dbg", "lins_dbg", "lins_dbg2", "lins_dbg3"):
            self.linattn(slotA, bufA, slotB, bufB)
            if st in ("linq_dbg", "lins_dbg", "lins_dbg2", "lins_dbg3"):
                self.emit_out(slotA, bufA)
            else:
                self.emit_out(slotB, bufB)
            r.finalize_and_emit()
            return

        self.ffn_call(slotA, bufA, call=0)
        if self.done:
            r.finalize_and_emit()
            return
        if st == "ffn1":
            self.emit_out(slotA, bufA)
            r.finalize_and_emit()
            return

        xio, b_xio = self.arena.alloc("xio1", [128, 3, D], F32, 3)
        for t in range(NT):
            xb = b_xio[t % 3]
            xap = xio[:, t % 3, :]
            d("qs", xap, x_in[t * 128:(t + 1) * 128, :], [xb])
            ROP("vector", lambda e, t=t, xap=xap: e.scalar_tensor_tensor(
                slotB[:, t * D:(t + 1) * D], slotA[:, t * D:(t + 1) * D], 0.5, xap,
                ALU.mult, ALU.add),
                reads=[bufA[t], xb], writes=[bufB[t]])
        self.arena.free(xio)
        if st == "res1":
            self.emit_out(slotB, bufB)
            r.finalize_and_emit()
            return

        self.linattn(slotB, bufB, slotA, bufA)
        if st == "lin":
            self.emit_out(slotA, bufA)
            r.finalize_and_emit()
            return

        for t in range(NT):
            d("qs", x2_spill[t * 128:(t + 1) * 128, :], slotA[:, t * D:(t + 1) * D],
              [Buf(f"x2sp{t}")], rbufs=[bufA[t]])
            ln_full_tile(slotA[:, t * D:(t + 1) * D], [bufA[t]],
                         slotB[:, t * D:(t + 1) * D], [bufB[t]], t % 16)

        self.ffn_call(slotB, bufB, call=1)

        xio, b_xio = self.arena.alloc("xio2", [128, 3, D], F32, 3)
        for t in range(NT):
            xb = b_xio[t % 3]
            xap = xio[:, t % 3, :]
            d("qs", xap, x2_spill[t * 128:(t + 1) * 128, :], [xb])
            ROP("vector", lambda e, t=t, xap=xap: e.scalar_tensor_tensor(
                slotA[:, t * D:(t + 1) * D], slotB[:, t * D:(t + 1) * D], 0.5, xap,
                ALU.mult, ALU.add),
                reads=[bufB[t], xb], writes=[bufA[t]])
        self.emit_out(slotA, bufA)
        r.finalize_and_emit()

    def emit_out(self, slot, bufs):
        r = self.rec
        y = self._y
        for t in range(NT):
            def fn(e, t=t):
                return e.dma_start(out=y[t * 128:(t + 1) * 128, :], in_=slot[:, t * D:(t + 1) * D])
            r.dma("qs", fn, reads=[bufs[t]], writes=[Buf(f"yo{t}")])

    def ffn_call(self, slot, bufs, call):
        free_slot = self.ctx["slotB"] if call == 0 else self.ctx["slotA"]
        free_bufs = self.ctx["bufB"] if call == 0 else self.ctx["bufA"]
        off = self.nc.lookup_mloc(free_slot).addr
        size = NT * D * 4
        self.arena.lend(off, size, free_bufs)
        for l in range(2):
            self.ffn_layer(slot, bufs, l, call)
            if self.done:
                return
        parents = self.arena.reclaim(off, size)
        for t in range(NT):
            free_bufs[t] = Buf(f"reslot{call}_{t}", parents)

    def ffn_layer(self, slot, bufs, l, call):
        c = self.ctx
        nc, r, ar = self.nc, self.rec, self.arena
        ROP = r.op
        d = c["d"]
        PT = self.P["PT"]
        pb = self.pb
        ident = c["ident"]; bident = c["bident"][0]
        maskf = c["maskf"]; bmaskf = c["bmaskf"][0]
        onesb = c["onesb"]; bonesb = c["bonesb"][0]
        fw_qkv, fwo, fw1, fb1, fw2 = c["fw_qkv"], c["fwo"], c["fw1"], c["fb1"], c["fw2"]
        fqs, fks = c["fqs"], c["fks"]
        ccsq, csxq, ccsk, csxk = c["c_csq"], c["c_sxq"], c["c_csk"], c["c_sxk"]
        lg_pp, lb_pp = c["lg_pp"], c["lb_pp"]
        b_lg, b_lb = c["b_lg"][0], c["b_lb"][0]
        b1a_pp, b1g_pp, b2_pp = c["b1a_pp"], c["b1g_pp"], c["b2_pp"]
        b_b1a, b_b1g, b_b2 = c["b_b1a"][0], c["b_b1g"][0], c["b_b2"][0]
        stat = c["stat"]; b_stat = c["b_stat"]
        constcol = self.constcol
        bcc = self.b_ccol

        wqkv_sb, b_wqkv = ar.alloc("wqkv", [128, NF, 1536], BF16)
        wo_sb, b_wo = ar.alloc("wo", [128, 4, 1024], BF16)
        cs_sb, b_cs = ar.alloc("cstag", [128, NT * 64], BF16)
        cq_sb, b_cq = ar.alloc("cqtab", [128, 4, NT * 64], BF16)
        qsb_sb, b_qsb = ar.alloc("qsb", [128, 4, 64], BF16)
        qsrow_bf, b_qsrowbf = ar.alloc("qsrowbf", [1, 128], BF16)
        xs_sb, b_xs = ar.alloc("xs", [128, 2, NF, 129], BF16, 2)
        cast_sb, b_cast = ar.alloc("cast", [128, 2, D], BF16, 2)
        qf_sb, b_qf = ar.alloc("qf", [128, 2, 512], BF16, 2)
        kf_sb, b_kf = ar.alloc("kf", [128, 6, 512], BF16, 6)
        v1_sb, b_v1 = ar.alloc("v1", [128, 6, FH * 65], BF16, 6)
        exp_sb, b_exp = ar.alloc("exps", [128, 2, 640], BF16, 2)
        at_sb, b_at = ar.alloc("atT", [128, 2, 640], BF16, 2)
        for _i in range(2):
            ROP("vector", lambda e, _i=_i: e.memset(exp_sb[:, _i, :], 0.0), writes=[b_exp[_i]])
            ROP("vector", lambda e, _i=_i: e.memset(at_sb[:, _i, :], 0.0), writes=[b_at[_i]])
        otm_sb, b_otm = ar.alloc("otm", [128, 2, 512], BF16, 2)
        ofm_sb, b_ofm = ar.alloc("ofm", [128, 2, 512], BF16, 2)
        qn_sb, b_qn = ar.alloc("qn", [128, 2, 512], F32, 2)
        c1_sb, b_c1 = ar.alloc("c1", [128, 2, 512], F32, 2)
        c2_sb, b_c2 = ar.alloc("c2", [128, 2, 512], F32, 2)
        sqs_sb, b_sqs = c["scr"], c["b_scr"]
        rn_sb, b_rn = ar.alloc("rn", [128, 4, 32], F32, 4)
        for _i in range(4):
            ROP("vector", lambda e, _i=_i: e.memset(rn_sb[:, _i, :], 0.0), writes=[b_rn[_i]])
        qpb_sb, b_qpb = ar.alloc("qpb", [128, 512], BF16)
        kpb_sb, b_kpb = ar.alloc("kpb", [128, 512], BF16)

        for f in range(NF):
            d("qp", wqkv_sb[:, f, :], fw_qkv[l, f * 128:(f + 1) * 128, :], [b_wqkv[0]])
        for ci in range(4):
            d("qp", wo_sb[:, ci, :], fwo[l, ci * 128:(ci + 1) * 128, :], [b_wo[0]])
        d("qp", qsrow_bf[:, 0:64], fqs[l].unsqueeze(0), [b_qsrowbf[0]])
        d("qp", qsrow_bf[:, 64:128], fks[l].unsqueeze(0), [b_qsrowbf[0]])
        pes = ExitStack()
        PS, PB = self.psum_phase(pes, {"sA": [128, 768], "sB": [128, 768],
                                       "pq": [128, 512], "pk": [128, 512],
                                       "pv": [128, 512]})
        ROP("tensor", lambda e: e.matmul(PS["pq"][:, 0:64], onesb[:], qsrow_bf[:, 0:64], start=True, stop=True),
            reads=[bonesb, b_qsrowbf[0]], writes=[PB["pq"]])
        ROP("tensor", lambda e: e.matmul(PS["pq"][:, 64:128], onesb[:], qsrow_bf[:, 64:128], start=True, stop=True),
            reads=[bonesb, b_qsrowbf[0]], writes=[PB["pq"]])
        ROP("vector", lambda e: e.tensor_copy(qsb_sb[:, 0, :], PS["pq"][:, 0:64]),
            reads=[PB["pq"]], writes=[b_qsb[0]])
        ROP("vector", lambda e: e.tensor_copy(qsb_sb[:, 2, :], PS["pq"][:, 64:128]),
            reads=[PB["pq"]], writes=[b_qsb[0]])
        ROP("vector", lambda e: e.tensor_copy(qsb_sb[:, 1, 0:32], qsb_sb[:, 0, 32:64]),
            reads=[b_qsb[0]], writes=[b_qsb[0]])
        ROP("vector", lambda e: e.tensor_copy(qsb_sb[:, 1, 32:64], qsb_sb[:, 0, 0:32]),
            reads=[b_qsb[0]], writes=[b_qsb[0]])
        ROP("vector", lambda e: e.tensor_copy(qsb_sb[:, 3, 0:32], qsb_sb[:, 2, 32:64]),
            reads=[b_qsb[0]], writes=[b_qsb[0]])
        ROP("vector", lambda e: e.tensor_copy(qsb_sb[:, 3, 32:64], qsb_sb[:, 2, 0:32]),
            reads=[b_qsb[0]], writes=[b_qsb[0]])
        for i, cc in enumerate((ccsq, csxq, ccsk, csxk)):
            d("qs", cs_sb[:], cc, [b_cs[0]])
            def mkt(e, i=i):
                tv = cs_sb[:].rearrange("p (t dd) -> p t dd", dd=64)
                qv = qsb_sb[:, i, :].unsqueeze(1).broadcast_to([128, NT, 64])
                ov = cq_sb[:, i, :].rearrange("p (t dd) -> p t dd", dd=64)
                return e.tensor_tensor(ov, tv, qv, ALU.mult)
            ROP("vector", mkt, reads=[b_cs[0], b_qsb[0]], writes=[b_cq[0]])
        for s in range(6):
            ROP("vector", lambda e, s=s: e.memset(
                v1_sb[:, s, :].rearrange("p (h u) -> p h u", u=65)[:, :, 64:65], 1.0),
                writes=[b_v1[s]])
        ROP("vector", lambda e: e.memset(xs_sb[:, 0, :, 0:1], 0.0), writes=[b_xs[0]])

        sc_reg = [PS["sA"][:, 0:640], PS["sB"][:, 0:640]]
        sc_buf = [PB["sA"], PB["sB"]]
        qkq_reg, qkq_buf = PS["pq"][:, 0:512], PB["pq"]
        qkk_reg, qkk_buf = PS["pk"][:, 0:512], PB["pk"]
        qkv_reg, qkv_buf = PS["pv"][:, 0:512], PB["pv"]
        av_reg = [PS["sA"][:, 640:705], PS["sB"][:, 640:705]]
        av_buf = [PB["sA"], PB["sB"]]
        ptbuf = pb["PT"]

        ktiles = lambda t: list(range(max(0, t - 4), t + 1))

        ntiles = 2 if self.stage == "ffn_dbg" else NT
        def stage_load(t):
            ts = t % 2
            cast_ap = cast_sb[:, ts, :]
            ROP("scalar", lambda e, t=t, a=cast_ap: e.copy(a, slot[:, t * D:(t + 1) * D]),
                reads=[bufs[t]], writes=[b_cast[ts]])
            for fc in range(NF):
                ROP("tensor", lambda e, fc=fc, a=cast_ap: e.transpose(
                    PT[:, fc * 128:(fc + 1) * 128], a[:, fc * 128:(fc + 1) * 128], ident[:]),
                    reads=[b_cast[ts], bident], writes=[ptbuf])
            ROP("vector", lambda e, ts=ts: e.tensor_copy(
                xs_sb[:, ts, :, 1:129], PT[:].rearrange("p (c u) -> p c u", u=128)),
                reads=[ptbuf], writes=[b_xs[ts]])
            if t < NT - 1:
                ROP("vector", lambda e, ts=ts: e.tensor_copy(
                    xs_sb[:, 1 - ts, :, 0:1], xs_sb[:, ts, :, 128:129]),
                    reads=[b_xs[ts]], writes=[b_xs[1 - ts]])
            def xs_view(fc, ts=ts):
                off = 1 if fc < 4 else 0
                return xs_sb[:, ts, fc, off:off + 128]
            for gi, (reg, rbuf) in enumerate(((qkq_reg, qkq_buf), (qkk_reg, qkk_buf), (qkv_reg, qkv_buf))):
                for fc in range(NF):
                    ROP("tensor", lambda e, gi=gi, fc=fc, reg=reg, ts=ts, xv=xs_view: e.matmul(
                        reg, xv(fc, ts), wqkv_sb[:, fc, gi * 512:(gi + 1) * 512],
                        start=(fc == 0), stop=(fc == NF - 1)),
                        reads=[b_xs[ts], b_wqkv[0]], writes=[rbuf])
            vs = t % 6
            ROP("vector", lambda e, vs=vs, qr=qkv_reg: e.tensor_copy(
                v1_sb[:, vs, :].rearrange("p (h u) -> p h u", u=65)[:, :, 0:64],
                qr.rearrange("p (h u) -> p h u", u=64)),
                reads=[qkv_buf], writes=[b_v1[vs]])

        def stage_prep(t):
            ts = t % 2
            tcol = t % 4
            rnb = b_rn[tcol]
            rss = rn_sb[:, tcol, :]
            streams = [
                (qkq_reg, qkq_buf, 0, 1, qf_sb[:, ts, :], b_qf[ts]),
                (qkk_reg, qkk_buf, 2, 3, kf_sb[:, t % 6, :], b_kf[t % 6]),
            ]
            for qi, (reg, rbuf, tabc, tabs, dstf, dstfb) in enumerate(streams):
                ROP("scalar", lambda e, reg=reg, qi=qi: e.activation(
                    sqs_sb[:, qi * 512:qi * 512 + 512], reg, AF.Square),
                    reads=[rbuf], writes=[b_sqs[0]])
            for qi, (reg, rbuf, tabc, tabs, dstf, dstfb) in enumerate(streams):
                sqv = sqs_sb[:, qi * 512:qi * 512 + 512].rearrange("p (h u) -> p h u", u=64)
                ROP("vector", lambda e, qi=qi, sqv=sqv, rss=rss: e.reduce_sum(
                    rss[:, qi * 8:qi * 8 + 8].unsqueeze(2), sqv, axis=AX.X),
                    reads=[b_sqs[0]], writes=[rnb])
            tiny = constcol(1e-24)
            nhalf = constcol(-0.5)
            ROP("scalar", lambda e, rss=rss: e.activation(rss[:, 16:32], rss[:, 0:16], AF.Ln, bias=tiny),
                reads=[rnb, bcc], writes=[rnb])
            ROP("scalar", lambda e, rss=rss: e.activation(rss[:, 0:16], rss[:, 16:32], AF.Exp, scale=nhalf),
                reads=[rnb, bcc], writes=[rnb])
            for qi, (reg, rbuf, tabc, tabs, dstf, dstfb) in enumerate(streams):
                regv = reg.rearrange("p (h u) -> p h u", u=64)
                qnv = qn_sb[:, qi, :].rearrange("p (h u) -> p h u", u=64)
                rnv = rss[:, qi * 8:qi * 8 + 8].unsqueeze(2).broadcast_to([128, 8, 64])
                ROP("vector", lambda e, regv=regv, rnv=rnv, qnv=qnv: e.tensor_tensor(
                    qnv, regv, rnv, ALU.mult), reads=[rbuf, rnb], writes=[b_qn[qi]])
            for qi, (reg, rbuf, tabc, tabs, dstf, dstfb) in enumerate(streams):
                qnv = qn_sb[:, qi, :].rearrange("p (h u) -> p h u", u=64)
                c1v = c1_sb[:, qi, :].rearrange("p (h u) -> p h u", u=64)
                tc = cq_sb[:, tabc, t * 64:(t + 1) * 64].unsqueeze(1).broadcast_to([128, 8, 64])
                ROP("gpsimd", lambda e, qnv=qnv, c1v=c1v, tc=tc: e.tensor_tensor(
                    c1v, qnv, tc, ALU.mult), reads=[b_qn[qi], b_cq[0]], writes=[b_c1[qi]])
            for qi, (reg, rbuf, tabc, tabs, dstf, dstfb) in enumerate(streams):
                qnv = qn_sb[:, qi, :].rearrange("p (h u) -> p h u", u=64)
                c2v = c2_sb[:, qi, :].rearrange("p (h u) -> p h u", u=64)
                qsw0 = qnv[:, :, 32:64]
                qsw1 = qnv[:, :, 0:32]
                ts0 = cq_sb[:, tabs, t * 64:t * 64 + 32].unsqueeze(1).broadcast_to([128, 8, 32])
                ts1 = cq_sb[:, tabs, t * 64 + 32:t * 64 + 64].unsqueeze(1).broadcast_to([128, 8, 32])
                ROP("vector", lambda e, qsw0=qsw0, c2v=c2v, ts0=ts0: e.tensor_tensor(
                    c2v[:, :, 0:32], qsw0, ts0, ALU.mult), reads=[b_qn[qi], b_cq[0]], writes=[b_c2[qi]])
                ROP("vector", lambda e, qsw1=qsw1, c2v=c2v, ts1=ts1: e.tensor_tensor(
                    c2v[:, :, 32:64], qsw1, ts1, ALU.mult), reads=[b_qn[qi], b_cq[0]], writes=[b_c2[qi]])
            for qi, (reg, rbuf, tabc, tabs, dstf, dstfb) in enumerate(streams):
                pbf = (qpb_sb, kpb_sb)[qi]
                pbfb = (b_qpb, b_kpb)[qi][0]
                ROP("vector", lambda e, pbf=pbf, qi=qi: e.tensor_add(
                    pbf[:], c1_sb[:, qi, :], c2_sb[:, qi, :]),
                    reads=[b_c1[qi], b_c2[qi]], writes=[pbfb])
                for ci2 in range(4):
                    ROP("tensor", lambda e, ci2=ci2, qi=qi, pbf=pbf: e.transpose(
                        PT[:, qi * 512 + ci2 * 128: qi * 512 + (ci2 + 1) * 128],
                        pbf[:, ci2 * 128:(ci2 + 1) * 128], ident[:]),
                        reads=[pbfb, bident], writes=[ptbuf])
                ROP("vector", lambda e, qi=qi, dstf=dstf: e.tensor_copy(
                    dstf, PT[:, qi * 512:(qi + 1) * 512]), reads=[ptbuf], writes=[dstfb])


        def stage_attn(t):
            ts = t % 2
            kts = ktiles(t)
            m = len(kts)
            for h in range(FH):
                acol = 64 + (t % 4) * 8 + (h % 4) * 2
                amax = stat[:, acol:acol + 1]
                rcol = stat[:, acol + 1:acol + 2]
                bamax = b_stat[16 + (t % 4) * 4 + h % 4]
                hreg = h % 2
                sreg, sbuf_ = sc_reg[hreg], sc_buf[hreg]
                hp = (h % 2) * 64
                hc = h // 2
                for ci2, kt in enumerate(kts):
                    ROP("tensor", lambda e, ci2=ci2, kt=kt, sreg=sreg, hp=hp, hc=hc, ts=ts: e.matmul(
                        sreg[:, ci2 * 128:(ci2 + 1) * 128],
                        qf_sb[hp:hp + 64, ts, hc * 128:(hc + 1) * 128],
                        kf_sb[hp:hp + 64, kt % 6, hc * 128:(hc + 1) * 128],
                        start=True, stop=True),
                        reads=[b_qf[ts], b_kf[kt % 6]], writes=[sbuf_])
                ROP("vector", lambda e, sreg=sreg, m=m, amax=amax: e.reduce_max(
                    amax, sreg[:, 0:m * 128], axis=AX.X, negate=True),
                    reads=[sbuf_], writes=[bamax])
                es_ = h % 2
                ROP("scalar", lambda e, sreg=sreg, m=m, es_=es_, amax=amax: e.activation(
                    exp_sb[:, es_, 0:m * 128], sreg[:, 0:m * 128], AF.Exp, bias=amax),
                    reads=[sbuf_, bamax], writes=[b_exp[es_]])
                for ci2 in range(m):
                    ROP("tensor", lambda e, ci2=ci2, es_=es_: e.transpose(
                        PT[:, ci2 * 128:(ci2 + 1) * 128],
                        exp_sb[:, es_, ci2 * 128:(ci2 + 1) * 128], ident[:]),
                        reads=[b_exp[es_], bident], writes=[ptbuf])
                ROP("vector", lambda e, m=m, es_=es_: e.tensor_tensor(
                    at_sb[:, es_, 0:m * 128], PT[:, 0:m * 128],
                    maskf[:, (5 - m) * 128:640], ALU.mult),
                    reads=[ptbuf, bmaskf], writes=[b_at[es_]])
                areg, abuf_ = av_reg[hreg], av_buf[hreg]
                for ci2, kt in enumerate(kts):
                    ROP("tensor", lambda e, ci2=ci2, kt=kt, areg=areg, h=h, es_=es_, m=m: e.matmul(
                        areg, at_sb[:, es_, ci2 * 128:(ci2 + 1) * 128],
                        v1_sb[:, kt % 6, h * 65:(h + 1) * 65],
                        start=(ci2 == 0), stop=(ci2 == m - 1)),
                        reads=[b_at[es_], b_v1[kt % 6]], writes=[abuf_])
                ROP("vector", lambda e, areg=areg, rcol=rcol: e.reciprocal(rcol, areg[:, 64:65]),
                    reads=[abuf_], writes=[bamax])
                ROP("scalar", lambda e, areg=areg, h=h, ts=ts, rcol=rcol: e.activation(
                    otm_sb[:, ts, h * 64:(h + 1) * 64], areg[:, 0:64], AF.Copy, scale=rcol),
                    reads=[abuf_, bamax], writes=[b_otm[ts]])
            for ci2 in range(4):
                ROP("tensor", lambda e, ci2=ci2, ts=ts: e.transpose(
                    PT[:, ci2 * 128:(ci2 + 1) * 128], otm_sb[:, ts, ci2 * 128:(ci2 + 1) * 128],
                    ident[:]), reads=[b_otm[ts], bident], writes=[ptbuf])
            ROP("vector", lambda e, ts=ts: e.tensor_copy(ofm_sb[:, ts, :], PT[:, 0:512]),
                reads=[ptbuf], writes=[b_ofm[ts]])
            for oh in range(2):
                wreg, wbuf_ = ((sc_reg[0][:, 0:512], sc_buf[0]), (sc_reg[1][:, 0:512], sc_buf[1]))[oh]
                for ci2 in range(4):
                    ROP("tensor", lambda e, ci2=ci2, oh=oh, wreg=wreg, ts=ts: e.matmul(
                        wreg, ofm_sb[:, ts, ci2 * 128:(ci2 + 1) * 128],
                        wo_sb[:, ci2, oh * 512:(oh + 1) * 512],
                        start=(ci2 == 0), stop=(ci2 == 3)),
                        reads=[b_ofm[ts], b_wo[0]], writes=[wbuf_])
                ROP("vector", lambda e, oh=oh, t=t, wreg=wreg: e.tensor_tensor(
                    slot[:, t * D + oh * 512: t * D + (oh + 1) * 512], wreg,
                    slot[:, t * D + oh * 512: t * D + (oh + 1) * 512], ALU.add),
                    reads=[wbuf_, bufs[t]], writes=[bufs[t]])


        for t in range(ntiles + 1):
            if t < ntiles:
                stage_load(t)
            if t >= 1:
                stage_attn(t - 1)
            if t < ntiles:
                stage_prep(t)

        if self.stage == "ffn_dbg" and l == 0 and call == 0:
            dbg = {"dbg_cast1": (cast_sb[:, 1, :], b_cast[1]),
                   "dbg_qf1": (qf_sb[:, 1, :], b_qf[1]),
                   "dbg_kf0": (kf_sb[:, 0, :], b_kf[0]),
                   "dbg_kf1": (kf_sb[:, 1, :], b_kf[1]),
                   "dbg_v10": (v1_sb[:, 0, :], b_v1[0]),
                   "dbg_v11": (v1_sb[:, 1, :], b_v1[1]),
                   "dbg_otm1": (otm_sb[:, 1, :], b_otm[1]),
                   "dbg_exp1": (exp_sb[:, 1, :], b_exp[1]),
                   "dbg_at1": (at_sb[:, 1, :], b_at[1]),
                   "dbg_xs1": (xs_sb[:, 1, :, :].rearrange("p a b -> p (a b)"), b_xs[1]),
                   "dbg_cq": (cq_sb[:, 0, :], b_cq[0]),
                   }
            for nm, (ap, bb) in dbg.items():
                o = self.nc.dram_tensor(nm, [128, ap.shape[-1] if ap.ndim == 2 else int(np.prod(ap.shape[1:]))],
                                        ap.dtype, kind="ExternalOutput").ap()
                r.dma("qs", (lambda e, o=o, ap=ap: e.dma_start(out=o, in_=ap)),
                      reads=[bb], writes=[Buf("dbo_" + nm)])
            self.emit_out(slot, bufs)
            self.done = True
            return

        ar.free(wqkv_sb, wo_sb, cs_sb, qsb_sb, qsrow_bf, xs_sb, cast_sb, qn_sb,
                c1_sb, c2_sb, qpb_sb, kpb_sb)
        ar.free(cq_sb, qf_sb, kf_sb, v1_sb, exp_sb, at_sb, otm_sb, ofm_sb, rn_sb)
        pes.close()
        self.psum_phase_end(PB)

        if self.stage == f"ffn_a{l}" and call == 0 and l == 1:
            self.emit_out(slot, bufs)
            self.done = True
            return
        if self.stage == f"ffn_a{l}" and call == 0:
            self.emit_out(slot, bufs)
            self.done = True
            return

        self.ffn_mlp(slot, bufs, l, call)
        if self.stage == f"ffn_l{l}" and call == 0:
            self.emit_out(slot, bufs)
            self.done = True
            return

    def ffn_mlp(self, slot, bufs, l, call):
        c = self.ctx
        nc, r, ar = self.nc, self.rec, self.arena
        ROP = r.op
        d = c["d"]
        PT = self.P["PT"]
        pb = self.pb
        ident = c["ident"]; bident = c["bident"][0]
        identf = c["identf"]; bidentf = c["bidentf"][0]
        fw1, fb1, fw2 = c["fw1"], c["fb1"], c["fw2"]
        lg_pp, lb_pp = c["lg_pp"], c["lb_pp"]
        b_lg, b_lb = c["b_lg"][0], c["b_lb"][0]
        b1a_pp, b1g_pp, b2_pp = c["b1a_pp"], c["b1g_pp"], c["b2_pp"]
        b_b1a, b_b1g, b_b2 = c["b_b1a"][0], c["b_b1g"][0], c["b_b2"][0]
        scr = c["scr"]; b_scr = c["b_scr"][0]
        constcol = self.constcol
        bcc = self.b_ccol

        st2, b_st2 = ar.alloc("st2", [128, 64], F32)
        sh2, b_sh2 = ar.alloc("sh2", [128, 32], F32)
        cm_sb, b_cm = ar.alloc("cm", [128, 32], F32)
        mr_sb, b_mr = ar.alloc("mr", [128, 64], F32)
        ROP("vector", lambda e: e.memset(mr_sb[:], 0.0), writes=[b_mr[0]])
        nrm_sb, b_nrm = ar.alloc("nrm", [128, 2, D], BF16, 2)
        ln2_sb, b_ln2 = ar.alloc("ln2", [128, NF, 1025], BF16)
        bnd_sb, b_bnd = ar.alloc("bnd", [128, NF, 1], BF16)
        h2_sb, b_h2 = ar.alloc("h2", [128, NH, 1024], BF16)
        w1b_sb, b_w1b = ar.alloc("w1b", [128, 2, NF, 256], BF16, 2)
        abf_sb, b_abf = ar.alloc("abf", [128, 2, 512], BF16, 2)
        gbf_sb, b_gbf = ar.alloc("gbf", [128, 2, 512], BF16, 2)
        wof_sb, b_wof = ar.alloc("wof", [128, NH, 128], BF16)
        w2t_sb, b_w2t = ar.alloc("w2t", [128, 1024], F32)

        for t in range(NT):
            ROP("scalar", lambda e, t=t: e.activation(
                scr[:, 0:512], slot[:, t * D:t * D + 512], AF.Square,
                accum_out=st2[:, 16 + t:17 + t]),
                reads=[bufs[t]], writes=[b_scr, b_st2[0]])
            ROP("scalar", lambda e, t=t: e.activation(
                scr[:, 512:1024], slot[:, t * D + 512:(t + 1) * D], AF.Square,
                accum_out=st2[:, 48 + t:49 + t]),
                reads=[bufs[t]], writes=[b_scr, b_st2[0]])
            ROP("vector", lambda e, t=t: e.reduce_sum(
                st2[:, t:t + 1], slot[:, t * D:t * D + 512], axis=AX.X),
                reads=[bufs[t]], writes=[b_st2[0]])
            ROP("vector", lambda e, t=t: e.reduce_sum(
                st2[:, 32 + t:33 + t], slot[:, t * D + 512:(t + 1) * D], axis=AX.X),
                reads=[bufs[t]], writes=[b_st2[0]])
        ROP("vector", lambda e: e.memset(sh2[0:1, :], 0.0), writes=[b_sh2[0]])
        d("qp", sh2[1:128, 0:16], st2[0:127, 32:48], [b_sh2[0]], rbufs=[b_st2[0]])
        d("qp", sh2[1:128, 16:32], st2[0:127, 48:64], [b_sh2[0]], rbufs=[b_st2[0]])
        d("qp", sh2[0:1, 1:16], st2[127:128, 32:47], [b_sh2[0]], rbufs=[b_st2[0]])
        d("qp", sh2[0:1, 17:32], st2[127:128, 48:63], [b_sh2[0]], rbufs=[b_st2[0]])
        ROP("vector", lambda e: e.tensor_add(cm_sb[:], st2[:, 0:32], sh2[:]),
            reads=[b_st2[0], b_sh2[0]], writes=[b_cm[0]])
        ROP("vector", lambda e: e.tensor_scalar(mr_sb[:, 0:16], cm_sb[:, 0:16], -1.0 / D, None, ALU.mult),
            reads=[b_cm[0]], writes=[b_mr[0]])
        ROP("vector", lambda e: e.tensor_mul(mr_sb[:, 16:32], mr_sb[:, 0:16], mr_sb[:, 0:16]),
            reads=[b_mr[0]], writes=[b_mr[0]])
        ROP("vector", lambda e: e.scalar_tensor_tensor(
            mr_sb[:, 32:48], cm_sb[:, 16:32], 1.0 / D, mr_sb[:, 16:32], ALU.mult, ALU.subtract),
            reads=[b_cm[0], b_mr[0]], writes=[b_mr[0]])
        eps5 = constcol(1e-5)
        nhalf = constcol(-0.5)
        ROP("scalar", lambda e: e.activation(mr_sb[:, 48:64], mr_sb[:, 32:48], AF.Ln, bias=eps5),
            reads=[b_mr[0], bcc], writes=[b_mr[0]])
        ROP("scalar", lambda e: e.activation(mr_sb[:, 16:32], mr_sb[:, 48:64], AF.Exp, scale=nhalf),
            reads=[b_mr[0], bcc], writes=[b_mr[0]])
        d("qp", mr_sb[0:127, 32:64], mr_sb[1:128, 0:32], [b_mr[0]], rbufs=[b_mr[0]])
        d("qp", mr_sb[127:128, 32:47], mr_sb[0:1, 1:16], [b_mr[0]], rbufs=[b_mr[0]])
        d("qp", mr_sb[127:128, 48:63], mr_sb[0:1, 17:32], [b_mr[0]], rbufs=[b_mr[0]])


        pes = ExitStack()
        PS, PB = self.psum_phase(pes, {"pa0": [128, 512], "pa1": [128, 512],
                                       "pg0": [128, 512], "pg1": [128, 512]})
        a_reg = [(PS["pa0"][:], PB["pa0"]), (PS["pa1"][:], PB["pa1"])]
        g_reg = [(PS["pg0"][:], PB["pg0"]), (PS["pg1"][:], PB["pg1"])]
        tb_reg = [(PS["pg0"][:, 0:128], PB["pg0"]), (PS["pg1"][:, 0:128], PB["pg1"])]
        ptbuf = pb["PT"]

        for g in range(2):
            if g == 0:
                ROP("vector", lambda e: e.memset(ln2_sb[:, :, 0:1], 0.0), writes=[b_ln2[0]])
            else:
                ROP("vector", lambda e: e.tensor_copy(ln2_sb[:, :, 0:1], bnd_sb[:]),
                    reads=[b_bnd[0]], writes=[b_ln2[0]])
            for tt in range(8):
                t = 8 * g + tt
                ns = tt % 2
                ROP("vector", lambda e, t=t, ns=ns: e.tensor_scalar(
                    nrm_sb[:, ns, 0:512], slot[:, t * D:t * D + 512],
                    mr_sb[:, t:t + 1], mr_sb[:, 16 + t:17 + t], ALU.add, ALU.mult),
                    reads=[bufs[t], b_mr[0]], writes=[b_nrm[ns]])
                ROP("vector", lambda e, t=t, ns=ns: e.tensor_scalar(
                    nrm_sb[:, ns, 512:1024], slot[:, t * D + 512:(t + 1) * D],
                    mr_sb[:, 32 + t:33 + t], mr_sb[:, 48 + t:49 + t], ALU.add, ALU.mult),
                    reads=[bufs[t], b_mr[0]], writes=[b_nrm[ns]])
                for fc in range(NF):
                    ROP("tensor", lambda e, fc=fc, ns=ns: e.transpose(
                        PT[:, fc * 128:(fc + 1) * 128], nrm_sb[:, ns, fc * 128:(fc + 1) * 128],
                        ident[:]), reads=[b_nrm[ns], bident], writes=[ptbuf])
                for fc in range(NF):
                    ROP("vector", lambda e, fc=fc, tt=tt, lc=l * 8 + fc: e.tensor_scalar(
                        ln2_sb[:, fc, 1 + tt * 128:1 + (tt + 1) * 128],
                        PT[:, fc * 128:(fc + 1) * 128],
                        lg_pp[:, lc:lc + 1], lb_pp[:, lc:lc + 1], ALU.mult, ALU.add),
                        reads=[ptbuf, b_lg, b_lb], writes=[b_ln2[0]])
            if g == 0:
                ROP("vector", lambda e: e.tensor_copy(bnd_sb[:], ln2_sb[:, :, 1024:1025]),
                    reads=[b_ln2[0]], writes=[b_bnd[0]])

            for j in range(NH):
                wj = min(128, HID - 128 * j)
                jp = j % 2
                d("qp", w1b_sb[:, jp, :, 0:wj],
                  fw1[l, :, 128 * j:128 * j + wj].rearrange("(a p) h -> p a h", p=128),
                  [b_w1b[jp]])
                d("qp", w1b_sb[:, jp, :, 128:128 + wj],
                  fw1[l, :, HID + 128 * j:HID + 128 * j + wj].rearrange("(a p) h -> p a h", p=128),
                  [b_w1b[jp]])
                for th in range(2):
                    (pa, pab), (pg, pgb) = a_reg[th], g_reg[th]
                    for fc in range(NF):
                        off = (1 if fc < 4 else 0) + th * 512
                        ROP("tensor", lambda e, fc=fc, jp=jp, wj=wj, pa=pa, off=off: e.matmul(
                            pa[0:wj, :], w1b_sb[:, jp, fc, 0:wj], ln2_sb[:, fc, off:off + 512],
                            start=(fc == 0), stop=(fc == NF - 1)),
                            reads=[b_w1b[jp], b_ln2[0]], writes=[pab])
                    for fc in range(NF):
                        off = (1 if fc < 4 else 0) + th * 512
                        ROP("tensor", lambda e, fc=fc, jp=jp, wj=wj, pg=pg, off=off: e.matmul(
                            pg[0:wj, :], w1b_sb[:, jp, fc, 128:128 + wj], ln2_sb[:, fc, off:off + 512],
                            start=(fc == 0), stop=(fc == NF - 1)),
                            reads=[b_w1b[jp], b_ln2[0]], writes=[pgb])
                    ROP("vector", lambda e, th=th, j=j, pa=pa: e.tensor_scalar(
                        abf_sb[:, th, :], pa, b1a_pp[:, l * NH + j:l * NH + j + 1], None, ALU.add),
                        reads=[pab, b_b1a], writes=[b_abf[th]])
                    ROP("scalar", lambda e, th=th, j=j, pg=pg: e.activation(
                        gbf_sb[:, th, :], pg, AF.Gelu, bias=b1g_pp[:, l * NH + j:l * NH + j + 1]),
                        reads=[pgb, b_b1g], writes=[b_gbf[th]])
                    ROP("vector", lambda e, th=th, j=j: e.tensor_mul(
                        h2_sb[:, j, th * 512:(th + 1) * 512], abf_sb[:, th, :], gbf_sb[:, th, :]),
                        reads=[b_abf[th], b_gbf[th]], writes=[b_h2[0]])

            for of in range(8):
                d("qp", wof_sb[:, 0:21, :],
                  fw2[l, 0:2688, of * 128:(of + 1) * 128].rearrange("(j p) o -> p j o", p=128),
                  [b_wof[0]])
                d("qp", wof_sb[0:42, 21, :], fw2[l, 2688:HID, of * 128:(of + 1) * 128],
                  [b_wof[0]])
                for th in range(2):
                    pa, pab = a_reg[th]
                    for j in range(NH):
                        wj = min(128, HID - 128 * j)
                        ROP("tensor", lambda e, j=j, wj=wj, th=th, pa=pa: e.matmul(
                            pa, wof_sb[0:wj, j, :], h2_sb[0:wj, j, th * 512:(th + 1) * 512],
                            start=(j == 0), stop=(j == NH - 1)),
                            reads=[b_wof[0], b_h2[0]], writes=[pab])
                    ROP("vector", lambda e, th=th, of=of, pa=pa: e.tensor_scalar(
                        w2t_sb[:, th * 512:(th + 1) * 512], pa,
                        b2_pp[:, l * 8 + of:l * 8 + of + 1], None, ALU.add),
                        reads=[pab, b_b2], writes=[b_w2t[0]])
                for tt in range(8):
                    t = 8 * g + tt
                    (tr, trb) = tb_reg[tt % 2]
                    ROP("tensor", lambda e, tt=tt, tr=tr: e.transpose(
                        tr[:, 0:128], w2t_sb[:, tt * 128:(tt + 1) * 128], identf[:]),
                        reads=[b_w2t[0], bidentf], writes=[trb])
                    ROP("vector", lambda e, t=t, of=of, tr=tr: e.tensor_tensor(
                        slot[:, t * D + of * 128:t * D + (of + 1) * 128], tr[:, 0:128],
                        slot[:, t * D + of * 128:t * D + (of + 1) * 128], ALU.add),
                        reads=[trb, bufs[t]], writes=[bufs[t]])

        ar.free(st2, sh2, cm_sb, mr_sb, nrm_sb, ln2_sb, bnd_sb, h2_sb, w1b_sb,
                abf_sb, gbf_sb, wof_sb, w2t_sb)
        pes.close()
        self.psum_phase_end(PB)

    def linattn(self, slotX, bufX, slotO, bufO):
        c = self.ctx
        nc, r, ar = self.nc, self.rec, self.arena
        ROP = r.op
        d = c["d"]
        PT = self.P["PT"]
        pb = self.pb
        ident = c["ident"]; bident = c["bident"][0]
        maskl = c["maskl"]; bmaskl = c["bmaskl"][0]
        onesb = c["onesb"]; bonesb = c["bonesb"][0]
        aw_qkv, aw_out, ab_qkv, ab_out = c["aw_qkv"], c["aw_out"], c["ab_qkv"], c["ab_out"]
        ln_full_tile = c["ln_full_tile"]
        QN = 4

        abrow, b_abrow = ar.alloc("abrow", [1, 3072], BF16)
        aorow, b_aorow = ar.alloc("aorow", [1, 1024], BF16)
        S_sb, b_S = ar.alloc("S", [128, 2, 520], F32, 2)
        Sb_sb, b_Sb = ar.alloc("Sbf", [128, 2, 520], BF16, 2)
        nrc_sb, b_nrc = ar.alloc("nrc", [128, 4], F32, 2)
        qtm_sb, b_qtm = ar.alloc("qtm", [128, QN, 1024], BF16, QN)
        ktm_sb, b_ktm = ar.alloc("ktm", [128, QN, 1024], BF16, QN)
        v1_sb, b_v1 = ar.alloc("v1l", [128, QN, LH * 65], BF16, QN)
        d("qp", abrow[:], ab_qkv.unsqueeze(0), [b_abrow[0]])
        d("qp", aorow[:], ab_out.unsqueeze(0), [b_aorow[0]])
        ROP("vector", lambda e: e.memset(S_sb[:, 1, :], 0.0), writes=[b_S[1]])
        ROP("vector", lambda e: e.memset(Sb_sb[:, 1, :], 0.0), writes=[b_Sb[1]])

        ptbuf = pb["PT"]

        def _quarter(qr):
            tiles = list(range(QN * qr, QN * qr + QN))
            ln1_sb, b_ln1 = ar.alloc("ln1f", [128, NF, QN * 128], BF16)
            lncast_sb, b_lncast = ar.alloc("lncast", [128, 2, D], BF16, 2)
            wst_sb, b_wst = ar.alloc("wst", [128, 1024], BF16)
            wt_sb, b_wt = ar.alloc("wtog", [128, NF, 512], BF16)
            emin_sb, b_emin = ar.alloc("emin", [128, 512], F32)
            eexp_sb, b_eexp = ar.alloc("eexp", [128, 512], F32)
            for s in range(QN):
                ROP("vector", lambda e, s=s: e.memset(
                    v1_sb[:, s, :].rearrange("p (h u) -> p h u", u=65)[:, :, 64:65], 1.0),
                    writes=[b_v1[s]])
            for tt, t in enumerate(tiles):
                ns = tt % 2
                ln_full_tile(slotX[:, t * D:(t + 1) * D], [bufX[t]],
                             lncast_sb[:, ns, :], [b_lncast[ns]], t % 16)
                for fc in range(NF):
                    ROP("tensor", lambda e, fc=fc, ns=ns: e.transpose(
                        PT[:, fc * 128:(fc + 1) * 128],
                        lncast_sb[:, ns, fc * 128:(fc + 1) * 128], ident[:]),
                        reads=[b_lncast[ns], bident], writes=[ptbuf])
                ROP("vector", lambda e, tt=tt: e.tensor_copy(
                    ln1_sb[:, :, tt * 128:(tt + 1) * 128],
                    PT[:].rearrange("p (cc u) -> p cc u", u=128)),
                    reads=[ptbuf], writes=[b_ln1[0]])
            pesq = ExitStack()
            PSq, PBq = self.psum_phase(pesq, {"l0": [128, 512], "l1": [128, 512],
                                              "l2": [128, 512]})
            qkrot = [(PSq["l0"][:], PBq["l0"]), (PSq["l1"][:], PBq["l1"]),
                     (PSq["l2"][:], PBq["l2"])]
            if self.stage in ("lins_dbg", "lins_dbg2"):
                def _mkcb(tag, names):
                    def _cb(sim, tag=tag, names=names):
                        import numpy as np
                        msg = [tag]
                        for nm in names:
                            try:
                                a = np.array(sim.tensor(nm)).astype(np.float32)
                                fin = np.isfinite(a)
                                msg.append(f"{nm}:max={np.abs(a[fin]).max() if fin.any() else -1:.4f},nan={int((~fin).sum())}")
                            except Exception as ex:
                                msg.append(f"{nm}:ERR")
                        print("[CB]", *msg)
                    return _cb
                def _fn0(e):
                    from concourse.bass_interp import add_callback
                    return add_callback(e, _mkcb("post-ln1", [ln1_sb.name]))
                r.op("vector", _fn0, reads=[b_ln1[0]])
            for og in range(6):
                for i in range(4):
                    d("qp", wst_sb[:], aw_qkv[og * 512 + i * 128: og * 512 + (i + 1) * 128, :],
                      [b_wst[0]])
                    for fc in range(NF):
                        ROP("tensor", lambda e, fc=fc, w=wst_sb: e.transpose(
                            PT[:, fc * 128:(fc + 1) * 128], w[:, fc * 128:(fc + 1) * 128],
                            ident[:]), reads=[b_wst[0], bident], writes=[ptbuf])
                    if self.stage in ("lins_dbg", "lins_dbg2", "linq_dbg") and og == 0 and i == 0:
                        def _fnq(e):
                            from concourse.bass_interp import add_callback2
                            def _cbq(sim, inst):
                                import numpy as np
                                from concourse.bass_interp import Direction
                                v = sim.instruction_executor.view_arg(
                                    inst.ins[0], Direction.READ, inst, reg_snapshot=None)
                                a = np.asarray(v).astype(np.float32)
                                fin = np.isfinite(a)
                                print("[CBQ] PT view:", a.shape,
                                      "max", np.abs(a[fin]).max() if fin.any() else -1,
                                      "nan", int((~fin).sum()))
                                try:
                                    sems = {k: v for k, v in sim.semaphores.items()} if hasattr(sim, "semaphores") else None
                                    print("[CBQ] sems:", sems)
                                except Exception as ex:
                                    print("[CBQ] semerr", ex)
                                st = [x for x in dir(sim) if "sem" in x.lower() or "engine" in x.lower()]
                                print("[CBQ] attrs:", st[:12])
                            return add_callback2(e, _cbq, [PT[:]])
                        r.op("vector", _fnq, reads=[ptbuf])
                        _lasttok = ptbuf.w
                        def _fnp(e, _lt=_lasttok):
                            from concourse.bass_interp import add_callback
                            def _cbp(sim, _lt=_lt):
                                import numpy as np
                                for nm2 in ("PsT", "PsT_set", "pp1_l0"):
                                    try:
                                        a = np.array(sim.tensor(nm2)).astype(np.float32)
                                        fin = np.isfinite(a)
                                        print("[CB]", nm2, "shape", a.shape, "max",
                                              np.abs(a[fin]).max() if fin.any() else -1,
                                              "nan", int((~fin).sum()))
                                    except Exception as ex:
                                        print("[CB]", nm2, "ERR", str(ex)[:80])
                            return add_callback(e, _cbp)
                        r.op("vector", _fnp, reads=[ptbuf])
                    ROP("vector", lambda e, i=i: e.tensor_copy(
                        wt_sb[:, :, i * 128:(i + 1) * 128],
                        PT[:].rearrange("p (cc u) -> p cc u", u=128)),
                        reads=[ptbuf], writes=[b_wt[0]])
                for tt in range(QN):
                    ps, psb = qkrot[tt % 3]
                    for fc in range(NF):
                        ROP("tensor", lambda e, fc=fc, tt=tt, ps=ps: e.matmul(
                            ps, ln1_sb[:, fc, tt * 128:(tt + 1) * 128], wt_sb[:, fc, :],
                            start=(fc == 0), stop=False),
                            reads=[b_ln1[0], b_wt[0]], writes=[psb])
                    ROP("tensor", lambda e, og=og, ps=ps: e.matmul(
                        ps, onesb[:], abrow[0:1, og * 512:(og + 1) * 512],
                        start=False, stop=True),
                        reads=[bonesb, b_abrow[0]], writes=[psb])
                    if og < 4:
                        dst = qtm_sb if og < 2 else ktm_sb
                        dbuf = (b_qtm if og < 2 else b_ktm)[tt]
                        dslice = (og % 2) * 512
                        if self.stage == "lins_dbg3" and og == 0 and tt == 0 and qr == 0:
                            psnap, b_psnap = ar.alloc("psnap", [128, 512], F32)
                            self._psnap = psnap
                            ROP("vector", lambda e, ps=ps: e.tensor_copy(psnap[:], ps),
                                reads=[psb], writes=[b_psnap[0]])
                        ROP("vector", lambda e, ps=ps: e.tensor_scalar_min(emin_sb[:], ps, 0.0),
                            reads=[psb], writes=[b_emin[0]])
                        ROP("scalar", lambda e: e.activation(eexp_sb[:], emin_sb[:], AF.Exp),
                            reads=[b_emin[0]], writes=[b_eexp[0]])
                        ROP("vector", lambda e, ps=ps, dst=dst, tt=tt, dslice=dslice: e.scalar_tensor_tensor(
                            dst[:, tt, dslice:dslice + 512], ps, 0.0, eexp_sb[:],
                            ALU.max, ALU.add),
                            reads=[psb, b_eexp[0]], writes=[dbuf])
                    else:
                        hof = 8 * (og - 4)
                        ROP("vector", lambda e, ps=ps, tt=tt, hof=hof: e.tensor_copy(
                            v1_sb[:, tt, hof * 65:(hof + 8) * 65].rearrange(
                                "p (h u) -> p h u", u=65)[:, :, 0:64],
                            ps.rearrange("p (h u) -> p h u", u=64)),
                            reads=[psb], writes=[b_v1[tt]])
                if self.stage in ("lins_dbg", "lins_dbg2"):
                    def _fn1(e, og=og):
                        from concourse.bass_interp import add_callback
                        return add_callback(e, _mkcb(f"post-og{og}",
                            [wt_sb.name, qtm_sb.name, v1_sb.name]))
                    r.op("vector", _fn1, reads=[b_wt[0], b_qtm[0], b_v1[0]])
            if self.stage == "linq_dbg":
                self.done = True
                return
            if self.stage == "lins_dbg":
                def _cb(sim, qtm_t=qtm_sb):
                    import numpy as np
                    a = np.array(sim.tensors.mems.get("ar7_qtm_53", [[0]])) if hasattr(sim, "tensors") else None
                    try:
                        a = np.array(sim.tensor("ar7_qtm_53")).astype(np.float32)
                        print("[CB] post-qkv qtm absmax:", np.abs(a).max())
                    except Exception as ex:
                        print("[CB] err", ex)
                def _fn(e):
                    from concourse.bass_interp import add_callback
                    return add_callback(e, _cb)
                r.op("vector", _fn, reads=[b_qtm[0], b_qtm[1], b_qtm[2], b_qtm[3]])
            ar.free(ln1_sb, lncast_sb, wst_sb, wt_sb, emin_sb, eexp_sb)
            pesq.close()
            self.psum_phase_end(PBq)

            pess = ExitStack()
            PSs, PBs = self.psum_phase(pess, {
                "at0": [128, 128], "at1": [128, 128], "at2": [128, 128],
                "dsA": [128, 512], "dsB": [128, 512],
                "avA": [128, 128], "avB": [128, 128]})
            woT_sb, b_woT = ar.alloc("woT", [128, NF, 1024], BF16)
            wos_sb, b_wos = ar.alloc("wst2", [128, 1024], BF16)
            qf_sb, b_qf = ar.alloc("qfl", [128, 2, 1024], BF16, 2)
            kf_sb, b_kf = ar.alloc("kfl", [128, 2, 1024], BF16, 2)
            at_sb, b_at = ar.alloc("atl", [128, 3, 128], BF16, 3)
            otm_sb, b_otm = ar.alloc("otml", [128, 1, 1024], BF16, 1)
            ofm_sb, b_ofm = ar.alloc("ofml", [128, 1, 1024], BF16, 1)
            for i in range(NF):
                d("qp", wos_sb[:], aw_out[i * 128:(i + 1) * 128, :], [b_wos[0]])
                for fc in range(NF):
                    ROP("tensor", lambda e, fc=fc, w=wos_sb: e.transpose(
                        PT[:, fc * 128:(fc + 1) * 128], w[:, fc * 128:(fc + 1) * 128],
                        ident[:]), reads=[b_wos[0], bident], writes=[ptbuf])
                ROP("vector", lambda e, i=i: e.tensor_copy(
                    woT_sb[:, :, i * 128:(i + 1) * 128],
                    PT[:].rearrange("p (cc u) -> p cc u", u=128)),
                    reads=[ptbuf], writes=[b_woT[0]])
            for tt in range(QN):
                tg = QN * qr + tt
                qs_ = tt % 2
                for fc in range(NF):
                    ROP("tensor", lambda e, fc=fc, tt=tt: e.transpose(
                        PT[:, fc * 128:(fc + 1) * 128],
                        qtm_sb[:, tt, fc * 128:(fc + 1) * 128], ident[:]),
                        reads=[b_qtm[tt], bident], writes=[ptbuf])
                ROP("vector", lambda e, qs_=qs_: e.tensor_copy(qf_sb[:, qs_, :], PT[:]),
                    reads=[ptbuf], writes=[b_qf[qs_]])
                for fc in range(NF):
                    ROP("tensor", lambda e, fc=fc, tt=tt: e.transpose(
                        PT[:, fc * 128:(fc + 1) * 128],
                        ktm_sb[:, tt, fc * 128:(fc + 1) * 128], ident[:]),
                        reads=[b_ktm[tt], bident], writes=[ptbuf])
                ROP("vector", lambda e, qs_=qs_: e.tensor_copy(kf_sb[:, qs_, :], PT[:]),
                    reads=[ptbuf], writes=[b_kf[qs_]])
                atrot = [(PSs["at0"][:, 0:128], PBs["at0"]),
                         (PSs["at1"][:, 0:128], PBs["at1"]),
                         (PSs["at2"][:, 0:128], PBs["at2"])]
                avrot = [(PSs["avA"][:, 0:65], PBs["avA"]),
                         (PSs["avB"][:, 0:65], PBs["avB"])]
                Sold = (tg + 1) % 2
                for h in range(LH):
                    hp = 64 * (h % 2)
                    hc = h // 2
                    areg, abuf_ = atrot[h % 3]
                    ROP("tensor", lambda e, areg=areg, hp=hp, hc=hc, qs_=qs_: e.matmul(
                        areg, kf_sb[hp:hp + 64, qs_, hc * 128:(hc + 1) * 128],
                        qf_sb[hp:hp + 64, qs_, hc * 128:(hc + 1) * 128],
                        start=True, stop=True),
                        reads=[b_kf[qs_], b_qf[qs_]], writes=[abuf_])
                    ats = h % 3
                    ROP("vector", lambda e, areg=areg, ats=ats: e.tensor_tensor(
                        at_sb[:, ats, :], areg, maskl[:], ALU.mult),
                        reads=[abuf_, bmaskl], writes=[b_at[ats]])
                    vreg, vbuf_ = avrot[h % 2]
                    ROP("tensor", lambda e, vreg=vreg, ats=ats, tt=tt, h=h: e.matmul(
                        vreg, at_sb[:, ats, :], v1_sb[:, tt, h * 65:(h + 1) * 65],
                        start=True, stop=False),
                        reads=[b_at[ats], b_v1[tt]], writes=[vbuf_])
                    ROP("tensor", lambda e, vreg=vreg, hp=hp, hc=hc, qs_=qs_, So=Sold: e.matmul(
                        vreg, qf_sb[hp:hp + 64, qs_, hc * 128:(hc + 1) * 128],
                        Sb_sb[hp:hp + 64, So, hc * 65:(hc + 1) * 65],
                        start=False, stop=True),
                        reads=[b_qf[qs_], b_Sb[Sold]], writes=[vbuf_])
                    nq = h % 2
                    ROP("vector", lambda e, vreg=vreg, nq=nq: e.tensor_scalar(
                        nrc_sb[:, nq * 2:nq * 2 + 1], vreg[:, 64:65], 1e-6, None, ALU.add),
                        reads=[vbuf_], writes=[b_nrc[nq]])
                    ROP("vector", lambda e, nq=nq: e.reciprocal(
                        nrc_sb[:, nq * 2 + 1:nq * 2 + 2], nrc_sb[:, nq * 2:nq * 2 + 1]),
                        reads=[b_nrc[nq]], writes=[b_nrc[nq]])
                    ROP("scalar", lambda e, vreg=vreg, h=h, nq=nq: e.activation(
                        otm_sb[:, 0, h * 64:(h + 1) * 64], vreg[:, 0:64], AF.Copy,
                        scale=nrc_sb[:, nq * 2 + 1:nq * 2 + 2]),
                        reads=[vbuf_, b_nrc[nq]], writes=[b_otm[0]])
                    dreg_b = PBs["dsA"] if hc < 4 else PBs["dsB"]
                    dap = (PSs["dsA"][hp:hp + 64, 65 * hc:65 * hc + 65] if hc < 4 else
                           PSs["dsB"][hp:hp + 64, 65 * (hc - 4):65 * (hc - 4) + 65])
                    ROP("tensor", lambda e, dap=dap, tt=tt, h=h: e.matmul(
                        dap, ktm_sb[:, tt, h * 64:(h + 1) * 64],
                        v1_sb[:, tt, h * 65:(h + 1) * 65],
                        start=True, stop=True),
                        reads=[b_ktm[tt], b_v1[tt]], writes=[dreg_b])
                Snew = tg % 2
                ROP("vector", lambda e, Snew=Snew, So=Sold, dsA=PSs["dsA"]: e.tensor_tensor(
                    S_sb[:, Snew, 0:260], dsA[:, 0:260], S_sb[:, So, 0:260], ALU.add),
                    reads=[PBs["dsA"], b_S[Sold]], writes=[b_S[Snew]])
                ROP("vector", lambda e, Snew=Snew, So=Sold, dsB=PSs["dsB"]: e.tensor_tensor(
                    S_sb[:, Snew, 260:520], dsB[:, 0:260], S_sb[:, So, 260:520], ALU.add),
                    reads=[PBs["dsB"], b_S[Sold]], writes=[b_S[Snew]])
                ROP("vector", lambda e, Snew=Snew: e.tensor_copy(
                    Sb_sb[:, Snew, 0:260], S_sb[:, Snew, 0:260]),
                    reads=[b_S[Snew]], writes=[b_Sb[Snew]])
                ROP("vector", lambda e, Snew=Snew: e.tensor_copy(
                    Sb_sb[:, Snew, 260:520], S_sb[:, Snew, 260:520]),
                    reads=[b_S[Snew]], writes=[b_Sb[Snew]])
                for fc in range(NF):
                    ROP("tensor", lambda e, fc=fc: e.transpose(
                        PT[:, fc * 128:(fc + 1) * 128],
                        otm_sb[:, 0, fc * 128:(fc + 1) * 128], ident[:]),
                        reads=[b_otm[0], bident], writes=[ptbuf])
                ROP("vector", lambda e: e.tensor_copy(ofm_sb[:, 0, :], PT[:]),
                    reads=[ptbuf], writes=[b_ofm[0]])
                for oh in range(2):
                    oreg, orb = ((PSs["dsA"][:, 0:512], PBs["dsA"]),
                                 (PSs["dsB"][:, 0:512], PBs["dsB"]))[oh]
                    for fc in range(NF):
                        ROP("tensor", lambda e, fc=fc, oh=oh, oreg=oreg: e.matmul(
                            oreg, ofm_sb[:, 0, fc * 128:(fc + 1) * 128],
                            woT_sb[:, fc, oh * 512:(oh + 1) * 512],
                            start=(fc == 0), stop=False),
                            reads=[b_ofm[0], b_woT[0]], writes=[orb])
                    ROP("tensor", lambda e, oh=oh, oreg=oreg: e.matmul(
                        oreg, onesb[:], aorow[0:1, oh * 512:(oh + 1) * 512],
                        start=False, stop=True),
                        reads=[bonesb, b_aorow[0]], writes=[orb])
                    ROP("vector", lambda e, oh=oh, tg=tg, oreg=oreg: e.tensor_tensor(
                        slotO[:, tg * D + oh * 512:tg * D + (oh + 1) * 512], oreg,
                        slotX[:, tg * D + oh * 512:tg * D + (oh + 1) * 512], ALU.add),
                        reads=[orb, bufX[tg]], writes=[bufO[tg]])
            if self.stage in ("lins_dbg", "lins_dbg2", "lins_dbg3"):
                self.done = True
                return
            ar.free(woT_sb, wos_sb, qf_sb, kf_sb, at_sb, otm_sb, ofm_sb)
            pess.close()
            self.psum_phase_end(PBs)

        for qr in range(4):
            _quarter(qr)
            if self.done:
                return

        ar.free(abrow, aorow, S_sb, Sb_sb, nrc_sb, qtm_sb, ktm_sb, v1_sb)


_CACHE = {}

def _get_mk():
    if STAGE not in _CACHE:
        _CACHE[STAGE] = MK(STAGE)
    return _CACHE[STAGE]

def kernel(L0_attn_x, L0_attn_w_qkv, L0_attn_b_qkv, L0_attn_w_out, L0_attn_b_out,
           L0_ffn_wqkv, L0_ffn_q_scale, L0_ffn_k_scale, L0_ffn_wo, L0_ffn_ln_g,
           L0_ffn_ln_b, L0_ffn_w1, L0_ffn_b1, L0_ffn_w2, L0_ffn_b2):
    mk = _get_mk()
    csq, sxq, csk, sxk = rot_tables()
    consts = {
        "c_ident": np.eye(128, dtype=np.float32).astype(BF),
        "c_identf": np.eye(128, dtype=np.float32),
        "c_maskf": mask_t_ffn(),
        "c_maskl": mask_t_lin(),
        "c_csq": csq, "c_sxq": sxq, "c_csk": csk, "c_sxk": sxk,
    }
    shared = {
        "aw_qkv": np.ascontiguousarray(L0_attn_w_qkv, np.float32),
        "ab_qkv": np.ascontiguousarray(L0_attn_b_qkv, np.float32),
        "aw_out": np.ascontiguousarray(L0_attn_w_out, np.float32),
        "ab_out": np.ascontiguousarray(L0_attn_b_out, np.float32),
        "fw_qkv": np.ascontiguousarray(L0_ffn_wqkv, np.float32),
        "fqs": np.ascontiguousarray(L0_ffn_q_scale, np.float32),
        "fks": np.ascontiguousarray(L0_ffn_k_scale, np.float32),
        "fwo": np.ascontiguousarray(L0_ffn_wo, np.float32),
        "flng": np.ascontiguousarray(L0_ffn_ln_g, np.float32),
        "flnb": np.ascontiguousarray(L0_ffn_ln_b, np.float32),
        "fw1": np.ascontiguousarray(L0_ffn_w1, np.float32),
        "fb1": np.ascontiguousarray(L0_ffn_b1, np.float32),
        "fw2": np.ascontiguousarray(L0_ffn_w2, np.float32),
        "fb2": np.ascontiguousarray(L0_ffn_b2, np.float32),
    }
    shared.update(consts)
    x = np.ascontiguousarray(L0_attn_x, np.float32)
    in_maps = []
    for c in range(8):
        m = dict(shared)
        m["x"] = np.ascontiguousarray(x[c])
        in_maps.append(m)
    res = run_bass_kernel_spmd(mk.nc, in_maps, core_ids=list(range(8)))
    if STAGE == "ffn_dbg":
        return res.results[0]
    return np.stack([res.results[c]["y"] for c in range(8)], axis=0)


# version 07
# speedup vs baseline: 1.0083x; 1.0083x over previous
import numpy as np
import ml_dtypes
from contextlib import ExitStack
import concourse.bass as bass
import concourse.mybir as mybir
from concourse.bass_utils import run_bass_kernel_spmd

F32 = mybir.dt.float32
BF16 = mybir.dt.bfloat16
AF = mybir.ActivationFunctionType
ALU = mybir.AluOpType
AX = mybir.AxisListType
BF = ml_dtypes.bfloat16

T = 2048
D = 1024
NT = 16
NF = 8
FH = 8
FDH = 64
WIN = 512
HID = 2730
NH = 22
QK_SCALE = 8.0
LH = 16

STAGE = "full"


def rot_tables():
    half = np.arange(0, FDH, 2, dtype=np.float64)
    inv_freq = 1.0 / 10000.0 ** (half / FDH)
    t = np.arange(T, dtype=np.float64)
    freqs = t[:, None] * inv_freq[None, :]
    freqs64 = np.concatenate([freqs, freqs], axis=-1)
    base = (half + 0.4 * FDH) / (1.4 * FDH)
    power = (t - T // 2) / float(WIN // 2)
    scale = base[None, :] ** power[:, None]
    scale64 = np.concatenate([scale, scale], axis=-1)
    cos, sin = np.cos(freqs64), np.sin(freqs64)
    sign = np.where(np.arange(FDH) < FDH // 2, -1.0, 1.0)[None, :]
    tabs = [QK_SCALE * cos * scale64, QK_SCALE * sign * sin * scale64,
            cos / scale64, sign * sin / scale64]
    out = []
    for a in tabs:
        out.append(np.ascontiguousarray(
            a.reshape(NT, 128, FDH).transpose(1, 0, 2).reshape(128, NT * FDH)
        ).astype(BF))
    return out

def mask_t_ffn():
    out = np.zeros((128, 5 * 128), np.float32)
    u = np.arange(128)[:, None]
    rr = np.arange(128)[None, :]
    for c in range(5):
        qk = 128 * (4 - c) + rr - u
        out[:, c * 128:(c + 1) * 128] = ((qk >= 0) & (qk <= WIN)).astype(np.float32)
    return out.astype(BF)

def mask_t_lin():
    k = np.arange(128)[:, None]
    q = np.arange(128)[None, :]
    return (k <= q).astype(np.float32).astype(BF)


class Buf:
    __slots__ = ("name", "w", "r")
    def __init__(self, name, parents=()):
        self.name = name
        self.w = None
        self.r = []
        for p in parents:
            if p.w is not None:
                self.r.append(p.w)
            self.r.extend(p.r)

ENGS = ("sync", "tensor", "vector", "scalar", "gpsimd")
QUEUES = {"qs": "sync", "qp": "gpsimd", "qa": "scalar"}

class Rec:
    def __init__(self, nc, es):
        self.nc = nc
        self.ops = {e: [] for e in ENGS}
        self.need_inc = {e: set() for e in ENGS}
        self.dmacnt = {q: 0 for q in QUEUES}
        self.emitted = {}
        self.sems = {}
        for e in ENGS:
            self.sems[e] = es.enter_context(nc.semaphore("sem_" + e))
        for q in QUEUES:
            self.sems[q] = es.enter_context(nc.semaphore("sem_" + q))

    def _record(self, eng, fn, reads, writes, queue):
        import sys
        fr = sys._getframe(2)
        label = f"{fr.f_code.co_name}:{fr.f_lineno}"
        deps = []
        for b in reads:
            if b.w is not None:
                deps.append(b.w)
        for b in writes:
            if b.w is not None:
                deps.append(b.w)
            deps.extend(b.r)
        fdeps = {}
        for (d0, d1) in deps:
            if d0 == eng and eng == "tensor":
                continue
            fdeps[d0] = max(fdeps.get(d0, 0), d1)
        fdeps = sorted(fdeps.items())
        idx = len(self.ops[eng])
        self.ops[eng].append((fn, fdeps, queue, label))
        if queue is None:
            tok = (eng, idx)
        else:
            self.dmacnt[queue] += 16
            tok = (queue, self.dmacnt[queue])
        for d0, d1 in fdeps:
            if d0 not in QUEUES:
                self.need_inc[d0].add(d1)
        for b in reads:
            b.r.append(tok)
        for b in writes:
            b.w = tok
            b.r = []
        return tok

    def op(self, eng, fn, reads=(), writes=()):
        return self._record(eng, fn, list(reads), list(writes), None)

    def dma(self, queue, fn, reads=(), writes=()):
        return self._record(QUEUES[queue], fn, list(reads), list(writes), queue)

    def finalize_and_emit(self):
        mil = {}
        for e in ENGS:
            mil[e] = {idx: k + 1 for k, idx in enumerate(sorted(self.need_inc[e]))}
        rec = self

        def mkreplay(eng):
            def replay(e):
                hw = {}
                qcnt = {q: 0 for q in QUEUES}
                for idx, (fn, fdeps, queue, label) in enumerate(rec.ops[eng]):
                    if queue is not None:
                        prev = qcnt[queue] * 16
                        qcnt[queue] += 1
                        if prev > 0 and hw.get(queue, 0) < prev:
                            hw[queue] = prev
                            e.wait_ge(rec.sems[queue], prev)
                    for d0, d1 in fdeps:
                        val = d1 if d0 in QUEUES else mil[d0][d1]
                        if hw.get(d0, 0) >= val:
                            continue
                        hw[d0] = val
                        e.wait_ge(rec.sems[d0], val)
                    ins = fn(e)
                    rec.emitted.setdefault(eng, []).append((ins.ins.name, idx))
                    if queue is not None:
                        ins.then_inc(rec.sems[queue], 16)
                    elif idx in mil[eng]:
                        ins.then_inc(rec.sems[eng], 1)
                if eng == "sync":
                    for q in QUEUES:
                        if rec.dmacnt[q] > 0 and hw.get(q, 0) < rec.dmacnt[q]:
                            e.wait_ge(rec.sems[q], rec.dmacnt[q])
            return replay

        with self.nc.Block() as blk:
            blk.sync(mkreplay("sync"))
            blk.tensor(mkreplay("tensor"))
            blk.vector(mkreplay("vector"))
            blk.scalar(mkreplay("scalar"))
            blk.gpsimd(mkreplay("gpsimd"))

    def nops(self):
        return {e: len(self.ops[e]) for e in ENGS}

class Arena:
    def __init__(self, nc, base, top):
        self.nc = nc
        self.base = base
        self.top = top
        self.free_iv = [(base, top)]
        self.live = {}
        self.hist = []
        self.count = 0

    def alloc(self, name, shape, dtype, nbufs=1):
        itemsize = np.dtype(mybir.dt.np(dtype)).itemsize
        size = int(np.prod(shape[1:])) * itemsize
        size = (size + 31) & ~31
        off = None
        for i, (s, e) in enumerate(self.free_iv):
            if e - s >= size:
                off = s
                if s + size < e:
                    self.free_iv[i] = (s + size, e)
                else:
                    del self.free_iv[i]
                break
        if off is None:
            raise MemoryError(f"arena overflow: {name} needs {size}; free={self.free_iv}")
        parents = []
        for (fo, fs, fb) in self.hist:
            if fo < off + size and off < fo + fs:
                parents.extend(fb)
        self.count += 1
        t = self.nc.alloc_sbuf_tensor_at(f"ar{self.count}_{name}", list(shape), dtype, offset=off)
        bufs = [Buf(f"{name}[{i}]", parents) for i in range(nbufs)]
        self.live[t.name] = (off, size, bufs)
        self.log = getattr(self, "log", [])
        self.log.append(("A", t.name, off, size))
        return t, bufs

    def free(self, *tensors):
        for t in tensors:
            off, size, bufs = self.live.pop(t.name)
            self.log = getattr(self, "log", [])
            self.log.append(("F", t.name, off, size))
            self.hist.append((off, size, bufs))
            self.free_iv.append((off, off + size))
        self._merge()

    def _merge(self):
        self.free_iv.sort()
        merged = []
        for s, e in self.free_iv:
            if merged and s <= merged[-1][1]:
                merged[-1] = (merged[-1][0], max(merged[-1][1], e))
            else:
                merged.append((s, e))
        self.free_iv = merged

    def lend(self, off, size, seed_bufs):
        self.hist.append((off, size, list(seed_bufs)))
        self.free_iv.append((off, off + size))
        self._merge()

    def reclaim(self, off, size):
        out = []
        niv = []
        for s, e in self.free_iv:
            if s < off + size and off < e:
                assert s <= off and e >= off + size, \
                    f"reclaim range busy: {(off, size)} vs free {(s, e)}"
                if s < off:
                    niv.append((s, off))
                if e > off + size:
                    niv.append((off + size, e))
            else:
                niv.append((s, e))
        self.free_iv = niv
        keep = []
        for (fo, fs, fb) in self.hist:
            if fo < off + size and off < fo + fs:
                out.extend(fb)
            else:
                keep.append((fo, fs, fb))
        self.hist = keep
        return out


class MK:
    def __init__(self, stage=STAGE):
        self.stage = stage
        self.es = ExitStack()
        self.nc = bass.Bass("TRN2", target_bir_lowering=False)
        self.input_names = []
        self.build()

    def psum_phase(self, es2, specs):
        tensors, bufs = {}, {}
        for name, shape in specs.items():
            self._psum_uid += 1
            t = es2.enter_context(self.nc.psum_tensor(f"pp{self._psum_uid}_{name}", list(shape), F32))
            tensors[name] = t
            bufs[name] = Buf(name + str(self._psum_uid), self.psum_prev)
        return tensors, bufs

    def psum_phase_end(self, bufs):
        self.psum_prev = list(bufs.values())

    def din(self, name, shape, dtype):
        self.input_names.append(name)
        return self.nc.dram_tensor(name, list(shape), dtype, kind="ExternalInput").ap()

    def sb(self, name, shape, dtype, nbufs=1):
        t = self.nc.alloc_sbuf_tensor(name, list(shape), dtype)
        return t, [Buf(f"{name}[{i}]") for i in range(nbufs)]

    def build(self):
        nc, es = self.nc, self.es
        r = self.rec = Rec(nc, es)
        st = self.stage

        x_in = self.din("x", [T, D], F32)
        aw_qkv = self.din("aw_qkv", [3072, 1024], F32)
        ab_qkv = self.din("ab_qkv", [3072], F32)
        aw_out = self.din("aw_out", [1024, 1024], F32)
        ab_out = self.din("ab_out", [1024], F32)
        fw_qkv = self.din("fw_qkv", [2, 1024, 1536], F32)
        fqs = self.din("fqs", [2, 64], F32)
        fks = self.din("fks", [2, 64], F32)
        fwo = self.din("fwo", [2, 512, 1024], F32)
        flng = self.din("flng", [2, 1024], F32)
        flnb = self.din("flnb", [2, 1024], F32)
        fw1 = self.din("fw1", [2, 1024, 5460], F32)
        fb1 = self.din("fb1", [2, 5460], F32)
        fw2 = self.din("fw2", [2, 2730, 1024], F32)
        fb2 = self.din("fb2", [2, 1024], F32)
        c_ident = self.din("c_ident", [128, 128], BF16)
        c_identf = self.din("c_identf", [128, 128], F32)
        c_maskf = self.din("c_maskf", [128, 640], BF16)
        c_maskl = self.din("c_maskl", [128, 128], BF16)
        c_csq = self.din("c_csq", [128, NT * 64], BF16)
        c_sxq = self.din("c_sxq", [128, NT * 64], BF16)
        c_csk = self.din("c_csk", [128, NT * 64], BF16)
        c_sxk = self.din("c_sxk", [128, NT * 64], BF16)
        y_out = nc.dram_tensor("y", [T, D], F32, kind="ExternalOutput").ap()
        self._y = y_out
        x2_spill = nc.dram_tensor("x2spill", [T, D], F32, kind="Internal").ap()

        slotA, bufA = self.sb("slotA", [128, NT * D], F32, NT)
        slotB, bufB = self.sb("slotB", [128, NT * D], F32, NT)
        ident, bident = self.sb("ident", [128, 128], BF16)
        identf, bidentf = self.sb("identf", [128, 128], F32)
        maskf, bmaskf = self.sb("maskf", [128, 640], BF16)
        maskl, bmaskl = self.sb("maskl", [128, 128], BF16)
        onesb, bonesb = self.sb("onesb", [1, 128], BF16)
        abqkv_pp, b_abqkv = self.sb("abqkv_pp", [128, 24], F32)
        lg_pp, b_lg = self.sb("lg_pp", [128, 16], F32)
        lb_pp, b_lb = self.sb("lb_pp", [128, 16], F32)
        b1a_pp, b_b1a = self.sb("b1a_pp", [128, 2 * NH], F32)
        b1g_pp, b_b1g = self.sb("b1g_pp", [128, 2 * NH], F32)
        b2_pp, b_b2 = self.sb("b2_pp", [128, 16], F32)
        qsrow, b_qsrow = self.sb("qsrow", [1, 128], F32)
        ksrow, b_ksrow = self.sb("ksrow", [1, 128], F32)
        stat, b_stat = self.sb("stat", [128, 128], F32, 48)
        scr, b_scr = self.sb("scr", [128, 1024], F32)

        PT = es.enter_context(nc.psum_tensor("PsT", [128, 1024], BF16))
        pb = {"PT": Buf("PT")}
        self.pb = pb
        self.P = dict(PT=PT)
        self.psum_prev = []
        self._psum_uid = 0
        self.done = False

        ccol_t, b_ccol = self.sb("ccol", [128, 8], F32)
        self._ccols = {}
        def constcol(val):
            if val not in self._ccols:
                i = len(self._ccols)
                assert i < 8
                ap = ccol_t[:, i:i + 1]
                r.op("vector", lambda e, a=ap, v=float(val): e.memset(a, v),
                     writes=[b_ccol[0]])
                self._ccols[val] = ap
            return self._ccols[val]
        self.constcol = constcol
        self.b_ccol = b_ccol[0]

        def d(q, out_ap, in_ap, wbufs, rbufs=(), slow=False):
            def fn(e, o=out_ap, i=in_ap, s=slow):
                if s:
                    with nc.allow_non_contiguous_dma("small"):
                        return e.dma_start(out=o, in_=i, allow_slow_non_contiguous=True)
                return e.dma_start(out=o, in_=i)
            return r.dma(q, fn, reads=list(rbufs), writes=list(wbufs))

        ROP = r.op

        d("qs", ident[:], c_ident, [bident[0]])
        d("qs", identf[:], c_identf, [bidentf[0]])
        d("qs", maskf[:], c_maskf, [bmaskf[0]])
        d("qs", maskl[:], c_maskl, [bmaskl[0]])
        ROP("vector", lambda e: e.memset(onesb[:], 1.0), writes=[bonesb[0]])
        ROP("vector", lambda e: e.memset(b1a_pp[:], 0.0), writes=[b_b1a[0]])
        ROP("vector", lambda e: e.memset(b1g_pp[:], 0.0), writes=[b_b1g[0]])
        d("qp", abqkv_pp[:], ab_qkv.rearrange("(c p) -> p c", p=128), [b_abqkv[0]], slow=True)
        for l in range(2):
            d("qp", lg_pp[:, l * 8:(l + 1) * 8], flng[l].rearrange("(c p) -> p c", p=128), [b_lg[0]], slow=True)
            d("qp", lb_pp[:, l * 8:(l + 1) * 8], flnb[l].rearrange("(c p) -> p c", p=128), [b_lb[0]], slow=True)
            d("qp", b2_pp[:, l * 8:(l + 1) * 8], fb2[l].rearrange("(c p) -> p c", p=128), [b_b2[0]], slow=True)
            d("qp", b1a_pp[:, l * NH:l * NH + 21], fb1[l, 0:2688].rearrange("(c p) -> p c", p=128), [b_b1a[0]], slow=True)
            d("qp", b1a_pp[0:42, l * NH + 21:l * NH + 22], fb1[l, 2688:2730].unsqueeze(1), [b_b1a[0]], slow=True)
            d("qp", b1g_pp[:, l * NH:l * NH + 21], fb1[l, 2730:5418].rearrange("(c p) -> p c", p=128), [b_b1g[0]], slow=True)
            d("qp", b1g_pp[0:42, l * NH + 21:l * NH + 22], fb1[l, 5418:5460].unsqueeze(1), [b_b1g[0]], slow=True)
            d("qs", qsrow[:, l * 64:(l + 1) * 64], fqs[l].unsqueeze(0), [b_qsrow[0]])
            d("qs", ksrow[:, l * 64:(l + 1) * 64], fks[l].unsqueeze(0), [b_ksrow[0]])

        def ln_full_tile(src_ap, src_bufs, dst_ap, dst_bufs, tslot, eps=1e-5):
            sc = stat[:, tslot * 4 + 0:tslot * 4 + 1]
            qc = stat[:, tslot * 4 + 1:tslot * 4 + 2]
            nm = stat[:, tslot * 4 + 2:tslot * 4 + 3]
            rs = stat[:, tslot * 4 + 3:tslot * 4 + 4]
            sb_ = b_stat[tslot]
            ROP("scalar", lambda e: e.activation(scr[:], src_ap, AF.Square),
                reads=src_bufs, writes=[b_scr[0]])
            ROP("vector", lambda e: e.reduce_sum(qc, scr[:], axis=AX.X),
                reads=[b_scr[0]], writes=[sb_])
            ROP("vector", lambda e: e.reduce_sum(sc, src_ap, axis=AX.X),
                reads=src_bufs, writes=[sb_])
            ROP("vector", lambda e: e.tensor_scalar(nm, sc, -1.0 / D, None, ALU.mult),
                reads=[sb_], writes=[sb_])
            ROP("vector", lambda e: e.tensor_mul(rs, nm, nm), reads=[sb_], writes=[sb_])
            ROP("vector", lambda e: e.scalar_tensor_tensor(qc, qc, 1.0 / D, rs, ALU.mult, ALU.subtract),
                reads=[sb_], writes=[sb_])
            epsc = self.constcol(eps)
            nhalf = self.constcol(-0.5)
            ROP("scalar", lambda e: e.activation(sc, qc, AF.Ln, bias=epsc),
                reads=[sb_, b_ccol[0]], writes=[sb_])
            ROP("scalar", lambda e: e.activation(rs, sc, AF.Exp, scale=nhalf),
                reads=[sb_, b_ccol[0]], writes=[sb_])
            ROP("vector", lambda e: e.tensor_scalar(dst_ap, src_ap, nm, rs, ALU.add, ALU.mult),
                reads=list(src_bufs) + [sb_], writes=dst_bufs)

        self.arena = Arena(nc, nc.sbuf_base, nc.sbuf_top)
        xio, b_xio = self.arena.alloc("xio0", [128, 3, D], F32, 3)
        for t in range(NT):
            xb = b_xio[t % 3]
            xap = xio[:, t % 3, :]
            d("qs", xap, x_in[t * 128:(t + 1) * 128, :], [xb])
            ln_full_tile(xap, [xb], slotA[:, t * D:(t + 1) * D], [bufA[t]], t % 16)
        self.arena.free(xio)

        self.ctx = dict(locals())

        if st == "ln1":
            self.emit_out(slotA, bufA)
            r.finalize_and_emit()
            return

        if st in ("lin_dbg", "linq_dbg", "lins_dbg", "lins_dbg2", "lins_dbg3"):
            self.linattn(slotA, bufA, slotB, bufB)
            if st in ("linq_dbg", "lins_dbg", "lins_dbg2", "lins_dbg3"):
                self.emit_out(slotA, bufA)
            else:
                self.emit_out(slotB, bufB)
            r.finalize_and_emit()
            return

        self.ffn_call(slotA, bufA, call=0)
        if self.done:
            r.finalize_and_emit()
            return
        if st == "ffn1":
            self.emit_out(slotA, bufA)
            r.finalize_and_emit()
            return

        xio, b_xio = self.arena.alloc("xio1", [128, 3, D], F32, 3)
        for t in range(NT):
            xb = b_xio[t % 3]
            xap = xio[:, t % 3, :]
            d("qs", xap, x_in[t * 128:(t + 1) * 128, :], [xb])
            ROP("vector", lambda e, t=t, xap=xap: e.scalar_tensor_tensor(
                slotB[:, t * D:(t + 1) * D], slotA[:, t * D:(t + 1) * D], 0.5, xap,
                ALU.mult, ALU.add),
                reads=[bufA[t], xb], writes=[bufB[t]])
        self.arena.free(xio)
        if st == "res1":
            self.emit_out(slotB, bufB)
            r.finalize_and_emit()
            return

        self.linattn(slotB, bufB, slotA, bufA)
        if st == "lin":
            self.emit_out(slotA, bufA)
            r.finalize_and_emit()
            return

        for t in range(NT):
            d("qs", x2_spill[t * 128:(t + 1) * 128, :], slotA[:, t * D:(t + 1) * D],
              [Buf(f"x2sp{t}")], rbufs=[bufA[t]])
            ln_full_tile(slotA[:, t * D:(t + 1) * D], [bufA[t]],
                         slotB[:, t * D:(t + 1) * D], [bufB[t]], t % 16)

        self.ffn_call(slotB, bufB, call=1)

        xio, b_xio = self.arena.alloc("xio2", [128, 3, D], F32, 3)
        for t in range(NT):
            xb = b_xio[t % 3]
            xap = xio[:, t % 3, :]
            d("qs", xap, x2_spill[t * 128:(t + 1) * 128, :], [xb])
            ROP("vector", lambda e, t=t, xap=xap: e.scalar_tensor_tensor(
                slotA[:, t * D:(t + 1) * D], slotB[:, t * D:(t + 1) * D], 0.5, xap,
                ALU.mult, ALU.add),
                reads=[bufB[t], xb], writes=[bufA[t]])
        self.emit_out(slotA, bufA)
        r.finalize_and_emit()

    def emit_out(self, slot, bufs):
        r = self.rec
        y = self._y
        for t in range(NT):
            def fn(e, t=t):
                return e.dma_start(out=y[t * 128:(t + 1) * 128, :], in_=slot[:, t * D:(t + 1) * D])
            r.dma("qs", fn, reads=[bufs[t]], writes=[Buf(f"yo{t}")])

    def ffn_call(self, slot, bufs, call):
        free_slot = self.ctx["slotB"] if call == 0 else self.ctx["slotA"]
        free_bufs = self.ctx["bufB"] if call == 0 else self.ctx["bufA"]
        off = self.nc.lookup_mloc(free_slot).addr
        size = NT * D * 4
        self.arena.lend(off, size, free_bufs)
        for l in range(2):
            self.ffn_layer(slot, bufs, l, call)
            if self.done:
                return
        parents = self.arena.reclaim(off, size)
        for t in range(NT):
            free_bufs[t] = Buf(f"reslot{call}_{t}", parents)

    def ffn_layer(self, slot, bufs, l, call):
        c = self.ctx
        nc, r, ar = self.nc, self.rec, self.arena
        ROP = r.op
        d = c["d"]
        PT = self.P["PT"]
        pb = self.pb
        ident = c["ident"]; bident = c["bident"][0]
        maskf = c["maskf"]; bmaskf = c["bmaskf"][0]
        onesb = c["onesb"]; bonesb = c["bonesb"][0]
        fw_qkv, fwo, fw1, fb1, fw2 = c["fw_qkv"], c["fwo"], c["fw1"], c["fb1"], c["fw2"]
        fqs, fks = c["fqs"], c["fks"]
        ccsq, csxq, ccsk, csxk = c["c_csq"], c["c_sxq"], c["c_csk"], c["c_sxk"]
        lg_pp, lb_pp = c["lg_pp"], c["lb_pp"]
        b_lg, b_lb = c["b_lg"][0], c["b_lb"][0]
        b1a_pp, b1g_pp, b2_pp = c["b1a_pp"], c["b1g_pp"], c["b2_pp"]
        b_b1a, b_b1g, b_b2 = c["b_b1a"][0], c["b_b1g"][0], c["b_b2"][0]
        stat = c["stat"]; b_stat = c["b_stat"]
        constcol = self.constcol
        bcc = self.b_ccol

        wqkv_sb, b_wqkv = ar.alloc("wqkv", [128, NF, 1536], BF16)
        wo_sb, b_wo = ar.alloc("wo", [128, 4, 1024], BF16)
        cs_sb, b_cs = ar.alloc("cstag", [128, NT * 64], BF16)
        cq_sb, b_cq = ar.alloc("cqtab", [128, 4, NT * 64], BF16)
        qsb_sb, b_qsb = ar.alloc("qsb", [128, 4, 64], BF16)
        qsrow_bf, b_qsrowbf = ar.alloc("qsrowbf", [1, 128], BF16)
        xs_sb, b_xs = ar.alloc("xs", [128, 2, NF, 129], BF16, 2)
        cast_sb, b_cast = ar.alloc("cast", [128, 2, D], BF16, 2)
        qf_sb, b_qf = ar.alloc("qf", [128, 2, 512], BF16, 2)
        kf_sb, b_kf = ar.alloc("kf", [128, 6, 512], BF16, 6)
        v1_sb, b_v1 = ar.alloc("v1", [128, 6, FH * 65], BF16, 6)
        exp_sb, b_exp = ar.alloc("exps", [128, 2, 640], BF16, 2)
        at_sb, b_at = ar.alloc("atT", [128, 2, 640], BF16, 2)
        for _i in range(2):
            ROP("vector", lambda e, _i=_i: e.memset(exp_sb[:, _i, :], 0.0), writes=[b_exp[_i]])
            ROP("vector", lambda e, _i=_i: e.memset(at_sb[:, _i, :], 0.0), writes=[b_at[_i]])
        otm_sb, b_otm = ar.alloc("otm", [128, 2, 512], BF16, 2)
        ofm_sb, b_ofm = ar.alloc("ofm", [128, 2, 512], BF16, 2)
        qn_sb, b_qn = ar.alloc("qn", [128, 2, 512], F32, 2)
        c1_sb, b_c1 = ar.alloc("c1", [128, 2, 512], F32, 2)
        c2_sb, b_c2 = ar.alloc("c2", [128, 2, 512], F32, 2)
        sqs_sb, b_sqs = c["scr"], c["b_scr"]
        rn_sb, b_rn = ar.alloc("rn", [128, 4, 32], F32, 4)
        for _i in range(4):
            ROP("vector", lambda e, _i=_i: e.memset(rn_sb[:, _i, :], 0.0), writes=[b_rn[_i]])
        qpb_sb, b_qpb = ar.alloc("qpb", [128, 512], BF16)
        kpb_sb, b_kpb = ar.alloc("kpb", [128, 512], BF16)
        st2_sb, b_st2g = ar.alloc("st2", [128, 64], F32)
        self._st2, self._b_st2 = st2_sb, b_st2g

        for f in range(NF):
            d("qp", wqkv_sb[:, f, :], fw_qkv[l, f * 128:(f + 1) * 128, :], [b_wqkv[0]])
        for ci in range(4):
            d("qp", wo_sb[:, ci, :], fwo[l, ci * 128:(ci + 1) * 128, :], [b_wo[0]])
        d("qp", qsrow_bf[:, 0:64], fqs[l].unsqueeze(0), [b_qsrowbf[0]])
        d("qp", qsrow_bf[:, 64:128], fks[l].unsqueeze(0), [b_qsrowbf[0]])
        pes = ExitStack()
        PS, PB = self.psum_phase(pes, {"sA": [128, 768], "sB": [128, 768],
                                       "pq": [128, 512], "pk": [128, 512],
                                       "pv": [128, 512]})
        ROP("tensor", lambda e: e.matmul(PS["pq"][:, 0:64], onesb[:], qsrow_bf[:, 0:64], start=True, stop=True),
            reads=[bonesb, b_qsrowbf[0]], writes=[PB["pq"]])
        ROP("tensor", lambda e: e.matmul(PS["pq"][:, 64:128], onesb[:], qsrow_bf[:, 64:128], start=True, stop=True),
            reads=[bonesb, b_qsrowbf[0]], writes=[PB["pq"]])
        ROP("vector", lambda e: e.tensor_copy(qsb_sb[:, 0, :], PS["pq"][:, 0:64]),
            reads=[PB["pq"]], writes=[b_qsb[0]])
        ROP("vector", lambda e: e.tensor_copy(qsb_sb[:, 2, :], PS["pq"][:, 64:128]),
            reads=[PB["pq"]], writes=[b_qsb[0]])
        ROP("vector", lambda e: e.tensor_copy(qsb_sb[:, 1, 0:32], qsb_sb[:, 0, 32:64]),
            reads=[b_qsb[0]], writes=[b_qsb[0]])
        ROP("vector", lambda e: e.tensor_copy(qsb_sb[:, 1, 32:64], qsb_sb[:, 0, 0:32]),
            reads=[b_qsb[0]], writes=[b_qsb[0]])
        ROP("vector", lambda e: e.tensor_copy(qsb_sb[:, 3, 0:32], qsb_sb[:, 2, 32:64]),
            reads=[b_qsb[0]], writes=[b_qsb[0]])
        ROP("vector", lambda e: e.tensor_copy(qsb_sb[:, 3, 32:64], qsb_sb[:, 2, 0:32]),
            reads=[b_qsb[0]], writes=[b_qsb[0]])
        for i, cc in enumerate((ccsq, csxq, ccsk, csxk)):
            d("qs", cs_sb[:], cc, [b_cs[0]])
            def mkt(e, i=i):
                tv = cs_sb[:].rearrange("p (t dd) -> p t dd", dd=64)
                qv = qsb_sb[:, i, :].unsqueeze(1).broadcast_to([128, NT, 64])
                ov = cq_sb[:, i, :].rearrange("p (t dd) -> p t dd", dd=64)
                return e.tensor_tensor(ov, tv, qv, ALU.mult)
            ROP("vector", mkt, reads=[b_cs[0], b_qsb[0]], writes=[b_cq[0]])
        for s in range(6):
            ROP("vector", lambda e, s=s: e.memset(
                v1_sb[:, s, :].rearrange("p (h u) -> p h u", u=65)[:, :, 64:65], 1.0),
                writes=[b_v1[s]])
        ROP("vector", lambda e: e.memset(xs_sb[:, 0, :, 0:1], 0.0), writes=[b_xs[0]])

        sc_reg = [PS["sA"][:, 0:640], PS["sB"][:, 0:640]]
        sc_buf = [PB["sA"], PB["sB"]]
        qkq_reg, qkq_buf = PS["pq"][:, 0:512], PB["pq"]
        qkk_reg, qkk_buf = PS["pk"][:, 0:512], PB["pk"]
        qkv_reg, qkv_buf = PS["pv"][:, 0:512], PB["pv"]
        av_reg = [PS["sA"][:, 640:705], PS["sB"][:, 640:705]]
        av_buf = [PB["sA"], PB["sB"]]
        ptbuf = pb["PT"]

        ktiles = lambda t: list(range(max(0, t - 4), t + 1))

        ntiles = 2 if self.stage == "ffn_dbg" else NT
        def stage_load(t):
            ts = t % 2
            cast_ap = cast_sb[:, ts, :]
            ROP("scalar", lambda e, t=t, a=cast_ap: e.copy(a, slot[:, t * D:(t + 1) * D]),
                reads=[bufs[t]], writes=[b_cast[ts]])
            for fc in range(NF):
                ROP("tensor", lambda e, fc=fc, a=cast_ap: e.transpose(
                    PT[:, fc * 128:(fc + 1) * 128], a[:, fc * 128:(fc + 1) * 128], ident[:]),
                    reads=[b_cast[ts], bident], writes=[ptbuf])
            ROP("vector", lambda e, ts=ts: e.tensor_copy(
                xs_sb[:, ts, :, 1:129], PT[:].rearrange("p (c u) -> p c u", u=128)),
                reads=[ptbuf], writes=[b_xs[ts]])
            if t < NT - 1:
                ROP("vector", lambda e, ts=ts: e.tensor_copy(
                    xs_sb[:, 1 - ts, :, 0:1], xs_sb[:, ts, :, 128:129]),
                    reads=[b_xs[ts]], writes=[b_xs[1 - ts]])
            def xs_view(fc, ts=ts):
                off = 1 if fc < 4 else 0
                return xs_sb[:, ts, fc, off:off + 128]
            for gi, (reg, rbuf) in enumerate(((qkq_reg, qkq_buf), (qkk_reg, qkk_buf), (qkv_reg, qkv_buf))):
                for fc in range(NF):
                    ROP("tensor", lambda e, gi=gi, fc=fc, reg=reg, ts=ts, xv=xs_view: e.matmul(
                        reg, xv(fc, ts), wqkv_sb[:, fc, gi * 512:(gi + 1) * 512],
                        start=(fc == 0), stop=(fc == NF - 1)),
                        reads=[b_xs[ts], b_wqkv[0]], writes=[rbuf])
            vs = t % 6
            ROP("vector", lambda e, vs=vs, qr=qkv_reg: e.tensor_copy(
                v1_sb[:, vs, :].rearrange("p (h u) -> p h u", u=65)[:, :, 0:64],
                qr.rearrange("p (h u) -> p h u", u=64)),
                reads=[qkv_buf], writes=[b_v1[vs]])

        def stage_prep(t):
            ts = t % 2
            tcol = t % 4
            rnb = b_rn[tcol]
            rss = rn_sb[:, tcol, :]
            streams = [
                (qkq_reg, qkq_buf, 0, 1, qf_sb[:, ts, :], b_qf[ts]),
                (qkk_reg, qkk_buf, 2, 3, kf_sb[:, t % 6, :], b_kf[t % 6]),
            ]
            for qi, (reg, rbuf, tabc, tabs, dstf, dstfb) in enumerate(streams):
                ROP("scalar", lambda e, reg=reg, qi=qi: e.activation(
                    sqs_sb[:, qi * 512:qi * 512 + 512], reg, AF.Square),
                    reads=[rbuf], writes=[b_sqs[0]])
            for qi, (reg, rbuf, tabc, tabs, dstf, dstfb) in enumerate(streams):
                sqv = sqs_sb[:, qi * 512:qi * 512 + 512].rearrange("p (h u) -> p h u", u=64)
                ROP("vector", lambda e, qi=qi, sqv=sqv, rss=rss: e.reduce_sum(
                    rss[:, qi * 8:qi * 8 + 8].unsqueeze(2), sqv, axis=AX.X),
                    reads=[b_sqs[0]], writes=[rnb])
            tiny = constcol(1e-24)
            nhalf = constcol(-0.5)
            ROP("scalar", lambda e, rss=rss: e.activation(rss[:, 16:32], rss[:, 0:16], AF.Ln, bias=tiny),
                reads=[rnb, bcc], writes=[rnb])
            ROP("scalar", lambda e, rss=rss: e.activation(rss[:, 0:16], rss[:, 16:32], AF.Exp, scale=nhalf),
                reads=[rnb, bcc], writes=[rnb])
            for qi, (reg, rbuf, tabc, tabs, dstf, dstfb) in enumerate(streams):
                regv = reg.rearrange("p (h u) -> p h u", u=64)
                qnv = qn_sb[:, qi, :].rearrange("p (h u) -> p h u", u=64)
                rnv = rss[:, qi * 8:qi * 8 + 8].unsqueeze(2).broadcast_to([128, 8, 64])
                ROP("vector", lambda e, regv=regv, rnv=rnv, qnv=qnv: e.tensor_tensor(
                    qnv, regv, rnv, ALU.mult), reads=[rbuf, rnb], writes=[b_qn[qi]])
            for qi, (reg, rbuf, tabc, tabs, dstf, dstfb) in enumerate(streams):
                qnv = qn_sb[:, qi, :].rearrange("p (h u) -> p h u", u=64)
                c1v = c1_sb[:, qi, :].rearrange("p (h u) -> p h u", u=64)
                tc = cq_sb[:, tabc, t * 64:(t + 1) * 64].unsqueeze(1).broadcast_to([128, 8, 64])
                ROP("gpsimd", lambda e, qnv=qnv, c1v=c1v, tc=tc: e.tensor_tensor(
                    c1v, qnv, tc, ALU.mult), reads=[b_qn[qi], b_cq[0]], writes=[b_c1[qi]])
            for qi, (reg, rbuf, tabc, tabs, dstf, dstfb) in enumerate(streams):
                qnv = qn_sb[:, qi, :].rearrange("p (h u) -> p h u", u=64)
                c2v = c2_sb[:, qi, :].rearrange("p (h u) -> p h u", u=64)
                qsw0 = qnv[:, :, 32:64]
                qsw1 = qnv[:, :, 0:32]
                ts0 = cq_sb[:, tabs, t * 64:t * 64 + 32].unsqueeze(1).broadcast_to([128, 8, 32])
                ts1 = cq_sb[:, tabs, t * 64 + 32:t * 64 + 64].unsqueeze(1).broadcast_to([128, 8, 32])
                ROP("vector", lambda e, qsw0=qsw0, c2v=c2v, ts0=ts0: e.tensor_tensor(
                    c2v[:, :, 0:32], qsw0, ts0, ALU.mult), reads=[b_qn[qi], b_cq[0]], writes=[b_c2[qi]])
                ROP("vector", lambda e, qsw1=qsw1, c2v=c2v, ts1=ts1: e.tensor_tensor(
                    c2v[:, :, 32:64], qsw1, ts1, ALU.mult), reads=[b_qn[qi], b_cq[0]], writes=[b_c2[qi]])
            for qi, (reg, rbuf, tabc, tabs, dstf, dstfb) in enumerate(streams):
                pbf = (qpb_sb, kpb_sb)[qi]
                pbfb = (b_qpb, b_kpb)[qi][0]
                ROP("vector", lambda e, pbf=pbf, qi=qi: e.tensor_add(
                    pbf[:], c1_sb[:, qi, :], c2_sb[:, qi, :]),
                    reads=[b_c1[qi], b_c2[qi]], writes=[pbfb])
                for ci2 in range(4):
                    ROP("tensor", lambda e, ci2=ci2, qi=qi, pbf=pbf: e.transpose(
                        PT[:, qi * 512 + ci2 * 128: qi * 512 + (ci2 + 1) * 128],
                        pbf[:, ci2 * 128:(ci2 + 1) * 128], ident[:]),
                        reads=[pbfb, bident], writes=[ptbuf])
                ROP("vector", lambda e, qi=qi, dstf=dstf: e.tensor_copy(
                    dstf, PT[:, qi * 512:(qi + 1) * 512]), reads=[ptbuf], writes=[dstfb])


        def stage_attn(t):
            ts = t % 2
            kts = ktiles(t)
            m = len(kts)
            for h in range(FH):
                acol = 64 + (t % 4) * 8 + (h % 4) * 2
                amax = stat[:, acol:acol + 1]
                rcol = stat[:, acol + 1:acol + 2]
                bamax = b_stat[16 + (t % 4) * 4 + h % 4]
                hreg = h % 2
                sreg, sbuf_ = sc_reg[hreg], sc_buf[hreg]
                hp = (h % 2) * 64
                hc = h // 2
                for ci2, kt in enumerate(kts):
                    ROP("tensor", lambda e, ci2=ci2, kt=kt, sreg=sreg, hp=hp, hc=hc, ts=ts: e.matmul(
                        sreg[:, ci2 * 128:(ci2 + 1) * 128],
                        qf_sb[hp:hp + 64, ts, hc * 128:(hc + 1) * 128],
                        kf_sb[hp:hp + 64, kt % 6, hc * 128:(hc + 1) * 128],
                        start=True, stop=True),
                        reads=[b_qf[ts], b_kf[kt % 6]], writes=[sbuf_])
                ROP("vector", lambda e, sreg=sreg, m=m, amax=amax: e.reduce_max(
                    amax, sreg[:, 0:m * 128], axis=AX.X, negate=True),
                    reads=[sbuf_], writes=[bamax])
                es_ = h % 2
                ROP("scalar", lambda e, sreg=sreg, m=m, es_=es_, amax=amax: e.activation(
                    exp_sb[:, es_, 0:m * 128], sreg[:, 0:m * 128], AF.Exp, bias=amax),
                    reads=[sbuf_, bamax], writes=[b_exp[es_]])
                for ci2 in range(m):
                    ROP("tensor", lambda e, ci2=ci2, es_=es_: e.transpose(
                        PT[:, ci2 * 128:(ci2 + 1) * 128],
                        exp_sb[:, es_, ci2 * 128:(ci2 + 1) * 128], ident[:]),
                        reads=[b_exp[es_], bident], writes=[ptbuf])
                ROP("vector", lambda e, m=m, es_=es_: e.tensor_tensor(
                    at_sb[:, es_, 0:m * 128], PT[:, 0:m * 128],
                    maskf[:, (5 - m) * 128:640], ALU.mult),
                    reads=[ptbuf, bmaskf], writes=[b_at[es_]])
                areg, abuf_ = av_reg[hreg], av_buf[hreg]
                for ci2, kt in enumerate(kts):
                    ROP("tensor", lambda e, ci2=ci2, kt=kt, areg=areg, h=h, es_=es_, m=m: e.matmul(
                        areg, at_sb[:, es_, ci2 * 128:(ci2 + 1) * 128],
                        v1_sb[:, kt % 6, h * 65:(h + 1) * 65],
                        start=(ci2 == 0), stop=(ci2 == m - 1)),
                        reads=[b_at[es_], b_v1[kt % 6]], writes=[abuf_])
                ROP("vector", lambda e, areg=areg, rcol=rcol: e.reciprocal(rcol, areg[:, 64:65]),
                    reads=[abuf_], writes=[bamax])
                ROP("scalar", lambda e, areg=areg, h=h, ts=ts, rcol=rcol: e.activation(
                    otm_sb[:, ts, h * 64:(h + 1) * 64], areg[:, 0:64], AF.Copy, scale=rcol),
                    reads=[abuf_, bamax], writes=[b_otm[ts]])
            for ci2 in range(4):
                ROP("tensor", lambda e, ci2=ci2, ts=ts: e.transpose(
                    PT[:, ci2 * 128:(ci2 + 1) * 128], otm_sb[:, ts, ci2 * 128:(ci2 + 1) * 128],
                    ident[:]), reads=[b_otm[ts], bident], writes=[ptbuf])
            ROP("vector", lambda e, ts=ts: e.tensor_copy(ofm_sb[:, ts, :], PT[:, 0:512]),
                reads=[ptbuf], writes=[b_ofm[ts]])
            for oh in range(2):
                wreg, wbuf_ = ((sc_reg[0][:, 0:512], sc_buf[0]), (sc_reg[1][:, 0:512], sc_buf[1]))[oh]
                for ci2 in range(4):
                    ROP("tensor", lambda e, ci2=ci2, oh=oh, wreg=wreg, ts=ts: e.matmul(
                        wreg, ofm_sb[:, ts, ci2 * 128:(ci2 + 1) * 128],
                        wo_sb[:, ci2, oh * 512:(oh + 1) * 512],
                        start=(ci2 == 0), stop=(ci2 == 3)),
                        reads=[b_ofm[ts], b_wo[0]], writes=[wbuf_])
                ROP("vector", lambda e, oh=oh, t=t, wreg=wreg: e.tensor_tensor(
                    slot[:, t * D + oh * 512: t * D + (oh + 1) * 512], wreg,
                    slot[:, t * D + oh * 512: t * D + (oh + 1) * 512], ALU.add),
                    reads=[wbuf_, bufs[t]], writes=[bufs[t]])
            ROP("scalar", lambda e, t=t: e.activation(
                sqs_sb[:, 0:512], slot[:, t * D:t * D + 512], AF.Square,
                accum_out=st2_sb[:, 16 + t:17 + t]),
                reads=[bufs[t]], writes=[b_sqs[0], b_st2g[0]])
            ROP("scalar", lambda e, t=t: e.activation(
                sqs_sb[:, 512:1024], slot[:, t * D + 512:(t + 1) * D], AF.Square,
                accum_out=st2_sb[:, 48 + t:49 + t]),
                reads=[bufs[t]], writes=[b_sqs[0], b_st2g[0]])
            ROP("vector", lambda e, t=t: e.reduce_sum(
                st2_sb[:, t:t + 1], slot[:, t * D:t * D + 512], axis=AX.X),
                reads=[bufs[t]], writes=[b_st2g[0]])
            ROP("vector", lambda e, t=t: e.reduce_sum(
                st2_sb[:, 32 + t:33 + t], slot[:, t * D + 512:(t + 1) * D], axis=AX.X),
                reads=[bufs[t]], writes=[b_st2g[0]])


        for t in range(ntiles + 1):
            if t < ntiles:
                stage_load(t)
            if t >= 1:
                stage_attn(t - 1)
            if t < ntiles:
                stage_prep(t)

        if self.stage == "ffn_dbg" and l == 0 and call == 0:
            dbg = {"dbg_cast1": (cast_sb[:, 1, :], b_cast[1]),
                   "dbg_qf1": (qf_sb[:, 1, :], b_qf[1]),
                   "dbg_kf0": (kf_sb[:, 0, :], b_kf[0]),
                   "dbg_kf1": (kf_sb[:, 1, :], b_kf[1]),
                   "dbg_v10": (v1_sb[:, 0, :], b_v1[0]),
                   "dbg_v11": (v1_sb[:, 1, :], b_v1[1]),
                   "dbg_otm1": (otm_sb[:, 1, :], b_otm[1]),
                   "dbg_exp1": (exp_sb[:, 1, :], b_exp[1]),
                   "dbg_at1": (at_sb[:, 1, :], b_at[1]),
                   "dbg_xs1": (xs_sb[:, 1, :, :].rearrange("p a b -> p (a b)"), b_xs[1]),
                   "dbg_cq": (cq_sb[:, 0, :], b_cq[0]),
                   }
            for nm, (ap, bb) in dbg.items():
                o = self.nc.dram_tensor(nm, [128, ap.shape[-1] if ap.ndim == 2 else int(np.prod(ap.shape[1:]))],
                                        ap.dtype, kind="ExternalOutput").ap()
                r.dma("qs", (lambda e, o=o, ap=ap: e.dma_start(out=o, in_=ap)),
                      reads=[bb], writes=[Buf("dbo_" + nm)])
            self.emit_out(slot, bufs)
            self.done = True
            return

        ar.free(wqkv_sb, wo_sb, cs_sb, qsb_sb, qsrow_bf, xs_sb, cast_sb, qn_sb,
                c1_sb, c2_sb, qpb_sb, kpb_sb)
        ar.free(cq_sb, qf_sb, kf_sb, v1_sb, exp_sb, at_sb, otm_sb, ofm_sb, rn_sb)
        pes.close()
        self.psum_phase_end(PB)

        if self.stage == f"ffn_a{l}" and call == 0 and l == 1:
            self.emit_out(slot, bufs)
            self.done = True
            return
        if self.stage == f"ffn_a{l}" and call == 0:
            self.emit_out(slot, bufs)
            self.done = True
            return

        self.ffn_mlp(slot, bufs, l, call)
        if self.stage == f"ffn_l{l}" and call == 0:
            self.emit_out(slot, bufs)
            self.done = True
            return

    def ffn_mlp(self, slot, bufs, l, call):
        c = self.ctx
        nc, r, ar = self.nc, self.rec, self.arena
        ROP = r.op
        d = c["d"]
        PT = self.P["PT"]
        pb = self.pb
        ident = c["ident"]; bident = c["bident"][0]
        identf = c["identf"]; bidentf = c["bidentf"][0]
        fw1, fb1, fw2 = c["fw1"], c["fb1"], c["fw2"]
        lg_pp, lb_pp = c["lg_pp"], c["lb_pp"]
        b_lg, b_lb = c["b_lg"][0], c["b_lb"][0]
        b1a_pp, b1g_pp, b2_pp = c["b1a_pp"], c["b1g_pp"], c["b2_pp"]
        b_b1a, b_b1g, b_b2 = c["b_b1a"][0], c["b_b1g"][0], c["b_b2"][0]
        scr = c["scr"]; b_scr = c["b_scr"][0]
        constcol = self.constcol
        bcc = self.b_ccol

        st2, b_st2 = self._st2, self._b_st2
        sh2, b_sh2 = ar.alloc("sh2", [128, 32], F32)
        cm_sb, b_cm = ar.alloc("cm", [128, 32], F32)
        mr_sb, b_mr = ar.alloc("mr", [128, 64], F32)
        ROP("vector", lambda e: e.memset(mr_sb[:], 0.0), writes=[b_mr[0]])
        nrm_sb, b_nrm = ar.alloc("nrm", [128, 2, D], BF16, 2)
        ln2_sb, b_ln2 = ar.alloc("ln2", [128, NF, 1025], BF16)
        bnd_sb, b_bnd = ar.alloc("bnd", [128, NF, 1], BF16)
        h2_sb, b_h2 = ar.alloc("h2", [128, NH, 1024], BF16)
        w1b_sb, b_w1b = ar.alloc("w1b", [128, 2, NF, 256], BF16, 2)
        abf_sb, b_abf = ar.alloc("abf", [128, 2, 512], BF16, 2)
        gbf_sb, b_gbf = ar.alloc("gbf", [128, 2, 512], BF16, 2)
        wof_sb, b_wof = ar.alloc("wof", [128, NH, 128], BF16)
        w2t_sb, b_w2t = ar.alloc("w2t", [128, 1024], F32)

        ROP("vector", lambda e: e.memset(sh2[0:1, :], 0.0), writes=[b_sh2[0]])
        d("qp", sh2[1:128, 0:16], st2[0:127, 32:48], [b_sh2[0]], rbufs=[b_st2[0]])
        d("qp", sh2[1:128, 16:32], st2[0:127, 48:64], [b_sh2[0]], rbufs=[b_st2[0]])
        d("qp", sh2[0:1, 1:16], st2[127:128, 32:47], [b_sh2[0]], rbufs=[b_st2[0]])
        d("qp", sh2[0:1, 17:32], st2[127:128, 48:63], [b_sh2[0]], rbufs=[b_st2[0]])
        ROP("vector", lambda e: e.tensor_add(cm_sb[:], st2[:, 0:32], sh2[:]),
            reads=[b_st2[0], b_sh2[0]], writes=[b_cm[0]])
        ROP("vector", lambda e: e.tensor_scalar(mr_sb[:, 0:16], cm_sb[:, 0:16], -1.0 / D, None, ALU.mult),
            reads=[b_cm[0]], writes=[b_mr[0]])
        ROP("vector", lambda e: e.tensor_mul(mr_sb[:, 16:32], mr_sb[:, 0:16], mr_sb[:, 0:16]),
            reads=[b_mr[0]], writes=[b_mr[0]])
        ROP("vector", lambda e: e.scalar_tensor_tensor(
            mr_sb[:, 32:48], cm_sb[:, 16:32], 1.0 / D, mr_sb[:, 16:32], ALU.mult, ALU.subtract),
            reads=[b_cm[0], b_mr[0]], writes=[b_mr[0]])
        eps5 = constcol(1e-5)
        nhalf = constcol(-0.5)
        ROP("scalar", lambda e: e.activation(mr_sb[:, 48:64], mr_sb[:, 32:48], AF.Ln, bias=eps5),
            reads=[b_mr[0], bcc], writes=[b_mr[0]])
        ROP("scalar", lambda e: e.activation(mr_sb[:, 16:32], mr_sb[:, 48:64], AF.Exp, scale=nhalf),
            reads=[b_mr[0], bcc], writes=[b_mr[0]])
        d("qp", mr_sb[0:127, 32:64], mr_sb[1:128, 0:32], [b_mr[0]], rbufs=[b_mr[0]])
        d("qp", mr_sb[127:128, 32:47], mr_sb[0:1, 1:16], [b_mr[0]], rbufs=[b_mr[0]])
        d("qp", mr_sb[127:128, 48:63], mr_sb[0:1, 17:32], [b_mr[0]], rbufs=[b_mr[0]])


        pes = ExitStack()
        PS, PB = self.psum_phase(pes, {"pa0": [128, 512], "pa1": [128, 512],
                                       "pg0": [128, 512], "pg1": [128, 512]})
        a_reg = [(PS["pa0"][:], PB["pa0"]), (PS["pa1"][:], PB["pa1"])]
        g_reg = [(PS["pg0"][:], PB["pg0"]), (PS["pg1"][:], PB["pg1"])]
        tb_reg = [(PS["pg0"][:, 0:128], PB["pg0"]), (PS["pg1"][:, 0:128], PB["pg1"])]
        ptbuf = pb["PT"]

        for g in range(2):
            if g == 0:
                ROP("vector", lambda e: e.memset(ln2_sb[:, :, 0:1], 0.0), writes=[b_ln2[0]])
            else:
                ROP("vector", lambda e: e.tensor_copy(ln2_sb[:, :, 0:1], bnd_sb[:]),
                    reads=[b_bnd[0]], writes=[b_ln2[0]])
            for tt in range(8):
                t = 8 * g + tt
                ns = tt % 2
                ROP("vector", lambda e, t=t, ns=ns: e.tensor_scalar(
                    nrm_sb[:, ns, 0:512], slot[:, t * D:t * D + 512],
                    mr_sb[:, t:t + 1], mr_sb[:, 16 + t:17 + t], ALU.add, ALU.mult),
                    reads=[bufs[t], b_mr[0]], writes=[b_nrm[ns]])
                ROP("vector", lambda e, t=t, ns=ns: e.tensor_scalar(
                    nrm_sb[:, ns, 512:1024], slot[:, t * D + 512:(t + 1) * D],
                    mr_sb[:, 32 + t:33 + t], mr_sb[:, 48 + t:49 + t], ALU.add, ALU.mult),
                    reads=[bufs[t], b_mr[0]], writes=[b_nrm[ns]])
                for fc in range(NF):
                    ROP("tensor", lambda e, fc=fc, ns=ns: e.transpose(
                        PT[:, fc * 128:(fc + 1) * 128], nrm_sb[:, ns, fc * 128:(fc + 1) * 128],
                        ident[:]), reads=[b_nrm[ns], bident], writes=[ptbuf])
                for fc in range(NF):
                    ROP("vector", lambda e, fc=fc, tt=tt, lc=l * 8 + fc: e.tensor_scalar(
                        ln2_sb[:, fc, 1 + tt * 128:1 + (tt + 1) * 128],
                        PT[:, fc * 128:(fc + 1) * 128],
                        lg_pp[:, lc:lc + 1], lb_pp[:, lc:lc + 1], ALU.mult, ALU.add),
                        reads=[ptbuf, b_lg, b_lb], writes=[b_ln2[0]])
            if g == 0:
                ROP("vector", lambda e: e.tensor_copy(bnd_sb[:], ln2_sb[:, :, 1024:1025]),
                    reads=[b_ln2[0]], writes=[b_bnd[0]])

            for j in range(NH):
                wj = min(128, HID - 128 * j)
                jp = j % 2
                d("qp", w1b_sb[:, jp, :, 0:wj],
                  fw1[l, :, 128 * j:128 * j + wj].rearrange("(a p) h -> p a h", p=128),
                  [b_w1b[jp]])
                d("qp", w1b_sb[:, jp, :, 128:128 + wj],
                  fw1[l, :, HID + 128 * j:HID + 128 * j + wj].rearrange("(a p) h -> p a h", p=128),
                  [b_w1b[jp]])
                for th in range(2):
                    (pa, pab), (pg, pgb) = a_reg[th], g_reg[th]
                    for fc in range(NF):
                        off = (1 if fc < 4 else 0) + th * 512
                        ROP("tensor", lambda e, fc=fc, jp=jp, wj=wj, pa=pa, off=off: e.matmul(
                            pa[0:wj, :], w1b_sb[:, jp, fc, 0:wj], ln2_sb[:, fc, off:off + 512],
                            start=(fc == 0), stop=(fc == NF - 1)),
                            reads=[b_w1b[jp], b_ln2[0]], writes=[pab])
                    for fc in range(NF):
                        off = (1 if fc < 4 else 0) + th * 512
                        ROP("tensor", lambda e, fc=fc, jp=jp, wj=wj, pg=pg, off=off: e.matmul(
                            pg[0:wj, :], w1b_sb[:, jp, fc, 128:128 + wj], ln2_sb[:, fc, off:off + 512],
                            start=(fc == 0), stop=(fc == NF - 1)),
                            reads=[b_w1b[jp], b_ln2[0]], writes=[pgb])
                    ROP("vector", lambda e, th=th, j=j, pa=pa: e.tensor_scalar(
                        abf_sb[:, th, :], pa, b1a_pp[:, l * NH + j:l * NH + j + 1], None, ALU.add),
                        reads=[pab, b_b1a], writes=[b_abf[th]])
                    ROP("scalar", lambda e, th=th, j=j, pg=pg: e.activation(
                        gbf_sb[:, th, :], pg, AF.Gelu, bias=b1g_pp[:, l * NH + j:l * NH + j + 1]),
                        reads=[pgb, b_b1g], writes=[b_gbf[th]])
                    ROP("vector", lambda e, th=th, j=j: e.tensor_mul(
                        h2_sb[:, j, th * 512:(th + 1) * 512], abf_sb[:, th, :], gbf_sb[:, th, :]),
                        reads=[b_abf[th], b_gbf[th]], writes=[b_h2[0]])

            for of in range(8):
                d("qp", wof_sb[:, 0:21, :],
                  fw2[l, 0:2688, of * 128:(of + 1) * 128].rearrange("(j p) o -> p j o", p=128),
                  [b_wof[0]])
                d("qp", wof_sb[0:42, 21, :], fw2[l, 2688:HID, of * 128:(of + 1) * 128],
                  [b_wof[0]])
                for th in range(2):
                    pa, pab = a_reg[th]
                    for j in range(NH):
                        wj = min(128, HID - 128 * j)
                        ROP("tensor", lambda e, j=j, wj=wj, th=th, pa=pa: e.matmul(
                            pa, wof_sb[0:wj, j, :], h2_sb[0:wj, j, th * 512:(th + 1) * 512],
                            start=(j == 0), stop=(j == NH - 1)),
                            reads=[b_wof[0], b_h2[0]], writes=[pab])
                    ROP("vector", lambda e, th=th, of=of, pa=pa: e.tensor_scalar(
                        w2t_sb[:, th * 512:(th + 1) * 512], pa,
                        b2_pp[:, l * 8 + of:l * 8 + of + 1], None, ALU.add),
                        reads=[pab, b_b2], writes=[b_w2t[0]])
                for tt in range(8):
                    t = 8 * g + tt
                    (tr, trb) = tb_reg[tt % 2]
                    ROP("tensor", lambda e, tt=tt, tr=tr: e.transpose(
                        tr[:, 0:128], w2t_sb[:, tt * 128:(tt + 1) * 128], identf[:]),
                        reads=[b_w2t[0], bidentf], writes=[trb])
                    ROP("vector", lambda e, t=t, of=of, tr=tr: e.tensor_tensor(
                        slot[:, t * D + of * 128:t * D + (of + 1) * 128], tr[:, 0:128],
                        slot[:, t * D + of * 128:t * D + (of + 1) * 128], ALU.add),
                        reads=[trb, bufs[t]], writes=[bufs[t]])

        ar.free(st2, sh2, cm_sb, mr_sb, nrm_sb, ln2_sb, bnd_sb, h2_sb, w1b_sb,
                abf_sb, gbf_sb, wof_sb, w2t_sb)
        self._st2 = None
        pes.close()
        self.psum_phase_end(PB)

    def linattn(self, slotX, bufX, slotO, bufO):
        c = self.ctx
        nc, r, ar = self.nc, self.rec, self.arena
        ROP = r.op
        d = c["d"]
        PT = self.P["PT"]
        pb = self.pb
        ident = c["ident"]; bident = c["bident"][0]
        maskl = c["maskl"]; bmaskl = c["bmaskl"][0]
        onesb = c["onesb"]; bonesb = c["bonesb"][0]
        aw_qkv, aw_out, ab_qkv, ab_out = c["aw_qkv"], c["aw_out"], c["ab_qkv"], c["ab_out"]
        ln_full_tile = c["ln_full_tile"]
        QN = 4

        abrow, b_abrow = ar.alloc("abrow", [1, 3072], BF16)
        aorow, b_aorow = ar.alloc("aorow", [1, 1024], BF16)
        S_sb, b_S = ar.alloc("S", [128, 2, 520], F32, 2)
        Sb_sb, b_Sb = ar.alloc("Sbf", [128, 2, 520], BF16, 2)
        nrc_sb, b_nrc = ar.alloc("nrc", [128, 4], F32, 2)
        qtm_sb, b_qtm = ar.alloc("qtm", [128, QN, 1024], BF16, QN)
        ktm_sb, b_ktm = ar.alloc("ktm", [128, QN, 1024], BF16, QN)
        v1_sb, b_v1 = ar.alloc("v1l", [128, QN, LH * 65], BF16, QN)
        d("qp", abrow[:], ab_qkv.unsqueeze(0), [b_abrow[0]])
        d("qp", aorow[:], ab_out.unsqueeze(0), [b_aorow[0]])
        ROP("vector", lambda e: e.memset(S_sb[:, 1, :], 0.0), writes=[b_S[1]])
        ROP("vector", lambda e: e.memset(Sb_sb[:, 1, :], 0.0), writes=[b_Sb[1]])

        ptbuf = pb["PT"]

        def _quarter(qr):
            tiles = list(range(QN * qr, QN * qr + QN))
            ln1_sb, b_ln1 = ar.alloc("ln1f", [128, NF, QN * 128], BF16)
            lncast_sb, b_lncast = ar.alloc("lncast", [128, 2, D], BF16, 2)
            wst_sb, b_wst = ar.alloc("wst", [128, 1024], BF16)
            wt_sb, b_wt = ar.alloc("wtog", [128, NF, 512], BF16)
            emin_sb, b_emin = ar.alloc("emin", [128, 512], F32)
            eexp_sb, b_eexp = ar.alloc("eexp", [128, 512], F32)
            for s in range(QN):
                ROP("vector", lambda e, s=s: e.memset(
                    v1_sb[:, s, :].rearrange("p (h u) -> p h u", u=65)[:, :, 64:65], 1.0),
                    writes=[b_v1[s]])
            for tt, t in enumerate(tiles):
                ns = tt % 2
                ln_full_tile(slotX[:, t * D:(t + 1) * D], [bufX[t]],
                             lncast_sb[:, ns, :], [b_lncast[ns]], t % 16)
                for fc in range(NF):
                    ROP("tensor", lambda e, fc=fc, ns=ns: e.transpose(
                        PT[:, fc * 128:(fc + 1) * 128],
                        lncast_sb[:, ns, fc * 128:(fc + 1) * 128], ident[:]),
                        reads=[b_lncast[ns], bident], writes=[ptbuf])
                ROP("vector", lambda e, tt=tt: e.tensor_copy(
                    ln1_sb[:, :, tt * 128:(tt + 1) * 128],
                    PT[:].rearrange("p (cc u) -> p cc u", u=128)),
                    reads=[ptbuf], writes=[b_ln1[0]])
            pesq = ExitStack()
            PSq, PBq = self.psum_phase(pesq, {"l0": [128, 512], "l1": [128, 512],
                                              "l2": [128, 512]})
            qkrot = [(PSq["l0"][:], PBq["l0"]), (PSq["l1"][:], PBq["l1"]),
                     (PSq["l2"][:], PBq["l2"])]
            if self.stage in ("lins_dbg", "lins_dbg2"):
                def _mkcb(tag, names):
                    def _cb(sim, tag=tag, names=names):
                        import numpy as np
                        msg = [tag]
                        for nm in names:
                            try:
                                a = np.array(sim.tensor(nm)).astype(np.float32)
                                fin = np.isfinite(a)
                                msg.append(f"{nm}:max={np.abs(a[fin]).max() if fin.any() else -1:.4f},nan={int((~fin).sum())}")
                            except Exception as ex:
                                msg.append(f"{nm}:ERR")
                        print("[CB]", *msg)
                    return _cb
                def _fn0(e):
                    from concourse.bass_interp import add_callback
                    return add_callback(e, _mkcb("post-ln1", [ln1_sb.name]))
                r.op("vector", _fn0, reads=[b_ln1[0]])
            for og in range(6):
                for i in range(4):
                    d("qp", wst_sb[:], aw_qkv[og * 512 + i * 128: og * 512 + (i + 1) * 128, :],
                      [b_wst[0]])
                    for fc in range(NF):
                        ROP("tensor", lambda e, fc=fc, w=wst_sb: e.transpose(
                            PT[:, fc * 128:(fc + 1) * 128], w[:, fc * 128:(fc + 1) * 128],
                            ident[:]), reads=[b_wst[0], bident], writes=[ptbuf])
                    if self.stage in ("lins_dbg", "lins_dbg2", "linq_dbg") and og == 0 and i == 0:
                        def _fnq(e):
                            from concourse.bass_interp import add_callback2
                            def _cbq(sim, inst):
                                import numpy as np
                                from concourse.bass_interp import Direction
                                v = sim.instruction_executor.view_arg(
                                    inst.ins[0], Direction.READ, inst, reg_snapshot=None)
                                a = np.asarray(v).astype(np.float32)
                                fin = np.isfinite(a)
                                print("[CBQ] PT view:", a.shape,
                                      "max", np.abs(a[fin]).max() if fin.any() else -1,
                                      "nan", int((~fin).sum()))
                                try:
                                    sems = {k: v for k, v in sim.semaphores.items()} if hasattr(sim, "semaphores") else None
                                    print("[CBQ] sems:", sems)
                                except Exception as ex:
                                    print("[CBQ] semerr", ex)
                                st = [x for x in dir(sim) if "sem" in x.lower() or "engine" in x.lower()]
                                print("[CBQ] attrs:", st[:12])
                            return add_callback2(e, _cbq, [PT[:]])
                        r.op("vector", _fnq, reads=[ptbuf])
                        _lasttok = ptbuf.w
                        def _fnp(e, _lt=_lasttok):
                            from concourse.bass_interp import add_callback
                            def _cbp(sim, _lt=_lt):
                                import numpy as np
                                for nm2 in ("PsT", "PsT_set", "pp1_l0"):
                                    try:
                                        a = np.array(sim.tensor(nm2)).astype(np.float32)
                                        fin = np.isfinite(a)
                                        print("[CB]", nm2, "shape", a.shape, "max",
                                              np.abs(a[fin]).max() if fin.any() else -1,
                                              "nan", int((~fin).sum()))
                                    except Exception as ex:
                                        print("[CB]", nm2, "ERR", str(ex)[:80])
                            return add_callback(e, _cbp)
                        r.op("vector", _fnp, reads=[ptbuf])
                    ROP("vector", lambda e, i=i: e.tensor_copy(
                        wt_sb[:, :, i * 128:(i + 1) * 128],
                        PT[:].rearrange("p (cc u) -> p cc u", u=128)),
                        reads=[ptbuf], writes=[b_wt[0]])
                for tt in range(QN):
                    ps, psb = qkrot[tt % 3]
                    for fc in range(NF):
                        ROP("tensor", lambda e, fc=fc, tt=tt, ps=ps: e.matmul(
                            ps, ln1_sb[:, fc, tt * 128:(tt + 1) * 128], wt_sb[:, fc, :],
                            start=(fc == 0), stop=False),
                            reads=[b_ln1[0], b_wt[0]], writes=[psb])
                    ROP("tensor", lambda e, og=og, ps=ps: e.matmul(
                        ps, onesb[:], abrow[0:1, og * 512:(og + 1) * 512],
                        start=False, stop=True),
                        reads=[bonesb, b_abrow[0]], writes=[psb])
                    if og < 4:
                        dst = qtm_sb if og < 2 else ktm_sb
                        dbuf = (b_qtm if og < 2 else b_ktm)[tt]
                        dslice = (og % 2) * 512
                        if self.stage == "lins_dbg3" and og == 0 and tt == 0 and qr == 0:
                            psnap, b_psnap = ar.alloc("psnap", [128, 512], F32)
                            self._psnap = psnap
                            ROP("vector", lambda e, ps=ps: e.tensor_copy(psnap[:], ps),
                                reads=[psb], writes=[b_psnap[0]])
                        ROP("vector", lambda e, ps=ps: e.tensor_scalar_min(emin_sb[:], ps, 0.0),
                            reads=[psb], writes=[b_emin[0]])
                        ROP("scalar", lambda e: e.activation(eexp_sb[:], emin_sb[:], AF.Exp),
                            reads=[b_emin[0]], writes=[b_eexp[0]])
                        ROP("vector", lambda e, ps=ps, dst=dst, tt=tt, dslice=dslice: e.scalar_tensor_tensor(
                            dst[:, tt, dslice:dslice + 512], ps, 0.0, eexp_sb[:],
                            ALU.max, ALU.add),
                            reads=[psb, b_eexp[0]], writes=[dbuf])
                    else:
                        hof = 8 * (og - 4)
                        ROP("vector", lambda e, ps=ps, tt=tt, hof=hof: e.tensor_copy(
                            v1_sb[:, tt, hof * 65:(hof + 8) * 65].rearrange(
                                "p (h u) -> p h u", u=65)[:, :, 0:64],
                            ps.rearrange("p (h u) -> p h u", u=64)),
                            reads=[psb], writes=[b_v1[tt]])
                if self.stage in ("lins_dbg", "lins_dbg2"):
                    def _fn1(e, og=og):
                        from concourse.bass_interp import add_callback
                        return add_callback(e, _mkcb(f"post-og{og}",
                            [wt_sb.name, qtm_sb.name, v1_sb.name]))
                    r.op("vector", _fn1, reads=[b_wt[0], b_qtm[0], b_v1[0]])
            if self.stage == "linq_dbg":
                self.done = True
                return
            if self.stage == "lins_dbg":
                def _cb(sim, qtm_t=qtm_sb):
                    import numpy as np
                    a = np.array(sim.tensors.mems.get("ar7_qtm_53", [[0]])) if hasattr(sim, "tensors") else None
                    try:
                        a = np.array(sim.tensor("ar7_qtm_53")).astype(np.float32)
                        print("[CB] post-qkv qtm absmax:", np.abs(a).max())
                    except Exception as ex:
                        print("[CB] err", ex)
                def _fn(e):
                    from concourse.bass_interp import add_callback
                    return add_callback(e, _cb)
                r.op("vector", _fn, reads=[b_qtm[0], b_qtm[1], b_qtm[2], b_qtm[3]])
            ar.free(ln1_sb, lncast_sb, wst_sb, wt_sb, emin_sb, eexp_sb)
            pesq.close()
            self.psum_phase_end(PBq)

            pess = ExitStack()
            PSs, PBs = self.psum_phase(pess, {
                "at0": [128, 128], "at1": [128, 128], "at2": [128, 128],
                "dsA": [128, 512], "dsB": [128, 512],
                "avA": [128, 128], "avB": [128, 128]})
            woT_sb, b_woT = ar.alloc("woT", [128, NF, 1024], BF16)
            wos_sb, b_wos = ar.alloc("wst2", [128, 1024], BF16)
            qf_sb, b_qf = ar.alloc("qfl", [128, 2, 1024], BF16, 2)
            kf_sb, b_kf = ar.alloc("kfl", [128, 2, 1024], BF16, 2)
            at_sb, b_at = ar.alloc("atl", [128, 3, 128], BF16, 3)
            otm_sb, b_otm = ar.alloc("otml", [128, 1, 1024], BF16, 1)
            ofm_sb, b_ofm = ar.alloc("ofml", [128, 1, 1024], BF16, 1)
            for i in range(NF):
                d("qp", wos_sb[:], aw_out[i * 128:(i + 1) * 128, :], [b_wos[0]])
                for fc in range(NF):
                    ROP("tensor", lambda e, fc=fc, w=wos_sb: e.transpose(
                        PT[:, fc * 128:(fc + 1) * 128], w[:, fc * 128:(fc + 1) * 128],
                        ident[:]), reads=[b_wos[0], bident], writes=[ptbuf])
                ROP("vector", lambda e, i=i: e.tensor_copy(
                    woT_sb[:, :, i * 128:(i + 1) * 128],
                    PT[:].rearrange("p (cc u) -> p cc u", u=128)),
                    reads=[ptbuf], writes=[b_woT[0]])
            for tt in range(QN):
                tg = QN * qr + tt
                qs_ = tt % 2
                for fc in range(NF):
                    ROP("tensor", lambda e, fc=fc, tt=tt: e.transpose(
                        PT[:, fc * 128:(fc + 1) * 128],
                        qtm_sb[:, tt, fc * 128:(fc + 1) * 128], ident[:]),
                        reads=[b_qtm[tt], bident], writes=[ptbuf])
                ROP("vector", lambda e, qs_=qs_: e.tensor_copy(qf_sb[:, qs_, :], PT[:]),
                    reads=[ptbuf], writes=[b_qf[qs_]])
                for fc in range(NF):
                    ROP("tensor", lambda e, fc=fc, tt=tt: e.transpose(
                        PT[:, fc * 128:(fc + 1) * 128],
                        ktm_sb[:, tt, fc * 128:(fc + 1) * 128], ident[:]),
                        reads=[b_ktm[tt], bident], writes=[ptbuf])
                ROP("vector", lambda e, qs_=qs_: e.tensor_copy(kf_sb[:, qs_, :], PT[:]),
                    reads=[ptbuf], writes=[b_kf[qs_]])
                atrot = [(PSs["at0"][:, 0:128], PBs["at0"]),
                         (PSs["at1"][:, 0:128], PBs["at1"]),
                         (PSs["at2"][:, 0:128], PBs["at2"])]
                avrot = [(PSs["avA"][:, 0:65], PBs["avA"]),
                         (PSs["avB"][:, 0:65], PBs["avB"])]
                Sold = (tg + 1) % 2
                for h in range(LH):
                    hp = 64 * (h % 2)
                    hc = h // 2
                    areg, abuf_ = atrot[h % 3]
                    ROP("tensor", lambda e, areg=areg, hp=hp, hc=hc, qs_=qs_: e.matmul(
                        areg, kf_sb[hp:hp + 64, qs_, hc * 128:(hc + 1) * 128],
                        qf_sb[hp:hp + 64, qs_, hc * 128:(hc + 1) * 128],
                        start=True, stop=True),
                        reads=[b_kf[qs_], b_qf[qs_]], writes=[abuf_])
                    ats = h % 3
                    ROP("vector", lambda e, areg=areg, ats=ats: e.tensor_tensor(
                        at_sb[:, ats, :], areg, maskl[:], ALU.mult),
                        reads=[abuf_, bmaskl], writes=[b_at[ats]])
                    vreg, vbuf_ = avrot[h % 2]
                    ROP("tensor", lambda e, vreg=vreg, ats=ats, tt=tt, h=h: e.matmul(
                        vreg, at_sb[:, ats, :], v1_sb[:, tt, h * 65:(h + 1) * 65],
                        start=True, stop=False),
                        reads=[b_at[ats], b_v1[tt]], writes=[vbuf_])
                    ROP("tensor", lambda e, vreg=vreg, hp=hp, hc=hc, qs_=qs_, So=Sold: e.matmul(
                        vreg, qf_sb[hp:hp + 64, qs_, hc * 128:(hc + 1) * 128],
                        Sb_sb[hp:hp + 64, So, hc * 65:(hc + 1) * 65],
                        start=False, stop=True),
                        reads=[b_qf[qs_], b_Sb[Sold]], writes=[vbuf_])
                    nq = h % 2
                    ROP("vector", lambda e, vreg=vreg, nq=nq: e.tensor_scalar(
                        nrc_sb[:, nq * 2:nq * 2 + 1], vreg[:, 64:65], 1e-6, None, ALU.add),
                        reads=[vbuf_], writes=[b_nrc[nq]])
                    ROP("vector", lambda e, nq=nq: e.reciprocal(
                        nrc_sb[:, nq * 2 + 1:nq * 2 + 2], nrc_sb[:, nq * 2:nq * 2 + 1]),
                        reads=[b_nrc[nq]], writes=[b_nrc[nq]])
                    ROP("scalar", lambda e, vreg=vreg, h=h, nq=nq: e.activation(
                        otm_sb[:, 0, h * 64:(h + 1) * 64], vreg[:, 0:64], AF.Copy,
                        scale=nrc_sb[:, nq * 2 + 1:nq * 2 + 2]),
                        reads=[vbuf_, b_nrc[nq]], writes=[b_otm[0]])
                    dreg_b = PBs["dsA"] if hc < 4 else PBs["dsB"]
                    dap = (PSs["dsA"][hp:hp + 64, 65 * hc:65 * hc + 65] if hc < 4 else
                           PSs["dsB"][hp:hp + 64, 65 * (hc - 4):65 * (hc - 4) + 65])
                    ROP("tensor", lambda e, dap=dap, tt=tt, h=h: e.matmul(
                        dap, ktm_sb[:, tt, h * 64:(h + 1) * 64],
                        v1_sb[:, tt, h * 65:(h + 1) * 65],
                        start=True, stop=True),
                        reads=[b_ktm[tt], b_v1[tt]], writes=[dreg_b])
                Snew = tg % 2
                ROP("vector", lambda e, Snew=Snew, So=Sold, dsA=PSs["dsA"]: e.tensor_tensor(
                    S_sb[:, Snew, 0:260], dsA[:, 0:260], S_sb[:, So, 0:260], ALU.add),
                    reads=[PBs["dsA"], b_S[Sold]], writes=[b_S[Snew]])
                ROP("vector", lambda e, Snew=Snew, So=Sold, dsB=PSs["dsB"]: e.tensor_tensor(
                    S_sb[:, Snew, 260:520], dsB[:, 0:260], S_sb[:, So, 260:520], ALU.add),
                    reads=[PBs["dsB"], b_S[Sold]], writes=[b_S[Snew]])
                ROP("vector", lambda e, Snew=Snew: e.tensor_copy(
                    Sb_sb[:, Snew, 0:260], S_sb[:, Snew, 0:260]),
                    reads=[b_S[Snew]], writes=[b_Sb[Snew]])
                ROP("vector", lambda e, Snew=Snew: e.tensor_copy(
                    Sb_sb[:, Snew, 260:520], S_sb[:, Snew, 260:520]),
                    reads=[b_S[Snew]], writes=[b_Sb[Snew]])
                for fc in range(NF):
                    ROP("tensor", lambda e, fc=fc: e.transpose(
                        PT[:, fc * 128:(fc + 1) * 128],
                        otm_sb[:, 0, fc * 128:(fc + 1) * 128], ident[:]),
                        reads=[b_otm[0], bident], writes=[ptbuf])
                ROP("vector", lambda e: e.tensor_copy(ofm_sb[:, 0, :], PT[:]),
                    reads=[ptbuf], writes=[b_ofm[0]])
                for oh in range(2):
                    oreg, orb = ((PSs["dsA"][:, 0:512], PBs["dsA"]),
                                 (PSs["dsB"][:, 0:512], PBs["dsB"]))[oh]
                    for fc in range(NF):
                        ROP("tensor", lambda e, fc=fc, oh=oh, oreg=oreg: e.matmul(
                            oreg, ofm_sb[:, 0, fc * 128:(fc + 1) * 128],
                            woT_sb[:, fc, oh * 512:(oh + 1) * 512],
                            start=(fc == 0), stop=False),
                            reads=[b_ofm[0], b_woT[0]], writes=[orb])
                    ROP("tensor", lambda e, oh=oh, oreg=oreg: e.matmul(
                        oreg, onesb[:], aorow[0:1, oh * 512:(oh + 1) * 512],
                        start=False, stop=True),
                        reads=[bonesb, b_aorow[0]], writes=[orb])
                    ROP("vector", lambda e, oh=oh, tg=tg, oreg=oreg: e.tensor_tensor(
                        slotO[:, tg * D + oh * 512:tg * D + (oh + 1) * 512], oreg,
                        slotX[:, tg * D + oh * 512:tg * D + (oh + 1) * 512], ALU.add),
                        reads=[orb, bufX[tg]], writes=[bufO[tg]])
            if self.stage in ("lins_dbg", "lins_dbg2", "lins_dbg3"):
                self.done = True
                return
            ar.free(woT_sb, wos_sb, qf_sb, kf_sb, at_sb, otm_sb, ofm_sb)
            pess.close()
            self.psum_phase_end(PBs)

        for qr in range(4):
            _quarter(qr)
            if self.done:
                return

        ar.free(abrow, aorow, S_sb, Sb_sb, nrc_sb, qtm_sb, ktm_sb, v1_sb)


_CACHE = {}

def _get_mk():
    if STAGE not in _CACHE:
        _CACHE[STAGE] = MK(STAGE)
    return _CACHE[STAGE]

def kernel(L0_attn_x, L0_attn_w_qkv, L0_attn_b_qkv, L0_attn_w_out, L0_attn_b_out,
           L0_ffn_wqkv, L0_ffn_q_scale, L0_ffn_k_scale, L0_ffn_wo, L0_ffn_ln_g,
           L0_ffn_ln_b, L0_ffn_w1, L0_ffn_b1, L0_ffn_w2, L0_ffn_b2):
    mk = _get_mk()
    csq, sxq, csk, sxk = rot_tables()
    consts = {
        "c_ident": np.eye(128, dtype=np.float32).astype(BF),
        "c_identf": np.eye(128, dtype=np.float32),
        "c_maskf": mask_t_ffn(),
        "c_maskl": mask_t_lin(),
        "c_csq": csq, "c_sxq": sxq, "c_csk": csk, "c_sxk": sxk,
    }
    shared = {
        "aw_qkv": np.ascontiguousarray(L0_attn_w_qkv, np.float32),
        "ab_qkv": np.ascontiguousarray(L0_attn_b_qkv, np.float32),
        "aw_out": np.ascontiguousarray(L0_attn_w_out, np.float32),
        "ab_out": np.ascontiguousarray(L0_attn_b_out, np.float32),
        "fw_qkv": np.ascontiguousarray(L0_ffn_wqkv, np.float32),
        "fqs": np.ascontiguousarray(L0_ffn_q_scale, np.float32),
        "fks": np.ascontiguousarray(L0_ffn_k_scale, np.float32),
        "fwo": np.ascontiguousarray(L0_ffn_wo, np.float32),
        "flng": np.ascontiguousarray(L0_ffn_ln_g, np.float32),
        "flnb": np.ascontiguousarray(L0_ffn_ln_b, np.float32),
        "fw1": np.ascontiguousarray(L0_ffn_w1, np.float32),
        "fb1": np.ascontiguousarray(L0_ffn_b1, np.float32),
        "fw2": np.ascontiguousarray(L0_ffn_w2, np.float32),
        "fb2": np.ascontiguousarray(L0_ffn_b2, np.float32),
    }
    shared.update(consts)
    x = np.ascontiguousarray(L0_attn_x, np.float32)
    in_maps = []
    for c in range(8):
        m = dict(shared)
        m["x"] = np.ascontiguousarray(x[c])
        in_maps.append(m)
    res = run_bass_kernel_spmd(mk.nc, in_maps, core_ids=list(range(8)))
    if STAGE == "ffn_dbg":
        return res.results[0]
    return np.stack([res.results[c]["y"] for c in range(8)], axis=0)


# version 08
# speedup vs baseline: 1.0257x; 1.0172x over previous
import numpy as np
import ml_dtypes
from contextlib import ExitStack
import concourse.bass as bass
import concourse.mybir as mybir
from concourse.bass_utils import run_bass_kernel_spmd

F32 = mybir.dt.float32
BF16 = mybir.dt.bfloat16
AF = mybir.ActivationFunctionType
ALU = mybir.AluOpType
AX = mybir.AxisListType
BF = ml_dtypes.bfloat16

T = 2048
D = 1024
NT = 16
NF = 8
FH = 8
FDH = 64
WIN = 512
HID = 2730
NH = 22
QK_SCALE = 8.0
LH = 16

STAGE = "full"


def rot_tables():
    half = np.arange(0, FDH, 2, dtype=np.float64)
    inv_freq = 1.0 / 10000.0 ** (half / FDH)
    t = np.arange(T, dtype=np.float64)
    freqs = t[:, None] * inv_freq[None, :]
    freqs64 = np.concatenate([freqs, freqs], axis=-1)
    base = (half + 0.4 * FDH) / (1.4 * FDH)
    power = (t - T // 2) / float(WIN // 2)
    scale = base[None, :] ** power[:, None]
    scale64 = np.concatenate([scale, scale], axis=-1)
    cos, sin = np.cos(freqs64), np.sin(freqs64)
    sign = np.where(np.arange(FDH) < FDH // 2, -1.0, 1.0)[None, :]
    tabs = [QK_SCALE * cos * scale64, QK_SCALE * sign * sin * scale64,
            cos / scale64, sign * sin / scale64]
    out = []
    for a in tabs:
        out.append(np.ascontiguousarray(
            a.reshape(NT, 128, FDH).transpose(1, 0, 2).reshape(128, NT * FDH)
        ).astype(BF))
    return out

def mask_t_ffn():
    out = np.zeros((128, 5 * 128), np.float32)
    u = np.arange(128)[:, None]
    rr = np.arange(128)[None, :]
    for c in range(5):
        qk = 128 * (4 - c) + rr - u
        out[:, c * 128:(c + 1) * 128] = ((qk >= 0) & (qk <= WIN)).astype(np.float32)
    return out.astype(BF)

def mask_t_lin():
    k = np.arange(128)[:, None]
    q = np.arange(128)[None, :]
    return (k <= q).astype(np.float32).astype(BF)


class Buf:
    __slots__ = ("name", "w", "r")
    def __init__(self, name, parents=()):
        self.name = name
        self.w = None
        self.r = []
        for p in parents:
            if p.w is not None:
                self.r.append(p.w)
            self.r.extend(p.r)

ENGS = ("sync", "tensor", "vector", "scalar", "gpsimd")
QUEUES = {"qs": "sync", "qp": "gpsimd", "qa": "scalar"}

class Rec:
    def __init__(self, nc, es):
        self.nc = nc
        self.ops = {e: [] for e in ENGS}
        self.need_inc = {e: set() for e in ENGS}
        self.dmacnt = {q: 0 for q in QUEUES}
        self.emitted = {}
        self.sems = {}
        for e in ENGS:
            self.sems[e] = es.enter_context(nc.semaphore("sem_" + e))
        for q in QUEUES:
            self.sems[q] = es.enter_context(nc.semaphore("sem_" + q))

    def _record(self, eng, fn, reads, writes, queue):
        import sys
        fr = sys._getframe(2)
        label = f"{fr.f_code.co_name}:{fr.f_lineno}"
        deps = []
        for b in reads:
            if b.w is not None:
                deps.append(b.w)
        for b in writes:
            if b.w is not None:
                deps.append(b.w)
            deps.extend(b.r)
        fdeps = {}
        for (d0, d1) in deps:
            if d0 == eng and eng == "tensor":
                continue
            fdeps[d0] = max(fdeps.get(d0, 0), d1)
        fdeps = sorted(fdeps.items())
        idx = len(self.ops[eng])
        self.ops[eng].append((fn, fdeps, queue, label))
        if queue is None:
            tok = (eng, idx)
        else:
            self.dmacnt[queue] += 16
            tok = (queue, self.dmacnt[queue])
        for d0, d1 in fdeps:
            if d0 not in QUEUES:
                self.need_inc[d0].add(d1)
        for b in reads:
            b.r.append(tok)
        for b in writes:
            b.w = tok
            b.r = []
        return tok

    def op(self, eng, fn, reads=(), writes=()):
        return self._record(eng, fn, list(reads), list(writes), None)

    def dma(self, queue, fn, reads=(), writes=()):
        return self._record(QUEUES[queue], fn, list(reads), list(writes), queue)

    def finalize_and_emit(self):
        mil = {}
        for e in ENGS:
            mil[e] = {idx: k + 1 for k, idx in enumerate(sorted(self.need_inc[e]))}
        rec = self

        def mkreplay(eng):
            def replay(e):
                hw = {}
                qcnt = {q: 0 for q in QUEUES}
                for idx, (fn, fdeps, queue, label) in enumerate(rec.ops[eng]):
                    if queue is not None:
                        prev = qcnt[queue] * 16
                        qcnt[queue] += 1
                        if prev > 0 and hw.get(queue, 0) < prev:
                            hw[queue] = prev
                            e.wait_ge(rec.sems[queue], prev)
                    for d0, d1 in fdeps:
                        val = d1 if d0 in QUEUES else mil[d0][d1]
                        if hw.get(d0, 0) >= val:
                            continue
                        hw[d0] = val
                        e.wait_ge(rec.sems[d0], val)
                    ins = fn(e)
                    rec.emitted.setdefault(eng, []).append((ins.ins.name, idx))
                    if queue is not None:
                        ins.then_inc(rec.sems[queue], 16)
                    elif idx in mil[eng]:
                        ins.then_inc(rec.sems[eng], 1)
                if eng == "sync":
                    for q in QUEUES:
                        if rec.dmacnt[q] > 0 and hw.get(q, 0) < rec.dmacnt[q]:
                            e.wait_ge(rec.sems[q], rec.dmacnt[q])
            return replay

        with self.nc.Block() as blk:
            blk.sync(mkreplay("sync"))
            blk.tensor(mkreplay("tensor"))
            blk.vector(mkreplay("vector"))
            blk.scalar(mkreplay("scalar"))
            blk.gpsimd(mkreplay("gpsimd"))

    def nops(self):
        return {e: len(self.ops[e]) for e in ENGS}

class Arena:
    def __init__(self, nc, base, top):
        self.nc = nc
        self.base = base
        self.top = top
        self.free_iv = [(base, top)]
        self.live = {}
        self.hist = []
        self.count = 0

    def alloc(self, name, shape, dtype, nbufs=1):
        itemsize = np.dtype(mybir.dt.np(dtype)).itemsize
        size = int(np.prod(shape[1:])) * itemsize
        size = (size + 31) & ~31
        off = None
        for i, (s, e) in enumerate(self.free_iv):
            if e - s >= size:
                off = s
                if s + size < e:
                    self.free_iv[i] = (s + size, e)
                else:
                    del self.free_iv[i]
                break
        if off is None:
            raise MemoryError(f"arena overflow: {name} needs {size}; free={self.free_iv}")
        parents = []
        for (fo, fs, fb) in self.hist:
            if fo < off + size and off < fo + fs:
                parents.extend(fb)
        self.count += 1
        t = self.nc.alloc_sbuf_tensor_at(f"ar{self.count}_{name}", list(shape), dtype, offset=off)
        bufs = [Buf(f"{name}[{i}]", parents) for i in range(nbufs)]
        self.live[t.name] = (off, size, bufs)
        self.log = getattr(self, "log", [])
        self.log.append(("A", t.name, off, size))
        return t, bufs

    def free(self, *tensors):
        for t in tensors:
            off, size, bufs = self.live.pop(t.name)
            self.log = getattr(self, "log", [])
            self.log.append(("F", t.name, off, size))
            self.hist.append((off, size, bufs))
            self.free_iv.append((off, off + size))
        self._merge()

    def _merge(self):
        self.free_iv.sort()
        merged = []
        for s, e in self.free_iv:
            if merged and s <= merged[-1][1]:
                merged[-1] = (merged[-1][0], max(merged[-1][1], e))
            else:
                merged.append((s, e))
        self.free_iv = merged

    def lend(self, off, size, seed_bufs):
        self.hist.append((off, size, list(seed_bufs)))
        self.free_iv.append((off, off + size))
        self._merge()

    def reclaim(self, off, size):
        out = []
        niv = []
        for s, e in self.free_iv:
            if s < off + size and off < e:
                assert s <= off and e >= off + size, \
                    f"reclaim range busy: {(off, size)} vs free {(s, e)}"
                if s < off:
                    niv.append((s, off))
                if e > off + size:
                    niv.append((off + size, e))
            else:
                niv.append((s, e))
        self.free_iv = niv
        keep = []
        for (fo, fs, fb) in self.hist:
            if fo < off + size and off < fo + fs:
                out.extend(fb)
            else:
                keep.append((fo, fs, fb))
        self.hist = keep
        return out


class MK:
    def __init__(self, stage=STAGE):
        self.stage = stage
        self.es = ExitStack()
        self.nc = bass.Bass("TRN2", target_bir_lowering=False)
        self.input_names = []
        self.build()

    def psum_phase(self, es2, specs):
        tensors, bufs = {}, {}
        for name, shape in specs.items():
            self._psum_uid += 1
            t = es2.enter_context(self.nc.psum_tensor(f"pp{self._psum_uid}_{name}", list(shape), F32))
            tensors[name] = t
            bufs[name] = Buf(name + str(self._psum_uid), self.psum_prev)
        return tensors, bufs

    def psum_phase_end(self, bufs):
        self.psum_prev = list(bufs.values())

    def din(self, name, shape, dtype):
        self.input_names.append(name)
        return self.nc.dram_tensor(name, list(shape), dtype, kind="ExternalInput").ap()

    def sb(self, name, shape, dtype, nbufs=1):
        t = self.nc.alloc_sbuf_tensor(name, list(shape), dtype)
        return t, [Buf(f"{name}[{i}]") for i in range(nbufs)]

    def build(self):
        nc, es = self.nc, self.es
        r = self.rec = Rec(nc, es)
        st = self.stage

        x_in = self.din("x", [T, D], F32)
        aw_qkv = self.din("aw_qkv", [3072, 1024], F32)
        ab_qkv = self.din("ab_qkv", [3072], F32)
        aw_out = self.din("aw_out", [1024, 1024], F32)
        ab_out = self.din("ab_out", [1024], F32)
        fw_qkv = self.din("fw_qkv", [2, 1024, 1536], F32)
        fqs = self.din("fqs", [2, 64], F32)
        fks = self.din("fks", [2, 64], F32)
        fwo = self.din("fwo", [2, 512, 1024], F32)
        flng = self.din("flng", [2, 1024], F32)
        flnb = self.din("flnb", [2, 1024], F32)
        fw1 = self.din("fw1", [2, 1024, 5460], F32)
        fb1 = self.din("fb1", [2, 5460], F32)
        fw2 = self.din("fw2", [2, 2730, 1024], F32)
        fb2 = self.din("fb2", [2, 1024], F32)
        c_ident = self.din("c_ident", [128, 128], BF16)
        c_identf = self.din("c_identf", [128, 128], F32)
        c_maskf = self.din("c_maskf", [128, 640], BF16)
        c_maskl = self.din("c_maskl", [128, 128], BF16)
        c_csq = self.din("c_csq", [128, NT * 64], BF16)
        c_sxq = self.din("c_sxq", [128, NT * 64], BF16)
        c_csk = self.din("c_csk", [128, NT * 64], BF16)
        c_sxk = self.din("c_sxk", [128, NT * 64], BF16)
        y_out = nc.dram_tensor("y", [T, D], F32, kind="ExternalOutput").ap()
        self._y = y_out
        x2_spill = nc.dram_tensor("x2spill", [T, D], F32, kind="Internal").ap()

        slotA, bufA = self.sb("slotA", [128, NT * D], F32, NT)
        slotB, bufB = self.sb("slotB", [128, NT * D], F32, NT)
        ident, bident = self.sb("ident", [128, 128], BF16)
        identf, bidentf = self.sb("identf", [128, 128], F32)
        maskf, bmaskf = self.sb("maskf", [128, 640], BF16)
        maskl, bmaskl = self.sb("maskl", [128, 128], BF16)
        onesb, bonesb = self.sb("onesb", [1, 128], BF16)
        abqkv_pp, b_abqkv = self.sb("abqkv_pp", [128, 24], F32)
        lg_pp, b_lg = self.sb("lg_pp", [128, 16], F32)
        lb_pp, b_lb = self.sb("lb_pp", [128, 16], F32)
        b1a_pp, b_b1a = self.sb("b1a_pp", [128, 2 * NH], F32)
        b1g_pp, b_b1g = self.sb("b1g_pp", [128, 2 * NH], F32)
        b2_pp, b_b2 = self.sb("b2_pp", [128, 16], F32)
        qsrow, b_qsrow = self.sb("qsrow", [1, 128], F32)
        ksrow, b_ksrow = self.sb("ksrow", [1, 128], F32)
        stat, b_stat = self.sb("stat", [128, 128], F32, 48)
        scr, b_scr = self.sb("scr", [128, 1024], F32)

        PT = es.enter_context(nc.psum_tensor("PsT", [128, 1024], BF16))
        pb = {"PT": Buf("PT")}
        self.pb = pb
        self.P = dict(PT=PT)
        self.psum_prev = []
        self._psum_uid = 0
        self.done = False

        ccol_t, b_ccol = self.sb("ccol", [128, 8], F32)
        self._ccols = {}
        def constcol(val):
            if val not in self._ccols:
                i = len(self._ccols)
                assert i < 8
                ap = ccol_t[:, i:i + 1]
                r.op("vector", lambda e, a=ap, v=float(val): e.memset(a, v),
                     writes=[b_ccol[0]])
                self._ccols[val] = ap
            return self._ccols[val]
        self.constcol = constcol
        self.b_ccol = b_ccol[0]

        def d(q, out_ap, in_ap, wbufs, rbufs=(), slow=False):
            def fn(e, o=out_ap, i=in_ap, s=slow):
                if s:
                    with nc.allow_non_contiguous_dma("small"):
                        return e.dma_start(out=o, in_=i, allow_slow_non_contiguous=True)
                return e.dma_start(out=o, in_=i)
            return r.dma(q, fn, reads=list(rbufs), writes=list(wbufs))

        ROP = r.op

        d("qs", ident[:], c_ident, [bident[0]])
        d("qs", identf[:], c_identf, [bidentf[0]])
        d("qs", maskf[:], c_maskf, [bmaskf[0]])
        d("qs", maskl[:], c_maskl, [bmaskl[0]])
        ROP("vector", lambda e: e.memset(onesb[:], 1.0), writes=[bonesb[0]])
        ROP("vector", lambda e: e.memset(b1a_pp[:], 0.0), writes=[b_b1a[0]])
        ROP("vector", lambda e: e.memset(b1g_pp[:], 0.0), writes=[b_b1g[0]])
        d("qp", abqkv_pp[:], ab_qkv.rearrange("(c p) -> p c", p=128), [b_abqkv[0]], slow=True)
        for l in range(2):
            d("qp", lg_pp[:, l * 8:(l + 1) * 8], flng[l].rearrange("(c p) -> p c", p=128), [b_lg[0]], slow=True)
            d("qp", lb_pp[:, l * 8:(l + 1) * 8], flnb[l].rearrange("(c p) -> p c", p=128), [b_lb[0]], slow=True)
            d("qp", b2_pp[:, l * 8:(l + 1) * 8], fb2[l].rearrange("(c p) -> p c", p=128), [b_b2[0]], slow=True)
            d("qp", b1a_pp[:, l * NH:l * NH + 21], fb1[l, 0:2688].rearrange("(c p) -> p c", p=128), [b_b1a[0]], slow=True)
            d("qp", b1a_pp[0:42, l * NH + 21:l * NH + 22], fb1[l, 2688:2730].unsqueeze(1), [b_b1a[0]], slow=True)
            d("qp", b1g_pp[:, l * NH:l * NH + 21], fb1[l, 2730:5418].rearrange("(c p) -> p c", p=128), [b_b1g[0]], slow=True)
            d("qp", b1g_pp[0:42, l * NH + 21:l * NH + 22], fb1[l, 5418:5460].unsqueeze(1), [b_b1g[0]], slow=True)
            d("qs", qsrow[:, l * 64:(l + 1) * 64], fqs[l].unsqueeze(0), [b_qsrow[0]])
            d("qs", ksrow[:, l * 64:(l + 1) * 64], fks[l].unsqueeze(0), [b_ksrow[0]])

        def ln_full_tile(src_ap, src_bufs, dst_ap, dst_bufs, tslot, eps=1e-5):
            sc = stat[:, tslot * 4 + 0:tslot * 4 + 1]
            qc = stat[:, tslot * 4 + 1:tslot * 4 + 2]
            nm = stat[:, tslot * 4 + 2:tslot * 4 + 3]
            rs = stat[:, tslot * 4 + 3:tslot * 4 + 4]
            sb_ = b_stat[tslot]
            ROP("scalar", lambda e: e.activation(scr[:], src_ap, AF.Square),
                reads=src_bufs, writes=[b_scr[0]])
            ROP("vector", lambda e: e.reduce_sum(qc, scr[:], axis=AX.X),
                reads=[b_scr[0]], writes=[sb_])
            ROP("vector", lambda e: e.reduce_sum(sc, src_ap, axis=AX.X),
                reads=src_bufs, writes=[sb_])
            ROP("vector", lambda e: e.tensor_scalar(nm, sc, -1.0 / D, None, ALU.mult),
                reads=[sb_], writes=[sb_])
            ROP("vector", lambda e: e.tensor_mul(rs, nm, nm), reads=[sb_], writes=[sb_])
            ROP("vector", lambda e: e.scalar_tensor_tensor(qc, qc, 1.0 / D, rs, ALU.mult, ALU.subtract),
                reads=[sb_], writes=[sb_])
            epsc = self.constcol(eps)
            nhalf = self.constcol(-0.5)
            ROP("scalar", lambda e: e.activation(sc, qc, AF.Ln, bias=epsc),
                reads=[sb_, b_ccol[0]], writes=[sb_])
            ROP("scalar", lambda e: e.activation(rs, sc, AF.Exp, scale=nhalf),
                reads=[sb_, b_ccol[0]], writes=[sb_])
            ROP("vector", lambda e: e.tensor_scalar(dst_ap, src_ap, nm, rs, ALU.add, ALU.mult),
                reads=list(src_bufs) + [sb_], writes=dst_bufs)

        self.arena = Arena(nc, nc.sbuf_base, nc.sbuf_top)
        xio, b_xio = self.arena.alloc("xio0", [128, 3, D], F32, 3)
        for t in range(NT):
            xb = b_xio[t % 3]
            xap = xio[:, t % 3, :]
            d("qs", xap, x_in[t * 128:(t + 1) * 128, :], [xb])
            ln_full_tile(xap, [xb], slotA[:, t * D:(t + 1) * D], [bufA[t]], t % 16)
        self.arena.free(xio)

        self.ctx = dict(locals())

        if st == "ln1":
            self.emit_out(slotA, bufA)
            r.finalize_and_emit()
            return

        if st in ("lin_dbg", "linq_dbg", "lins_dbg", "lins_dbg2", "lins_dbg3"):
            self.linattn(slotA, bufA, slotB, bufB)
            if st in ("linq_dbg", "lins_dbg", "lins_dbg2", "lins_dbg3"):
                self.emit_out(slotA, bufA)
            else:
                self.emit_out(slotB, bufB)
            r.finalize_and_emit()
            return

        self.ffn_call(slotA, bufA, call=0)
        if self.done:
            r.finalize_and_emit()
            return
        if st == "ffn1":
            self.emit_out(slotA, bufA)
            r.finalize_and_emit()
            return

        xio, b_xio = self.arena.alloc("xio1", [128, 3, D], F32, 3)
        for t in range(NT):
            xb = b_xio[t % 3]
            xap = xio[:, t % 3, :]
            d("qs", xap, x_in[t * 128:(t + 1) * 128, :], [xb])
            ROP("vector", lambda e, t=t, xap=xap: e.scalar_tensor_tensor(
                slotB[:, t * D:(t + 1) * D], slotA[:, t * D:(t + 1) * D], 0.5, xap,
                ALU.mult, ALU.add),
                reads=[bufA[t], xb], writes=[bufB[t]])
        self.arena.free(xio)
        if st == "res1":
            self.emit_out(slotB, bufB)
            r.finalize_and_emit()
            return

        self.linattn(slotB, bufB, slotA, bufA)
        if st == "lin":
            self.emit_out(slotA, bufA)
            r.finalize_and_emit()
            return

        for t in range(NT):
            d("qs", x2_spill[t * 128:(t + 1) * 128, :], slotA[:, t * D:(t + 1) * D],
              [Buf(f"x2sp{t}")], rbufs=[bufA[t]])
            ln_full_tile(slotA[:, t * D:(t + 1) * D], [bufA[t]],
                         slotB[:, t * D:(t + 1) * D], [bufB[t]], t % 16)

        self.ffn_call(slotB, bufB, call=1)

        xio, b_xio = self.arena.alloc("xio2", [128, 3, D], F32, 3)
        for t in range(NT):
            xb = b_xio[t % 3]
            xap = xio[:, t % 3, :]
            d("qs", xap, x2_spill[t * 128:(t + 1) * 128, :], [xb])
            ROP("vector", lambda e, t=t, xap=xap: e.scalar_tensor_tensor(
                slotA[:, t * D:(t + 1) * D], slotB[:, t * D:(t + 1) * D], 0.5, xap,
                ALU.mult, ALU.add),
                reads=[bufB[t], xb], writes=[bufA[t]])
        self.emit_out(slotA, bufA)
        r.finalize_and_emit()

    def emit_out(self, slot, bufs):
        r = self.rec
        y = self._y
        for t in range(NT):
            def fn(e, t=t):
                return e.dma_start(out=y[t * 128:(t + 1) * 128, :], in_=slot[:, t * D:(t + 1) * D])
            r.dma("qs", fn, reads=[bufs[t]], writes=[Buf(f"yo{t}")])

    def ffn_call(self, slot, bufs, call):
        free_slot = self.ctx["slotB"] if call == 0 else self.ctx["slotA"]
        free_bufs = self.ctx["bufB"] if call == 0 else self.ctx["bufA"]
        off = self.nc.lookup_mloc(free_slot).addr
        size = NT * D * 4
        self.arena.lend(off, size, free_bufs)
        for l in range(2):
            self.ffn_layer(slot, bufs, l, call)
            if self.done:
                return
        parents = self.arena.reclaim(off, size)
        for t in range(NT):
            free_bufs[t] = Buf(f"reslot{call}_{t}", parents)

    def ffn_layer(self, slot, bufs, l, call):
        c = self.ctx
        nc, r, ar = self.nc, self.rec, self.arena
        ROP = r.op
        d = c["d"]
        PT = self.P["PT"]
        pb = self.pb
        ident = c["ident"]; bident = c["bident"][0]
        maskf = c["maskf"]; bmaskf = c["bmaskf"][0]
        onesb = c["onesb"]; bonesb = c["bonesb"][0]
        fw_qkv, fwo, fw1, fb1, fw2 = c["fw_qkv"], c["fwo"], c["fw1"], c["fb1"], c["fw2"]
        fqs, fks = c["fqs"], c["fks"]
        ccsq, csxq, ccsk, csxk = c["c_csq"], c["c_sxq"], c["c_csk"], c["c_sxk"]
        lg_pp, lb_pp = c["lg_pp"], c["lb_pp"]
        b_lg, b_lb = c["b_lg"][0], c["b_lb"][0]
        b1a_pp, b1g_pp, b2_pp = c["b1a_pp"], c["b1g_pp"], c["b2_pp"]
        b_b1a, b_b1g, b_b2 = c["b_b1a"][0], c["b_b1g"][0], c["b_b2"][0]
        stat = c["stat"]; b_stat = c["b_stat"]
        constcol = self.constcol
        bcc = self.b_ccol

        wqkv_sb, b_wqkv = ar.alloc("wqkv", [128, NF, 1536], BF16)
        wo_sb, b_wo = ar.alloc("wo", [128, 4, 1024], BF16)
        cs_sb, b_cs = ar.alloc("cstag", [128, NT * 64], BF16)
        cq_sb, b_cq = ar.alloc("cqtab", [128, 4, NT * 64], BF16)
        qsb_sb, b_qsb = ar.alloc("qsb", [128, 4, 64], BF16)
        qsrow_bf, b_qsrowbf = ar.alloc("qsrowbf", [1, 128], BF16)
        xs_sb, b_xs = ar.alloc("xs", [128, 2, NF, 129], BF16, 2)
        cast_sb, b_cast = ar.alloc("cast", [128, 2, D], BF16, 2)
        qf_sb, b_qf = ar.alloc("qf", [128, 2, 512], BF16, 2)
        kf_sb, b_kf = ar.alloc("kf", [128, 6, 512], BF16, 6)
        v1_sb, b_v1 = ar.alloc("v1", [128, 6, FH * 65], BF16, 6)
        exp_sb, b_exp = ar.alloc("exps", [128, 2, 640], BF16, 2)
        at_sb, b_at = ar.alloc("atT", [128, 2, 640], BF16, 2)
        for _i in range(2):
            ROP("vector", lambda e, _i=_i: e.memset(exp_sb[:, _i, :], 0.0), writes=[b_exp[_i]])
            ROP("vector", lambda e, _i=_i: e.memset(at_sb[:, _i, :], 0.0), writes=[b_at[_i]])
        otm_sb, b_otm = ar.alloc("otm", [128, 2, 512], BF16, 2)
        ofm_sb, b_ofm = ar.alloc("ofm", [128, 2, 512], BF16, 2)
        qn_sb, b_qn = ar.alloc("qn", [128, 2, 512], F32, 2)
        c1_sb, b_c1 = ar.alloc("c1", [128, 2, 512], F32, 2)
        c2_sb, b_c2 = ar.alloc("c2", [128, 2, 512], F32, 2)
        sqs_sb, b_sqs = c["scr"], c["b_scr"]
        rn_sb, b_rn = ar.alloc("rn", [128, 4, 32], F32, 4)
        for _i in range(4):
            ROP("vector", lambda e, _i=_i: e.memset(rn_sb[:, _i, :], 0.0), writes=[b_rn[_i]])
        qpb_sb, b_qpb = ar.alloc("qpb", [128, 512], BF16)
        kpb_sb, b_kpb = ar.alloc("kpb", [128, 512], BF16)
        st2_sb, b_st2g = ar.alloc("st2", [128, 64], F32)
        self._st2, self._b_st2 = st2_sb, b_st2g

        for f in range(NF):
            d("qp", wqkv_sb[:, f, :], fw_qkv[l, f * 128:(f + 1) * 128, :], [b_wqkv[0]])
        for ci in range(4):
            d("qp", wo_sb[:, ci, :], fwo[l, ci * 128:(ci + 1) * 128, :], [b_wo[0]])
        d("qp", qsrow_bf[:, 0:64], fqs[l].unsqueeze(0), [b_qsrowbf[0]])
        d("qp", qsrow_bf[:, 64:128], fks[l].unsqueeze(0), [b_qsrowbf[0]])
        pes = ExitStack()
        PS, PB = self.psum_phase(pes, {"sA": [128, 768], "sB": [128, 768],
                                       "pq": [128, 512], "pk": [128, 512],
                                       "pv": [128, 512]})
        ROP("tensor", lambda e: e.matmul(PS["pq"][:, 0:64], onesb[:], qsrow_bf[:, 0:64], start=True, stop=True),
            reads=[bonesb, b_qsrowbf[0]], writes=[PB["pq"]])
        ROP("tensor", lambda e: e.matmul(PS["pq"][:, 64:128], onesb[:], qsrow_bf[:, 64:128], start=True, stop=True),
            reads=[bonesb, b_qsrowbf[0]], writes=[PB["pq"]])
        ROP("vector", lambda e: e.tensor_copy(qsb_sb[:, 0, :], PS["pq"][:, 0:64]),
            reads=[PB["pq"]], writes=[b_qsb[0]])
        ROP("vector", lambda e: e.tensor_copy(qsb_sb[:, 2, :], PS["pq"][:, 64:128]),
            reads=[PB["pq"]], writes=[b_qsb[0]])
        ROP("vector", lambda e: e.tensor_copy(qsb_sb[:, 1, 0:32], qsb_sb[:, 0, 32:64]),
            reads=[b_qsb[0]], writes=[b_qsb[0]])
        ROP("vector", lambda e: e.tensor_copy(qsb_sb[:, 1, 32:64], qsb_sb[:, 0, 0:32]),
            reads=[b_qsb[0]], writes=[b_qsb[0]])
        ROP("vector", lambda e: e.tensor_copy(qsb_sb[:, 3, 0:32], qsb_sb[:, 2, 32:64]),
            reads=[b_qsb[0]], writes=[b_qsb[0]])
        ROP("vector", lambda e: e.tensor_copy(qsb_sb[:, 3, 32:64], qsb_sb[:, 2, 0:32]),
            reads=[b_qsb[0]], writes=[b_qsb[0]])
        for i, cc in enumerate((ccsq, csxq, ccsk, csxk)):
            d("qs", cs_sb[:], cc, [b_cs[0]])
            def mkt(e, i=i):
                tv = cs_sb[:].rearrange("p (t dd) -> p t dd", dd=64)
                qv = qsb_sb[:, i, :].unsqueeze(1).broadcast_to([128, NT, 64])
                ov = cq_sb[:, i, :].rearrange("p (t dd) -> p t dd", dd=64)
                return e.tensor_tensor(ov, tv, qv, ALU.mult)
            ROP("vector", mkt, reads=[b_cs[0], b_qsb[0]], writes=[b_cq[0]])
        for s in range(6):
            ROP("vector", lambda e, s=s: e.memset(
                v1_sb[:, s, :].rearrange("p (h u) -> p h u", u=65)[:, :, 64:65], 1.0),
                writes=[b_v1[s]])
        ROP("vector", lambda e: e.memset(xs_sb[:, 0, :, 0:1], 0.0), writes=[b_xs[0]])

        sc_reg = [PS["sA"][:, 0:640], PS["sB"][:, 0:640]]
        sc_buf = [PB["sA"], PB["sB"]]
        qkq_reg, qkq_buf = PS["pq"][:, 0:512], PB["pq"]
        qkk_reg, qkk_buf = PS["pk"][:, 0:512], PB["pk"]
        qkv_reg, qkv_buf = PS["pv"][:, 0:512], PB["pv"]
        av_reg = [PS["sA"][:, 640:705], PS["sB"][:, 640:705]]
        av_buf = [PB["sA"], PB["sB"]]
        ptbuf = pb["PT"]

        ktiles = lambda t: list(range(max(0, t - 4), t + 1))

        ntiles = 2 if self.stage == "ffn_dbg" else NT
        def stage_load(t):
            ts = t % 2
            cast_ap = cast_sb[:, ts, :]
            ROP("scalar", lambda e, t=t, a=cast_ap: e.copy(a, slot[:, t * D:(t + 1) * D]),
                reads=[bufs[t]], writes=[b_cast[ts]])
            for fc in range(NF):
                ROP("tensor", lambda e, fc=fc, a=cast_ap: e.transpose(
                    PT[:, fc * 128:(fc + 1) * 128], a[:, fc * 128:(fc + 1) * 128], ident[:]),
                    reads=[b_cast[ts], bident], writes=[ptbuf])
            ROP("vector", lambda e, ts=ts: e.tensor_copy(
                xs_sb[:, ts, :, 1:129], PT[:].rearrange("p (c u) -> p c u", u=128)),
                reads=[ptbuf], writes=[b_xs[ts]])
            if t < NT - 1:
                ROP("vector", lambda e, ts=ts: e.tensor_copy(
                    xs_sb[:, 1 - ts, :, 0:1], xs_sb[:, ts, :, 128:129]),
                    reads=[b_xs[ts]], writes=[b_xs[1 - ts]])
            def xs_view(fc, ts=ts):
                off = 1 if fc < 4 else 0
                return xs_sb[:, ts, fc, off:off + 128]
            for gi, (reg, rbuf) in enumerate(((qkq_reg, qkq_buf), (qkk_reg, qkk_buf), (qkv_reg, qkv_buf))):
                for fc in range(NF):
                    ROP("tensor", lambda e, gi=gi, fc=fc, reg=reg, ts=ts, xv=xs_view: e.matmul(
                        reg, xv(fc, ts), wqkv_sb[:, fc, gi * 512:(gi + 1) * 512],
                        start=(fc == 0), stop=(fc == NF - 1)),
                        reads=[b_xs[ts], b_wqkv[0]], writes=[rbuf])
            vs = t % 6
            ROP("vector", lambda e, vs=vs, qr=qkv_reg: e.tensor_copy(
                v1_sb[:, vs, :].rearrange("p (h u) -> p h u", u=65)[:, :, 0:64],
                qr.rearrange("p (h u) -> p h u", u=64)),
                reads=[qkv_buf], writes=[b_v1[vs]])

        def stage_prep(t):
            ts = t % 2
            tcol = t % 4
            rnb = b_rn[tcol]
            rss = rn_sb[:, tcol, :]
            streams = [
                (qkq_reg, qkq_buf, 0, 1, qf_sb[:, ts, :], b_qf[ts]),
                (qkk_reg, qkk_buf, 2, 3, kf_sb[:, t % 6, :], b_kf[t % 6]),
            ]
            for qi, (reg, rbuf, tabc, tabs, dstf, dstfb) in enumerate(streams):
                ROP("scalar", lambda e, reg=reg, qi=qi: e.activation(
                    sqs_sb[:, qi * 512:qi * 512 + 512], reg, AF.Square),
                    reads=[rbuf], writes=[b_sqs[0]])
            for qi, (reg, rbuf, tabc, tabs, dstf, dstfb) in enumerate(streams):
                sqv = sqs_sb[:, qi * 512:qi * 512 + 512].rearrange("p (h u) -> p h u", u=64)
                ROP("vector", lambda e, qi=qi, sqv=sqv, rss=rss: e.reduce_sum(
                    rss[:, qi * 8:qi * 8 + 8].unsqueeze(2), sqv, axis=AX.X),
                    reads=[b_sqs[0]], writes=[rnb])
            tiny = constcol(1e-24)
            nhalf = constcol(-0.5)
            ROP("scalar", lambda e, rss=rss: e.activation(rss[:, 16:32], rss[:, 0:16], AF.Ln, bias=tiny),
                reads=[rnb, bcc], writes=[rnb])
            ROP("scalar", lambda e, rss=rss: e.activation(rss[:, 0:16], rss[:, 16:32], AF.Exp, scale=nhalf),
                reads=[rnb, bcc], writes=[rnb])
            for qi, (reg, rbuf, tabc, tabs, dstf, dstfb) in enumerate(streams):
                regv = reg.rearrange("p (h u) -> p h u", u=64)
                qnv = qn_sb[:, qi, :].rearrange("p (h u) -> p h u", u=64)
                rnv = rss[:, qi * 8:qi * 8 + 8].unsqueeze(2).broadcast_to([128, 8, 64])
                ROP("vector", lambda e, regv=regv, rnv=rnv, qnv=qnv: e.tensor_tensor(
                    qnv, regv, rnv, ALU.mult), reads=[rbuf, rnb], writes=[b_qn[qi]])
            for qi, (reg, rbuf, tabc, tabs, dstf, dstfb) in enumerate(streams):
                qnv = qn_sb[:, qi, :].rearrange("p (h u) -> p h u", u=64)
                c1v = c1_sb[:, qi, :].rearrange("p (h u) -> p h u", u=64)
                tc = cq_sb[:, tabc, t * 64:(t + 1) * 64].unsqueeze(1).broadcast_to([128, 8, 64])
                ROP("gpsimd", lambda e, qnv=qnv, c1v=c1v, tc=tc: e.tensor_tensor(
                    c1v, qnv, tc, ALU.mult), reads=[b_qn[qi], b_cq[0]], writes=[b_c1[qi]])
            for qi, (reg, rbuf, tabc, tabs, dstf, dstfb) in enumerate(streams):
                qnv = qn_sb[:, qi, :].rearrange("p (h u) -> p h u", u=64)
                c2v = c2_sb[:, qi, :].rearrange("p (h u) -> p h u", u=64)
                qsw0 = qnv[:, :, 32:64]
                qsw1 = qnv[:, :, 0:32]
                ts0 = cq_sb[:, tabs, t * 64:t * 64 + 32].unsqueeze(1).broadcast_to([128, 8, 32])
                ts1 = cq_sb[:, tabs, t * 64 + 32:t * 64 + 64].unsqueeze(1).broadcast_to([128, 8, 32])
                ROP("vector", lambda e, qsw0=qsw0, c2v=c2v, ts0=ts0: e.tensor_tensor(
                    c2v[:, :, 0:32], qsw0, ts0, ALU.mult), reads=[b_qn[qi], b_cq[0]], writes=[b_c2[qi]])
                ROP("vector", lambda e, qsw1=qsw1, c2v=c2v, ts1=ts1: e.tensor_tensor(
                    c2v[:, :, 32:64], qsw1, ts1, ALU.mult), reads=[b_qn[qi], b_cq[0]], writes=[b_c2[qi]])
            for qi, (reg, rbuf, tabc, tabs, dstf, dstfb) in enumerate(streams):
                pbf = (qpb_sb, kpb_sb)[qi]
                pbfb = (b_qpb, b_kpb)[qi][0]
                ROP("vector", lambda e, pbf=pbf, qi=qi: e.tensor_add(
                    pbf[:], c1_sb[:, qi, :], c2_sb[:, qi, :]),
                    reads=[b_c1[qi], b_c2[qi]], writes=[pbfb])
                for ci2 in range(4):
                    ROP("tensor", lambda e, ci2=ci2, qi=qi, pbf=pbf: e.transpose(
                        PT[:, qi * 512 + ci2 * 128: qi * 512 + (ci2 + 1) * 128],
                        pbf[:, ci2 * 128:(ci2 + 1) * 128], ident[:]),
                        reads=[pbfb, bident], writes=[ptbuf])
                ROP("vector", lambda e, qi=qi, dstf=dstf: e.tensor_copy(
                    dstf, PT[:, qi * 512:(qi + 1) * 512]), reads=[ptbuf], writes=[dstfb])


        def stage_attn(t):
            ts = t % 2
            kts = ktiles(t)
            m = len(kts)

            def head_front(h):
                hreg = h % 2
                sreg, sbuf_ = sc_reg[hreg], sc_buf[hreg]
                hp = (h % 2) * 64
                hc = h // 2
                acol = 64 + (t % 4) * 8 + (h % 4) * 2
                amax = stat[:, acol:acol + 1]
                bamax = b_stat[16 + (t % 4) * 4 + h % 4]
                for ci2, kt in enumerate(kts):
                    ROP("tensor", lambda e, ci2=ci2, kt=kt, sreg=sreg, hp=hp, hc=hc, ts=ts: e.matmul(
                        sreg[:, ci2 * 128:(ci2 + 1) * 128],
                        qf_sb[hp:hp + 64, ts, hc * 128:(hc + 1) * 128],
                        kf_sb[hp:hp + 64, kt % 6, hc * 128:(hc + 1) * 128],
                        start=True, stop=True),
                        reads=[b_qf[ts], b_kf[kt % 6]], writes=[sbuf_])
                ROP("vector", lambda e, sreg=sreg, m=m, amax=amax: e.reduce_max(
                    amax, sreg[:, 0:m * 128], axis=AX.X, negate=True),
                    reads=[sbuf_], writes=[bamax])
                es_ = h % 2
                ROP("scalar", lambda e, sreg=sreg, m=m, es_=es_, amax=amax: e.activation(
                    exp_sb[:, es_, 0:m * 128], sreg[:, 0:m * 128], AF.Exp, bias=amax),
                    reads=[sbuf_, bamax], writes=[b_exp[es_]])

            def head_back(h):
                hreg = h % 2
                sreg, sbuf_ = sc_reg[hreg], sc_buf[hreg]
                es_ = h % 2
                acol = 64 + (t % 4) * 8 + (h % 4) * 2
                rcol = stat[:, acol + 1:acol + 2]
                bamax = b_stat[16 + (t % 4) * 4 + h % 4]
                for ci2 in range(m):
                    ROP("tensor", lambda e, ci2=ci2, es_=es_: e.transpose(
                        PT[:, ci2 * 128:(ci2 + 1) * 128],
                        exp_sb[:, es_, ci2 * 128:(ci2 + 1) * 128], ident[:]),
                        reads=[b_exp[es_], bident], writes=[ptbuf])
                ROP("vector", lambda e, m=m, es_=es_: e.tensor_tensor(
                    at_sb[:, es_, 0:m * 128], PT[:, 0:m * 128],
                    maskf[:, (5 - m) * 128:640], ALU.mult),
                    reads=[ptbuf, bmaskf], writes=[b_at[es_]])
                areg, abuf_ = av_reg[hreg], av_buf[hreg]
                for ci2, kt in enumerate(kts):
                    ROP("tensor", lambda e, ci2=ci2, kt=kt, areg=areg, h=h, es_=es_, m=m: e.matmul(
                        areg, at_sb[:, es_, ci2 * 128:(ci2 + 1) * 128],
                        v1_sb[:, kt % 6, h * 65:(h + 1) * 65],
                        start=(ci2 == 0), stop=(ci2 == m - 1)),
                        reads=[b_at[es_], b_v1[kt % 6]], writes=[abuf_])
                ROP("vector", lambda e, areg=areg, rcol=rcol: e.reciprocal(rcol, areg[:, 64:65]),
                    reads=[abuf_], writes=[bamax])
                ROP("scalar", lambda e, areg=areg, h=h, ts=ts, rcol=rcol: e.activation(
                    otm_sb[:, ts, h * 64:(h + 1) * 64], areg[:, 0:64], AF.Copy, scale=rcol),
                    reads=[abuf_, bamax], writes=[b_otm[ts]])

            for h in range(FH):
                head_front(h)
                if h >= 1:
                    head_back(h - 1)
            head_back(FH - 1)
            for ci2 in range(4):
                ROP("tensor", lambda e, ci2=ci2, ts=ts: e.transpose(
                    PT[:, ci2 * 128:(ci2 + 1) * 128], otm_sb[:, ts, ci2 * 128:(ci2 + 1) * 128],
                    ident[:]), reads=[b_otm[ts], bident], writes=[ptbuf])
            ROP("vector", lambda e, ts=ts: e.tensor_copy(ofm_sb[:, ts, :], PT[:, 0:512]),
                reads=[ptbuf], writes=[b_ofm[ts]])
            for oh in range(2):
                wreg, wbuf_ = ((sc_reg[0][:, 0:512], sc_buf[0]), (sc_reg[1][:, 0:512], sc_buf[1]))[oh]
                for ci2 in range(4):
                    ROP("tensor", lambda e, ci2=ci2, oh=oh, wreg=wreg, ts=ts: e.matmul(
                        wreg, ofm_sb[:, ts, ci2 * 128:(ci2 + 1) * 128],
                        wo_sb[:, ci2, oh * 512:(oh + 1) * 512],
                        start=(ci2 == 0), stop=(ci2 == 3)),
                        reads=[b_ofm[ts], b_wo[0]], writes=[wbuf_])
                ROP("vector", lambda e, oh=oh, t=t, wreg=wreg: e.tensor_tensor(
                    slot[:, t * D + oh * 512: t * D + (oh + 1) * 512], wreg,
                    slot[:, t * D + oh * 512: t * D + (oh + 1) * 512], ALU.add),
                    reads=[wbuf_, bufs[t]], writes=[bufs[t]])
            ROP("scalar", lambda e, t=t: e.activation(
                sqs_sb[:, 0:512], slot[:, t * D:t * D + 512], AF.Square,
                accum_out=st2_sb[:, 16 + t:17 + t]),
                reads=[bufs[t]], writes=[b_sqs[0], b_st2g[0]])
            ROP("scalar", lambda e, t=t: e.activation(
                sqs_sb[:, 512:1024], slot[:, t * D + 512:(t + 1) * D], AF.Square,
                accum_out=st2_sb[:, 48 + t:49 + t]),
                reads=[bufs[t]], writes=[b_sqs[0], b_st2g[0]])
            ROP("vector", lambda e, t=t: e.reduce_sum(
                st2_sb[:, t:t + 1], slot[:, t * D:t * D + 512], axis=AX.X),
                reads=[bufs[t]], writes=[b_st2g[0]])
            ROP("vector", lambda e, t=t: e.reduce_sum(
                st2_sb[:, 32 + t:33 + t], slot[:, t * D + 512:(t + 1) * D], axis=AX.X),
                reads=[bufs[t]], writes=[b_st2g[0]])


        for t in range(ntiles + 1):
            if t < ntiles:
                stage_load(t)
            if t >= 1:
                stage_attn(t - 1)
            if t < ntiles:
                stage_prep(t)

        if self.stage == "ffn_dbg" and l == 0 and call == 0:
            dbg = {"dbg_cast1": (cast_sb[:, 1, :], b_cast[1]),
                   "dbg_qf1": (qf_sb[:, 1, :], b_qf[1]),
                   "dbg_kf0": (kf_sb[:, 0, :], b_kf[0]),
                   "dbg_kf1": (kf_sb[:, 1, :], b_kf[1]),
                   "dbg_v10": (v1_sb[:, 0, :], b_v1[0]),
                   "dbg_v11": (v1_sb[:, 1, :], b_v1[1]),
                   "dbg_otm1": (otm_sb[:, 1, :], b_otm[1]),
                   "dbg_exp1": (exp_sb[:, 1, :], b_exp[1]),
                   "dbg_at1": (at_sb[:, 1, :], b_at[1]),
                   "dbg_xs1": (xs_sb[:, 1, :, :].rearrange("p a b -> p (a b)"), b_xs[1]),
                   "dbg_cq": (cq_sb[:, 0, :], b_cq[0]),
                   }
            for nm, (ap, bb) in dbg.items():
                o = self.nc.dram_tensor(nm, [128, ap.shape[-1] if ap.ndim == 2 else int(np.prod(ap.shape[1:]))],
                                        ap.dtype, kind="ExternalOutput").ap()
                r.dma("qs", (lambda e, o=o, ap=ap: e.dma_start(out=o, in_=ap)),
                      reads=[bb], writes=[Buf("dbo_" + nm)])
            self.emit_out(slot, bufs)
            self.done = True
            return

        ar.free(wqkv_sb, wo_sb, cs_sb, qsb_sb, qsrow_bf, xs_sb, cast_sb, qn_sb,
                c1_sb, c2_sb, qpb_sb, kpb_sb)
        ar.free(cq_sb, qf_sb, kf_sb, v1_sb, exp_sb, at_sb, otm_sb, ofm_sb, rn_sb)
        pes.close()
        self.psum_phase_end(PB)

        if self.stage == f"ffn_a{l}" and call == 0 and l == 1:
            self.emit_out(slot, bufs)
            self.done = True
            return
        if self.stage == f"ffn_a{l}" and call == 0:
            self.emit_out(slot, bufs)
            self.done = True
            return

        self.ffn_mlp(slot, bufs, l, call)
        if self.stage == f"ffn_l{l}" and call == 0:
            self.emit_out(slot, bufs)
            self.done = True
            return

    def ffn_mlp(self, slot, bufs, l, call):
        c = self.ctx
        nc, r, ar = self.nc, self.rec, self.arena
        ROP = r.op
        d = c["d"]
        PT = self.P["PT"]
        pb = self.pb
        ident = c["ident"]; bident = c["bident"][0]
        identf = c["identf"]; bidentf = c["bidentf"][0]
        fw1, fb1, fw2 = c["fw1"], c["fb1"], c["fw2"]
        lg_pp, lb_pp = c["lg_pp"], c["lb_pp"]
        b_lg, b_lb = c["b_lg"][0], c["b_lb"][0]
        b1a_pp, b1g_pp, b2_pp = c["b1a_pp"], c["b1g_pp"], c["b2_pp"]
        b_b1a, b_b1g, b_b2 = c["b_b1a"][0], c["b_b1g"][0], c["b_b2"][0]
        scr = c["scr"]; b_scr = c["b_scr"][0]
        constcol = self.constcol
        bcc = self.b_ccol

        st2, b_st2 = self._st2, self._b_st2
        sh2, b_sh2 = ar.alloc("sh2", [128, 32], F32)
        cm_sb, b_cm = ar.alloc("cm", [128, 32], F32)
        mr_sb, b_mr = ar.alloc("mr", [128, 64], F32)
        ROP("vector", lambda e: e.memset(mr_sb[:], 0.0), writes=[b_mr[0]])
        nrm_sb, b_nrm = ar.alloc("nrm", [128, 2, D], BF16, 2)
        ln2_sb, b_ln2 = ar.alloc("ln2", [128, NF, 1025], BF16)
        bnd_sb, b_bnd = ar.alloc("bnd", [128, NF, 1], BF16)
        h2_sb, b_h2 = ar.alloc("h2", [128, NH, 1024], BF16)
        w1b_sb, b_w1b = ar.alloc("w1b", [128, 2, NF, 256], BF16, 2)
        abf_sb, b_abf = ar.alloc("abf", [128, 2, 512], BF16, 2)
        gbf_sb, b_gbf = ar.alloc("gbf", [128, 2, 512], BF16, 2)
        wof_sb, b_wof = ar.alloc("wof", [128, NH, 128], BF16)
        w2t_sb, b_w2t = ar.alloc("w2t", [128, 1024], F32)

        ROP("vector", lambda e: e.memset(sh2[0:1, :], 0.0), writes=[b_sh2[0]])
        d("qp", sh2[1:128, 0:16], st2[0:127, 32:48], [b_sh2[0]], rbufs=[b_st2[0]])
        d("qp", sh2[1:128, 16:32], st2[0:127, 48:64], [b_sh2[0]], rbufs=[b_st2[0]])
        d("qp", sh2[0:1, 1:16], st2[127:128, 32:47], [b_sh2[0]], rbufs=[b_st2[0]])
        d("qp", sh2[0:1, 17:32], st2[127:128, 48:63], [b_sh2[0]], rbufs=[b_st2[0]])
        ROP("vector", lambda e: e.tensor_add(cm_sb[:], st2[:, 0:32], sh2[:]),
            reads=[b_st2[0], b_sh2[0]], writes=[b_cm[0]])
        ROP("vector", lambda e: e.tensor_scalar(mr_sb[:, 0:16], cm_sb[:, 0:16], -1.0 / D, None, ALU.mult),
            reads=[b_cm[0]], writes=[b_mr[0]])
        ROP("vector", lambda e: e.tensor_mul(mr_sb[:, 16:32], mr_sb[:, 0:16], mr_sb[:, 0:16]),
            reads=[b_mr[0]], writes=[b_mr[0]])
        ROP("vector", lambda e: e.scalar_tensor_tensor(
            mr_sb[:, 32:48], cm_sb[:, 16:32], 1.0 / D, mr_sb[:, 16:32], ALU.mult, ALU.subtract),
            reads=[b_cm[0], b_mr[0]], writes=[b_mr[0]])
        eps5 = constcol(1e-5)
        nhalf = constcol(-0.5)
        ROP("scalar", lambda e: e.activation(mr_sb[:, 48:64], mr_sb[:, 32:48], AF.Ln, bias=eps5),
            reads=[b_mr[0], bcc], writes=[b_mr[0]])
        ROP("scalar", lambda e: e.activation(mr_sb[:, 16:32], mr_sb[:, 48:64], AF.Exp, scale=nhalf),
            reads=[b_mr[0], bcc], writes=[b_mr[0]])
        d("qp", mr_sb[0:127, 32:64], mr_sb[1:128, 0:32], [b_mr[0]], rbufs=[b_mr[0]])
        d("qp", mr_sb[127:128, 32:47], mr_sb[0:1, 1:16], [b_mr[0]], rbufs=[b_mr[0]])
        d("qp", mr_sb[127:128, 48:63], mr_sb[0:1, 17:32], [b_mr[0]], rbufs=[b_mr[0]])


        pes = ExitStack()
        PS, PB = self.psum_phase(pes, {"pa0": [128, 512], "pa1": [128, 512],
                                       "pg0": [128, 512], "pg1": [128, 512]})
        a_reg = [(PS["pa0"][:], PB["pa0"]), (PS["pa1"][:], PB["pa1"])]
        g_reg = [(PS["pg0"][:], PB["pg0"]), (PS["pg1"][:], PB["pg1"])]
        tb_reg = [(PS["pg0"][:, 0:128], PB["pg0"]), (PS["pg1"][:, 0:128], PB["pg1"])]
        ptbuf = pb["PT"]

        for g in range(2):
            if g == 0:
                ROP("vector", lambda e: e.memset(ln2_sb[:, :, 0:1], 0.0), writes=[b_ln2[0]])
            else:
                ROP("vector", lambda e: e.tensor_copy(ln2_sb[:, :, 0:1], bnd_sb[:]),
                    reads=[b_bnd[0]], writes=[b_ln2[0]])
            for tt in range(8):
                t = 8 * g + tt
                ns = tt % 2
                ROP("vector", lambda e, t=t, ns=ns: e.tensor_scalar(
                    nrm_sb[:, ns, 0:512], slot[:, t * D:t * D + 512],
                    mr_sb[:, t:t + 1], mr_sb[:, 16 + t:17 + t], ALU.add, ALU.mult),
                    reads=[bufs[t], b_mr[0]], writes=[b_nrm[ns]])
                ROP("vector", lambda e, t=t, ns=ns: e.tensor_scalar(
                    nrm_sb[:, ns, 512:1024], slot[:, t * D + 512:(t + 1) * D],
                    mr_sb[:, 32 + t:33 + t], mr_sb[:, 48 + t:49 + t], ALU.add, ALU.mult),
                    reads=[bufs[t], b_mr[0]], writes=[b_nrm[ns]])
                for fc in range(NF):
                    ROP("tensor", lambda e, fc=fc, ns=ns: e.transpose(
                        PT[:, fc * 128:(fc + 1) * 128], nrm_sb[:, ns, fc * 128:(fc + 1) * 128],
                        ident[:]), reads=[b_nrm[ns], bident], writes=[ptbuf])
                for fc in range(NF):
                    ROP("vector", lambda e, fc=fc, tt=tt, lc=l * 8 + fc: e.tensor_scalar(
                        ln2_sb[:, fc, 1 + tt * 128:1 + (tt + 1) * 128],
                        PT[:, fc * 128:(fc + 1) * 128],
                        lg_pp[:, lc:lc + 1], lb_pp[:, lc:lc + 1], ALU.mult, ALU.add),
                        reads=[ptbuf, b_lg, b_lb], writes=[b_ln2[0]])
            if g == 0:
                ROP("vector", lambda e: e.tensor_copy(bnd_sb[:], ln2_sb[:, :, 1024:1025]),
                    reads=[b_ln2[0]], writes=[b_bnd[0]])

            for j in range(NH):
                wj = min(128, HID - 128 * j)
                jp = j % 2
                d("qp", w1b_sb[:, jp, :, 0:wj],
                  fw1[l, :, 128 * j:128 * j + wj].rearrange("(a p) h -> p a h", p=128),
                  [b_w1b[jp]])
                d("qp", w1b_sb[:, jp, :, 128:128 + wj],
                  fw1[l, :, HID + 128 * j:HID + 128 * j + wj].rearrange("(a p) h -> p a h", p=128),
                  [b_w1b[jp]])
                for th in range(2):
                    (pa, pab), (pg, pgb) = a_reg[th], g_reg[th]
                    for fc in range(NF):
                        off = (1 if fc < 4 else 0) + th * 512
                        ROP("tensor", lambda e, fc=fc, jp=jp, wj=wj, pa=pa, off=off: e.matmul(
                            pa[0:wj, :], w1b_sb[:, jp, fc, 0:wj], ln2_sb[:, fc, off:off + 512],
                            start=(fc == 0), stop=(fc == NF - 1)),
                            reads=[b_w1b[jp], b_ln2[0]], writes=[pab])
                    for fc in range(NF):
                        off = (1 if fc < 4 else 0) + th * 512
                        ROP("tensor", lambda e, fc=fc, jp=jp, wj=wj, pg=pg, off=off: e.matmul(
                            pg[0:wj, :], w1b_sb[:, jp, fc, 128:128 + wj], ln2_sb[:, fc, off:off + 512],
                            start=(fc == 0), stop=(fc == NF - 1)),
                            reads=[b_w1b[jp], b_ln2[0]], writes=[pgb])
                    ROP("vector", lambda e, th=th, j=j, pa=pa: e.tensor_scalar(
                        abf_sb[:, th, :], pa, b1a_pp[:, l * NH + j:l * NH + j + 1], None, ALU.add),
                        reads=[pab, b_b1a], writes=[b_abf[th]])
                    ROP("scalar", lambda e, th=th, j=j, pg=pg: e.activation(
                        gbf_sb[:, th, :], pg, AF.Gelu, bias=b1g_pp[:, l * NH + j:l * NH + j + 1]),
                        reads=[pgb, b_b1g], writes=[b_gbf[th]])
                    ROP("vector", lambda e, th=th, j=j: e.tensor_mul(
                        h2_sb[:, j, th * 512:(th + 1) * 512], abf_sb[:, th, :], gbf_sb[:, th, :]),
                        reads=[b_abf[th], b_gbf[th]], writes=[b_h2[0]])

            for of in range(8):
                d("qp", wof_sb[:, 0:21, :],
                  fw2[l, 0:2688, of * 128:(of + 1) * 128].rearrange("(j p) o -> p j o", p=128),
                  [b_wof[0]])
                d("qp", wof_sb[0:42, 21, :], fw2[l, 2688:HID, of * 128:(of + 1) * 128],
                  [b_wof[0]])
                for th in range(2):
                    pa, pab = a_reg[th]
                    for j in range(NH):
                        wj = min(128, HID - 128 * j)
                        ROP("tensor", lambda e, j=j, wj=wj, th=th, pa=pa: e.matmul(
                            pa, wof_sb[0:wj, j, :], h2_sb[0:wj, j, th * 512:(th + 1) * 512],
                            start=(j == 0), stop=(j == NH - 1)),
                            reads=[b_wof[0], b_h2[0]], writes=[pab])
                    ROP("vector", lambda e, th=th, of=of, pa=pa: e.tensor_scalar(
                        w2t_sb[:, th * 512:(th + 1) * 512], pa,
                        b2_pp[:, l * 8 + of:l * 8 + of + 1], None, ALU.add),
                        reads=[pab, b_b2], writes=[b_w2t[0]])
                for tt in range(8):
                    t = 8 * g + tt
                    (tr, trb) = tb_reg[tt % 2]
                    ROP("tensor", lambda e, tt=tt, tr=tr: e.transpose(
                        tr[:, 0:128], w2t_sb[:, tt * 128:(tt + 1) * 128], identf[:]),
                        reads=[b_w2t[0], bidentf], writes=[trb])
                    ROP("vector", lambda e, t=t, of=of, tr=tr: e.tensor_tensor(
                        slot[:, t * D + of * 128:t * D + (of + 1) * 128], tr[:, 0:128],
                        slot[:, t * D + of * 128:t * D + (of + 1) * 128], ALU.add),
                        reads=[trb, bufs[t]], writes=[bufs[t]])

        ar.free(st2, sh2, cm_sb, mr_sb, nrm_sb, ln2_sb, bnd_sb, h2_sb, w1b_sb,
                abf_sb, gbf_sb, wof_sb, w2t_sb)
        self._st2 = None
        pes.close()
        self.psum_phase_end(PB)

    def linattn(self, slotX, bufX, slotO, bufO):
        c = self.ctx
        nc, r, ar = self.nc, self.rec, self.arena
        ROP = r.op
        d = c["d"]
        PT = self.P["PT"]
        pb = self.pb
        ident = c["ident"]; bident = c["bident"][0]
        maskl = c["maskl"]; bmaskl = c["bmaskl"][0]
        onesb = c["onesb"]; bonesb = c["bonesb"][0]
        aw_qkv, aw_out, ab_qkv, ab_out = c["aw_qkv"], c["aw_out"], c["ab_qkv"], c["ab_out"]
        ln_full_tile = c["ln_full_tile"]
        QN = 4

        abrow, b_abrow = ar.alloc("abrow", [1, 3072], BF16)
        aorow, b_aorow = ar.alloc("aorow", [1, 1024], BF16)
        S_sb, b_S = ar.alloc("S", [128, 2, 520], F32, 2)
        Sb_sb, b_Sb = ar.alloc("Sbf", [128, 2, 520], BF16, 2)
        nrc_sb, b_nrc = ar.alloc("nrc", [128, 4], F32, 2)
        qtm_sb, b_qtm = ar.alloc("qtm", [128, QN, 1024], BF16, QN)
        ktm_sb, b_ktm = ar.alloc("ktm", [128, QN, 1024], BF16, QN)
        v1_sb, b_v1 = ar.alloc("v1l", [128, QN, LH * 65], BF16, QN)
        d("qp", abrow[:], ab_qkv.unsqueeze(0), [b_abrow[0]])
        d("qp", aorow[:], ab_out.unsqueeze(0), [b_aorow[0]])
        ROP("vector", lambda e: e.memset(S_sb[:, 1, :], 0.0), writes=[b_S[1]])
        ROP("vector", lambda e: e.memset(Sb_sb[:, 1, :], 0.0), writes=[b_Sb[1]])

        ptbuf = pb["PT"]

        def _quarter(qr):
            tiles = list(range(QN * qr, QN * qr + QN))
            ln1_sb, b_ln1 = ar.alloc("ln1f", [128, NF, QN * 128], BF16)
            lncast_sb, b_lncast = ar.alloc("lncast", [128, 2, D], BF16, 2)
            wst_sb, b_wst = ar.alloc("wst", [128, 1024], BF16)
            wt_sb, b_wt = ar.alloc("wtog", [128, NF, 512], BF16)
            emin_sb, b_emin = ar.alloc("emin", [128, 512], F32)
            eexp_sb, b_eexp = ar.alloc("eexp", [128, 512], F32)
            for s in range(QN):
                ROP("vector", lambda e, s=s: e.memset(
                    v1_sb[:, s, :].rearrange("p (h u) -> p h u", u=65)[:, :, 64:65], 1.0),
                    writes=[b_v1[s]])
            for tt, t in enumerate(tiles):
                ns = tt % 2
                ln_full_tile(slotX[:, t * D:(t + 1) * D], [bufX[t]],
                             lncast_sb[:, ns, :], [b_lncast[ns]], t % 16)
                for fc in range(NF):
                    ROP("tensor", lambda e, fc=fc, ns=ns: e.transpose(
                        PT[:, fc * 128:(fc + 1) * 128],
                        lncast_sb[:, ns, fc * 128:(fc + 1) * 128], ident[:]),
                        reads=[b_lncast[ns], bident], writes=[ptbuf])
                ROP("vector", lambda e, tt=tt: e.tensor_copy(
                    ln1_sb[:, :, tt * 128:(tt + 1) * 128],
                    PT[:].rearrange("p (cc u) -> p cc u", u=128)),
                    reads=[ptbuf], writes=[b_ln1[0]])
            pesq = ExitStack()
            PSq, PBq = self.psum_phase(pesq, {"l0": [128, 512], "l1": [128, 512],
                                              "l2": [128, 512]})
            qkrot = [(PSq["l0"][:], PBq["l0"]), (PSq["l1"][:], PBq["l1"]),
                     (PSq["l2"][:], PBq["l2"])]
            if self.stage in ("lins_dbg", "lins_dbg2"):
                def _mkcb(tag, names):
                    def _cb(sim, tag=tag, names=names):
                        import numpy as np
                        msg = [tag]
                        for nm in names:
                            try:
                                a = np.array(sim.tensor(nm)).astype(np.float32)
                                fin = np.isfinite(a)
                                msg.append(f"{nm}:max={np.abs(a[fin]).max() if fin.any() else -1:.4f},nan={int((~fin).sum())}")
                            except Exception as ex:
                                msg.append(f"{nm}:ERR")
                        print("[CB]", *msg)
                    return _cb
                def _fn0(e):
                    from concourse.bass_interp import add_callback
                    return add_callback(e, _mkcb("post-ln1", [ln1_sb.name]))
                r.op("vector", _fn0, reads=[b_ln1[0]])
            for og in range(6):
                for i in range(4):
                    d("qp", wst_sb[:], aw_qkv[og * 512 + i * 128: og * 512 + (i + 1) * 128, :],
                      [b_wst[0]])
                    for fc in range(NF):
                        ROP("tensor", lambda e, fc=fc, w=wst_sb: e.transpose(
                            PT[:, fc * 128:(fc + 1) * 128], w[:, fc * 128:(fc + 1) * 128],
                            ident[:]), reads=[b_wst[0], bident], writes=[ptbuf])
                    if self.stage in ("lins_dbg", "lins_dbg2", "linq_dbg") and og == 0 and i == 0:
                        def _fnq(e):
                            from concourse.bass_interp import add_callback2
                            def _cbq(sim, inst):
                                import numpy as np
                                from concourse.bass_interp import Direction
                                v = sim.instruction_executor.view_arg(
                                    inst.ins[0], Direction.READ, inst, reg_snapshot=None)
                                a = np.asarray(v).astype(np.float32)
                                fin = np.isfinite(a)
                                print("[CBQ] PT view:", a.shape,
                                      "max", np.abs(a[fin]).max() if fin.any() else -1,
                                      "nan", int((~fin).sum()))
                                try:
                                    sems = {k: v for k, v in sim.semaphores.items()} if hasattr(sim, "semaphores") else None
                                    print("[CBQ] sems:", sems)
                                except Exception as ex:
                                    print("[CBQ] semerr", ex)
                                st = [x for x in dir(sim) if "sem" in x.lower() or "engine" in x.lower()]
                                print("[CBQ] attrs:", st[:12])
                            return add_callback2(e, _cbq, [PT[:]])
                        r.op("vector", _fnq, reads=[ptbuf])
                        _lasttok = ptbuf.w
                        def _fnp(e, _lt=_lasttok):
                            from concourse.bass_interp import add_callback
                            def _cbp(sim, _lt=_lt):
                                import numpy as np
                                for nm2 in ("PsT", "PsT_set", "pp1_l0"):
                                    try:
                                        a = np.array(sim.tensor(nm2)).astype(np.float32)
                                        fin = np.isfinite(a)
                                        print("[CB]", nm2, "shape", a.shape, "max",
                                              np.abs(a[fin]).max() if fin.any() else -1,
                                              "nan", int((~fin).sum()))
                                    except Exception as ex:
                                        print("[CB]", nm2, "ERR", str(ex)[:80])
                            return add_callback(e, _cbp)
                        r.op("vector", _fnp, reads=[ptbuf])
                    ROP("vector", lambda e, i=i: e.tensor_copy(
                        wt_sb[:, :, i * 128:(i + 1) * 128],
                        PT[:].rearrange("p (cc u) -> p cc u", u=128)),
                        reads=[ptbuf], writes=[b_wt[0]])
                for tt in range(QN):
                    ps, psb = qkrot[tt % 3]
                    for fc in range(NF):
                        ROP("tensor", lambda e, fc=fc, tt=tt, ps=ps: e.matmul(
                            ps, ln1_sb[:, fc, tt * 128:(tt + 1) * 128], wt_sb[:, fc, :],
                            start=(fc == 0), stop=False),
                            reads=[b_ln1[0], b_wt[0]], writes=[psb])
                    ROP("tensor", lambda e, og=og, ps=ps: e.matmul(
                        ps, onesb[:], abrow[0:1, og * 512:(og + 1) * 512],
                        start=False, stop=True),
                        reads=[bonesb, b_abrow[0]], writes=[psb])
                    if og < 4:
                        dst = qtm_sb if og < 2 else ktm_sb
                        dbuf = (b_qtm if og < 2 else b_ktm)[tt]
                        dslice = (og % 2) * 512
                        if self.stage == "lins_dbg3" and og == 0 and tt == 0 and qr == 0:
                            psnap, b_psnap = ar.alloc("psnap", [128, 512], F32)
                            self._psnap = psnap
                            ROP("vector", lambda e, ps=ps: e.tensor_copy(psnap[:], ps),
                                reads=[psb], writes=[b_psnap[0]])
                        ROP("vector", lambda e, ps=ps: e.tensor_scalar_min(emin_sb[:], ps, 0.0),
                            reads=[psb], writes=[b_emin[0]])
                        ROP("scalar", lambda e: e.activation(eexp_sb[:], emin_sb[:], AF.Exp),
                            reads=[b_emin[0]], writes=[b_eexp[0]])
                        ROP("vector", lambda e, ps=ps, dst=dst, tt=tt, dslice=dslice: e.scalar_tensor_tensor(
                            dst[:, tt, dslice:dslice + 512], ps, 0.0, eexp_sb[:],
                            ALU.max, ALU.add),
                            reads=[psb, b_eexp[0]], writes=[dbuf])
                    else:
                        hof = 8 * (og - 4)
                        ROP("vector", lambda e, ps=ps, tt=tt, hof=hof: e.tensor_copy(
                            v1_sb[:, tt, hof * 65:(hof + 8) * 65].rearrange(
                                "p (h u) -> p h u", u=65)[:, :, 0:64],
                            ps.rearrange("p (h u) -> p h u", u=64)),
                            reads=[psb], writes=[b_v1[tt]])
                if self.stage in ("lins_dbg", "lins_dbg2"):
                    def _fn1(e, og=og):
                        from concourse.bass_interp import add_callback
                        return add_callback(e, _mkcb(f"post-og{og}",
                            [wt_sb.name, qtm_sb.name, v1_sb.name]))
                    r.op("vector", _fn1, reads=[b_wt[0], b_qtm[0], b_v1[0]])
            if self.stage == "linq_dbg":
                self.done = True
                return
            if self.stage == "lins_dbg":
                def _cb(sim, qtm_t=qtm_sb):
                    import numpy as np
                    a = np.array(sim.tensors.mems.get("ar7_qtm_53", [[0]])) if hasattr(sim, "tensors") else None
                    try:
                        a = np.array(sim.tensor("ar7_qtm_53")).astype(np.float32)
                        print("[CB] post-qkv qtm absmax:", np.abs(a).max())
                    except Exception as ex:
                        print("[CB] err", ex)
                def _fn(e):
                    from concourse.bass_interp import add_callback
                    return add_callback(e, _cb)
                r.op("vector", _fn, reads=[b_qtm[0], b_qtm[1], b_qtm[2], b_qtm[3]])
            ar.free(ln1_sb, lncast_sb, wst_sb, wt_sb, emin_sb, eexp_sb)
            pesq.close()
            self.psum_phase_end(PBq)

            pess = ExitStack()
            PSs, PBs = self.psum_phase(pess, {
                "at0": [128, 128], "at1": [128, 128], "at2": [128, 128],
                "dsA": [128, 512], "dsB": [128, 512],
                "avA": [128, 128], "avB": [128, 128]})
            woT_sb, b_woT = ar.alloc("woT", [128, NF, 1024], BF16)
            wos_sb, b_wos = ar.alloc("wst2", [128, 1024], BF16)
            qf_sb, b_qf = ar.alloc("qfl", [128, 2, 1024], BF16, 2)
            kf_sb, b_kf = ar.alloc("kfl", [128, 2, 1024], BF16, 2)
            at_sb, b_at = ar.alloc("atl", [128, 3, 128], BF16, 3)
            otm_sb, b_otm = ar.alloc("otml", [128, 1, 1024], BF16, 1)
            ofm_sb, b_ofm = ar.alloc("ofml", [128, 1, 1024], BF16, 1)
            for i in range(NF):
                d("qp", wos_sb[:], aw_out[i * 128:(i + 1) * 128, :], [b_wos[0]])
                for fc in range(NF):
                    ROP("tensor", lambda e, fc=fc, w=wos_sb: e.transpose(
                        PT[:, fc * 128:(fc + 1) * 128], w[:, fc * 128:(fc + 1) * 128],
                        ident[:]), reads=[b_wos[0], bident], writes=[ptbuf])
                ROP("vector", lambda e, i=i: e.tensor_copy(
                    woT_sb[:, :, i * 128:(i + 1) * 128],
                    PT[:].rearrange("p (cc u) -> p cc u", u=128)),
                    reads=[ptbuf], writes=[b_woT[0]])
            for tt in range(QN):
                tg = QN * qr + tt
                qs_ = tt % 2
                for fc in range(NF):
                    ROP("tensor", lambda e, fc=fc, tt=tt: e.transpose(
                        PT[:, fc * 128:(fc + 1) * 128],
                        qtm_sb[:, tt, fc * 128:(fc + 1) * 128], ident[:]),
                        reads=[b_qtm[tt], bident], writes=[ptbuf])
                ROP("vector", lambda e, qs_=qs_: e.tensor_copy(qf_sb[:, qs_, :], PT[:]),
                    reads=[ptbuf], writes=[b_qf[qs_]])
                for fc in range(NF):
                    ROP("tensor", lambda e, fc=fc, tt=tt: e.transpose(
                        PT[:, fc * 128:(fc + 1) * 128],
                        ktm_sb[:, tt, fc * 128:(fc + 1) * 128], ident[:]),
                        reads=[b_ktm[tt], bident], writes=[ptbuf])
                ROP("vector", lambda e, qs_=qs_: e.tensor_copy(kf_sb[:, qs_, :], PT[:]),
                    reads=[ptbuf], writes=[b_kf[qs_]])
                atrot = [(PSs["at0"][:, 0:128], PBs["at0"]),
                         (PSs["at1"][:, 0:128], PBs["at1"]),
                         (PSs["at2"][:, 0:128], PBs["at2"])]
                avrot = [(PSs["avA"][:, 0:65], PBs["avA"]),
                         (PSs["avB"][:, 0:65], PBs["avB"])]
                Sold = (tg + 1) % 2
                for h in range(LH):
                    hp = 64 * (h % 2)
                    hc = h // 2
                    areg, abuf_ = atrot[h % 3]
                    ROP("tensor", lambda e, areg=areg, hp=hp, hc=hc, qs_=qs_: e.matmul(
                        areg, kf_sb[hp:hp + 64, qs_, hc * 128:(hc + 1) * 128],
                        qf_sb[hp:hp + 64, qs_, hc * 128:(hc + 1) * 128],
                        start=True, stop=True),
                        reads=[b_kf[qs_], b_qf[qs_]], writes=[abuf_])
                    ats = h % 3
                    ROP("vector", lambda e, areg=areg, ats=ats: e.tensor_tensor(
                        at_sb[:, ats, :], areg, maskl[:], ALU.mult),
                        reads=[abuf_, bmaskl], writes=[b_at[ats]])
                    vreg, vbuf_ = avrot[h % 2]
                    ROP("tensor", lambda e, vreg=vreg, ats=ats, tt=tt, h=h: e.matmul(
                        vreg, at_sb[:, ats, :], v1_sb[:, tt, h * 65:(h + 1) * 65],
                        start=True, stop=False),
                        reads=[b_at[ats], b_v1[tt]], writes=[vbuf_])
                    ROP("tensor", lambda e, vreg=vreg, hp=hp, hc=hc, qs_=qs_, So=Sold: e.matmul(
                        vreg, qf_sb[hp:hp + 64, qs_, hc * 128:(hc + 1) * 128],
                        Sb_sb[hp:hp + 64, So, hc * 65:(hc + 1) * 65],
                        start=False, stop=True),
                        reads=[b_qf[qs_], b_Sb[Sold]], writes=[vbuf_])
                    nq = h % 2
                    ROP("vector", lambda e, vreg=vreg, nq=nq: e.tensor_scalar(
                        nrc_sb[:, nq * 2:nq * 2 + 1], vreg[:, 64:65], 1e-6, None, ALU.add),
                        reads=[vbuf_], writes=[b_nrc[nq]])
                    ROP("vector", lambda e, nq=nq: e.reciprocal(
                        nrc_sb[:, nq * 2 + 1:nq * 2 + 2], nrc_sb[:, nq * 2:nq * 2 + 1]),
                        reads=[b_nrc[nq]], writes=[b_nrc[nq]])
                    ROP("scalar", lambda e, vreg=vreg, h=h, nq=nq: e.activation(
                        otm_sb[:, 0, h * 64:(h + 1) * 64], vreg[:, 0:64], AF.Copy,
                        scale=nrc_sb[:, nq * 2 + 1:nq * 2 + 2]),
                        reads=[vbuf_, b_nrc[nq]], writes=[b_otm[0]])
                    dreg_b = PBs["dsA"] if hc < 4 else PBs["dsB"]
                    dap = (PSs["dsA"][hp:hp + 64, 65 * hc:65 * hc + 65] if hc < 4 else
                           PSs["dsB"][hp:hp + 64, 65 * (hc - 4):65 * (hc - 4) + 65])
                    ROP("tensor", lambda e, dap=dap, tt=tt, h=h: e.matmul(
                        dap, ktm_sb[:, tt, h * 64:(h + 1) * 64],
                        v1_sb[:, tt, h * 65:(h + 1) * 65],
                        start=True, stop=True),
                        reads=[b_ktm[tt], b_v1[tt]], writes=[dreg_b])
                Snew = tg % 2
                ROP("vector", lambda e, Snew=Snew, So=Sold, dsA=PSs["dsA"]: e.tensor_tensor(
                    S_sb[:, Snew, 0:260], dsA[:, 0:260], S_sb[:, So, 0:260], ALU.add),
                    reads=[PBs["dsA"], b_S[Sold]], writes=[b_S[Snew]])
                ROP("vector", lambda e, Snew=Snew, So=Sold, dsB=PSs["dsB"]: e.tensor_tensor(
                    S_sb[:, Snew, 260:520], dsB[:, 0:260], S_sb[:, So, 260:520], ALU.add),
                    reads=[PBs["dsB"], b_S[Sold]], writes=[b_S[Snew]])
                ROP("vector", lambda e, Snew=Snew: e.tensor_copy(
                    Sb_sb[:, Snew, 0:260], S_sb[:, Snew, 0:260]),
                    reads=[b_S[Snew]], writes=[b_Sb[Snew]])
                ROP("vector", lambda e, Snew=Snew: e.tensor_copy(
                    Sb_sb[:, Snew, 260:520], S_sb[:, Snew, 260:520]),
                    reads=[b_S[Snew]], writes=[b_Sb[Snew]])
                for fc in range(NF):
                    ROP("tensor", lambda e, fc=fc: e.transpose(
                        PT[:, fc * 128:(fc + 1) * 128],
                        otm_sb[:, 0, fc * 128:(fc + 1) * 128], ident[:]),
                        reads=[b_otm[0], bident], writes=[ptbuf])
                ROP("vector", lambda e: e.tensor_copy(ofm_sb[:, 0, :], PT[:]),
                    reads=[ptbuf], writes=[b_ofm[0]])
                for oh in range(2):
                    oreg, orb = ((PSs["dsA"][:, 0:512], PBs["dsA"]),
                                 (PSs["dsB"][:, 0:512], PBs["dsB"]))[oh]
                    for fc in range(NF):
                        ROP("tensor", lambda e, fc=fc, oh=oh, oreg=oreg: e.matmul(
                            oreg, ofm_sb[:, 0, fc * 128:(fc + 1) * 128],
                            woT_sb[:, fc, oh * 512:(oh + 1) * 512],
                            start=(fc == 0), stop=False),
                            reads=[b_ofm[0], b_woT[0]], writes=[orb])
                    ROP("tensor", lambda e, oh=oh, oreg=oreg: e.matmul(
                        oreg, onesb[:], aorow[0:1, oh * 512:(oh + 1) * 512],
                        start=False, stop=True),
                        reads=[bonesb, b_aorow[0]], writes=[orb])
                    ROP("vector", lambda e, oh=oh, tg=tg, oreg=oreg: e.tensor_tensor(
                        slotO[:, tg * D + oh * 512:tg * D + (oh + 1) * 512], oreg,
                        slotX[:, tg * D + oh * 512:tg * D + (oh + 1) * 512], ALU.add),
                        reads=[orb, bufX[tg]], writes=[bufO[tg]])
            if self.stage in ("lins_dbg", "lins_dbg2", "lins_dbg3"):
                self.done = True
                return
            ar.free(woT_sb, wos_sb, qf_sb, kf_sb, at_sb, otm_sb, ofm_sb)
            pess.close()
            self.psum_phase_end(PBs)

        for qr in range(4):
            _quarter(qr)
            if self.done:
                return

        ar.free(abrow, aorow, S_sb, Sb_sb, nrc_sb, qtm_sb, ktm_sb, v1_sb)


_CACHE = {}

def _get_mk():
    if STAGE not in _CACHE:
        _CACHE[STAGE] = MK(STAGE)
    return _CACHE[STAGE]

def kernel(L0_attn_x, L0_attn_w_qkv, L0_attn_b_qkv, L0_attn_w_out, L0_attn_b_out,
           L0_ffn_wqkv, L0_ffn_q_scale, L0_ffn_k_scale, L0_ffn_wo, L0_ffn_ln_g,
           L0_ffn_ln_b, L0_ffn_w1, L0_ffn_b1, L0_ffn_w2, L0_ffn_b2):
    mk = _get_mk()
    csq, sxq, csk, sxk = rot_tables()
    consts = {
        "c_ident": np.eye(128, dtype=np.float32).astype(BF),
        "c_identf": np.eye(128, dtype=np.float32),
        "c_maskf": mask_t_ffn(),
        "c_maskl": mask_t_lin(),
        "c_csq": csq, "c_sxq": sxq, "c_csk": csk, "c_sxk": sxk,
    }
    shared = {
        "aw_qkv": np.ascontiguousarray(L0_attn_w_qkv, np.float32),
        "ab_qkv": np.ascontiguousarray(L0_attn_b_qkv, np.float32),
        "aw_out": np.ascontiguousarray(L0_attn_w_out, np.float32),
        "ab_out": np.ascontiguousarray(L0_attn_b_out, np.float32),
        "fw_qkv": np.ascontiguousarray(L0_ffn_wqkv, np.float32),
        "fqs": np.ascontiguousarray(L0_ffn_q_scale, np.float32),
        "fks": np.ascontiguousarray(L0_ffn_k_scale, np.float32),
        "fwo": np.ascontiguousarray(L0_ffn_wo, np.float32),
        "flng": np.ascontiguousarray(L0_ffn_ln_g, np.float32),
        "flnb": np.ascontiguousarray(L0_ffn_ln_b, np.float32),
        "fw1": np.ascontiguousarray(L0_ffn_w1, np.float32),
        "fb1": np.ascontiguousarray(L0_ffn_b1, np.float32),
        "fw2": np.ascontiguousarray(L0_ffn_w2, np.float32),
        "fb2": np.ascontiguousarray(L0_ffn_b2, np.float32),
    }
    shared.update(consts)
    x = np.ascontiguousarray(L0_attn_x, np.float32)
    in_maps = []
    for c in range(8):
        m = dict(shared)
        m["x"] = np.ascontiguousarray(x[c])
        in_maps.append(m)
    res = run_bass_kernel_spmd(mk.nc, in_maps, core_ids=list(range(8)))
    if STAGE == "ffn_dbg":
        return res.results[0]
    return np.stack([res.results[c]["y"] for c in range(8)], axis=0)


# version 16
# speedup vs baseline: 1.0585x; 1.0321x over previous
import numpy as np
import ml_dtypes
from contextlib import ExitStack
import concourse.bass as bass
import concourse.mybir as mybir
from concourse.bass_utils import run_bass_kernel_spmd

F32 = mybir.dt.float32
BF16 = mybir.dt.bfloat16
AF = mybir.ActivationFunctionType
ALU = mybir.AluOpType
AX = mybir.AxisListType
BF = ml_dtypes.bfloat16

T = 2048
D = 1024
NT = 16
NF = 8
FH = 8
FDH = 64
WIN = 512
HID = 2730
NH = 22
QK_SCALE = 8.0
LH = 16

STAGE = "full"


def rot_tables():
    half = np.arange(0, FDH, 2, dtype=np.float64)
    inv_freq = 1.0 / 10000.0 ** (half / FDH)
    t = np.arange(T, dtype=np.float64)
    freqs = t[:, None] * inv_freq[None, :]
    freqs64 = np.concatenate([freqs, freqs], axis=-1)
    base = (half + 0.4 * FDH) / (1.4 * FDH)
    power = (t - T // 2) / float(WIN // 2)
    scale = base[None, :] ** power[:, None]
    scale64 = np.concatenate([scale, scale], axis=-1)
    cos, sin = np.cos(freqs64), np.sin(freqs64)
    sign = np.where(np.arange(FDH) < FDH // 2, -1.0, 1.0)[None, :]
    tabs = [QK_SCALE * cos * scale64, QK_SCALE * sign * sin * scale64,
            cos / scale64, sign * sin / scale64]
    out = []
    for a in tabs:
        out.append(np.ascontiguousarray(
            a.reshape(NT, 128, FDH).transpose(1, 0, 2).reshape(128, NT * FDH)
        ).astype(BF))
    return out

def mask_t_ffn():
    out = np.zeros((128, 5 * 128), np.float32)
    u = np.arange(128)[:, None]
    rr = np.arange(128)[None, :]
    for c in range(5):
        qk = 128 * (4 - c) + rr - u
        out[:, c * 128:(c + 1) * 128] = ((qk >= 0) & (qk <= WIN)).astype(np.float32)
    return out.astype(BF)

def mask_t_lin():
    k = np.arange(128)[:, None]
    q = np.arange(128)[None, :]
    return (k <= q).astype(np.float32).astype(BF)


class Buf:
    __slots__ = ("name", "w", "r")
    def __init__(self, name, parents=()):
        self.name = name
        self.w = None
        self.r = []
        for p in parents:
            if p.w is not None:
                self.r.append(p.w)
            self.r.extend(p.r)

ENGS = ("sync", "tensor", "vector", "scalar", "gpsimd")
QUEUES = {"qs": "sync", "qp": "gpsimd", "qa": "scalar"}

class Rec:
    def __init__(self, nc, es):
        self.nc = nc
        self.ops = {e: [] for e in ENGS}
        self.need_inc = {e: set() for e in ENGS}
        self.dmacnt = {q: 0 for q in QUEUES}
        self.emitted = {}
        self.sems = {}
        for e in ENGS:
            self.sems[e] = es.enter_context(nc.semaphore("sem_" + e))
        for q in QUEUES:
            self.sems[q] = es.enter_context(nc.semaphore("sem_" + q))

    def _record(self, eng, fn, reads, writes, queue):
        import sys
        fr = sys._getframe(2)
        label = f"{fr.f_code.co_name}:{fr.f_lineno}"
        deps = []
        for b in reads:
            if b.w is not None:
                deps.append(b.w)
        for b in writes:
            if b.w is not None:
                deps.append(b.w)
            deps.extend(b.r)
        fdeps = {}
        for (d0, d1) in deps:
            if d0 == eng and eng == "tensor":
                continue
            fdeps[d0] = max(fdeps.get(d0, 0), d1)
        fdeps = sorted(fdeps.items())
        idx = len(self.ops[eng])
        self.ops[eng].append((fn, fdeps, queue, label))
        if queue is None:
            tok = (eng, idx)
        else:
            self.dmacnt[queue] += 16
            tok = (queue, self.dmacnt[queue])
        for d0, d1 in fdeps:
            if d0 not in QUEUES:
                self.need_inc[d0].add(d1)
        for b in reads:
            b.r.append(tok)
        for b in writes:
            b.w = tok
            b.r = []
        return tok

    def op(self, eng, fn, reads=(), writes=()):
        return self._record(eng, fn, list(reads), list(writes), None)

    def dma(self, queue, fn, reads=(), writes=()):
        return self._record(QUEUES[queue], fn, list(reads), list(writes), queue)

    def finalize_and_emit(self):
        mil = {}
        for e in ENGS:
            mil[e] = {idx: k + 1 for k, idx in enumerate(sorted(self.need_inc[e]))}
        rec = self

        def mkreplay(eng):
            def replay(e):
                hw = {}
                qcnt = {q: 0 for q in QUEUES}
                for idx, (fn, fdeps, queue, label) in enumerate(rec.ops[eng]):
                    if queue is not None:
                        prev = qcnt[queue] * 16
                        qcnt[queue] += 1
                        if prev > 0 and hw.get(queue, 0) < prev:
                            hw[queue] = prev
                            e.wait_ge(rec.sems[queue], prev)
                    for d0, d1 in fdeps:
                        val = d1 if d0 in QUEUES else mil[d0][d1]
                        if hw.get(d0, 0) >= val:
                            continue
                        hw[d0] = val
                        e.wait_ge(rec.sems[d0], val)
                    ins = fn(e)
                    rec.emitted.setdefault(eng, []).append((ins.ins.name, idx))
                    if queue is not None:
                        ins.then_inc(rec.sems[queue], 16)
                    elif idx in mil[eng]:
                        ins.then_inc(rec.sems[eng], 1)
                if eng == "sync":
                    for q in QUEUES:
                        if rec.dmacnt[q] > 0 and hw.get(q, 0) < rec.dmacnt[q]:
                            e.wait_ge(rec.sems[q], rec.dmacnt[q])
            return replay

        with self.nc.Block() as blk:
            blk.sync(mkreplay("sync"))
            blk.tensor(mkreplay("tensor"))
            blk.vector(mkreplay("vector"))
            blk.scalar(mkreplay("scalar"))
            blk.gpsimd(mkreplay("gpsimd"))

    def nops(self):
        return {e: len(self.ops[e]) for e in ENGS}

class Arena:
    def __init__(self, nc, base, top):
        self.nc = nc
        self.base = base
        self.top = top
        self.free_iv = [(base, top)]
        self.live = {}
        self.hist = []
        self.count = 0

    def alloc(self, name, shape, dtype, nbufs=1):
        itemsize = np.dtype(mybir.dt.np(dtype)).itemsize
        size = int(np.prod(shape[1:])) * itemsize
        size = (size + 31) & ~31
        off = None
        for i, (s, e) in enumerate(self.free_iv):
            if e - s >= size:
                off = s
                if s + size < e:
                    self.free_iv[i] = (s + size, e)
                else:
                    del self.free_iv[i]
                break
        if off is None:
            raise MemoryError(f"arena overflow: {name} needs {size}; free={self.free_iv}")
        parents = []
        for (fo, fs, fb) in self.hist:
            if fo < off + size and off < fo + fs:
                parents.extend(fb)
        self.count += 1
        t = self.nc.alloc_sbuf_tensor_at(f"ar{self.count}_{name}", list(shape), dtype, offset=off)
        bufs = [Buf(f"{name}[{i}]", parents) for i in range(nbufs)]
        self.live[t.name] = (off, size, bufs)
        self.log = getattr(self, "log", [])
        self.log.append(("A", t.name, off, size))
        return t, bufs

    def free(self, *tensors):
        for t in tensors:
            off, size, bufs = self.live.pop(t.name)
            self.log = getattr(self, "log", [])
            self.log.append(("F", t.name, off, size))
            self.hist.append((off, size, bufs))
            self.free_iv.append((off, off + size))
        self._merge()

    def _merge(self):
        self.free_iv.sort()
        merged = []
        for s, e in self.free_iv:
            if merged and s <= merged[-1][1]:
                merged[-1] = (merged[-1][0], max(merged[-1][1], e))
            else:
                merged.append((s, e))
        self.free_iv = merged

    def lend(self, off, size, seed_bufs):
        self.hist.append((off, size, list(seed_bufs)))
        self.free_iv.append((off, off + size))
        self._merge()

    def reclaim(self, off, size):
        out = []
        niv = []
        for s, e in self.free_iv:
            if s < off + size and off < e:
                assert s <= off and e >= off + size, \
                    f"reclaim range busy: {(off, size)} vs free {(s, e)}"
                if s < off:
                    niv.append((s, off))
                if e > off + size:
                    niv.append((off + size, e))
            else:
                niv.append((s, e))
        self.free_iv = niv
        keep = []
        for (fo, fs, fb) in self.hist:
            if fo < off + size and off < fo + fs:
                out.extend(fb)
            else:
                keep.append((fo, fs, fb))
        self.hist = keep
        return out


class MK:
    def __init__(self, stage=STAGE):
        self.stage = stage
        self.es = ExitStack()
        self.nc = bass.Bass("TRN2", target_bir_lowering=False)
        self.input_names = []
        self.build()

    def psum_phase(self, es2, specs):
        tensors, bufs = {}, {}
        for name, shape in specs.items():
            self._psum_uid += 1
            t = es2.enter_context(self.nc.psum_tensor(f"pp{self._psum_uid}_{name}", list(shape), F32))
            tensors[name] = t
            bufs[name] = Buf(name + str(self._psum_uid), self.psum_prev)
        return tensors, bufs

    def psum_phase_end(self, bufs):
        self.psum_prev = list(bufs.values())

    def din(self, name, shape, dtype):
        self.input_names.append(name)
        return self.nc.dram_tensor(name, list(shape), dtype, kind="ExternalInput").ap()

    def sb(self, name, shape, dtype, nbufs=1):
        t = self.nc.alloc_sbuf_tensor(name, list(shape), dtype)
        return t, [Buf(f"{name}[{i}]") for i in range(nbufs)]

    def build(self):
        nc, es = self.nc, self.es
        r = self.rec = Rec(nc, es)
        st = self.stage

        x_in = self.din("x", [T, D], F32)
        aw_qkv = self.din("aw_qkv", [3072, 1024], F32)
        ab_qkv = self.din("ab_qkv", [3072], F32)
        aw_out = self.din("aw_out", [1024, 1024], F32)
        ab_out = self.din("ab_out", [1024], F32)
        fw_qkv = self.din("fw_qkv", [2, 1024, 1536], F32)
        fqs = self.din("fqs", [2, 64], F32)
        fks = self.din("fks", [2, 64], F32)
        fwo = self.din("fwo", [2, 512, 1024], F32)
        flng = self.din("flng", [2, 1024], F32)
        flnb = self.din("flnb", [2, 1024], F32)
        fw1 = self.din("fw1", [2, 1024, 5460], F32)
        fb1 = self.din("fb1", [2, 5460], F32)
        fw2 = self.din("fw2", [2, 2730, 1024], F32)
        fb2 = self.din("fb2", [2, 1024], F32)
        c_ident = self.din("c_ident", [128, 128], BF16)
        c_identf = self.din("c_identf", [128, 128], F32)
        c_maskf = self.din("c_maskf", [128, 640], BF16)
        c_maskl = self.din("c_maskl", [128, 128], BF16)
        c_csq = self.din("c_csq", [128, NT * 64], BF16)
        c_sxq = self.din("c_sxq", [128, NT * 64], BF16)
        c_csk = self.din("c_csk", [128, NT * 64], BF16)
        c_sxk = self.din("c_sxk", [128, NT * 64], BF16)
        y_out = nc.dram_tensor("y", [T, D], F32, kind="ExternalOutput").ap()
        self._y = y_out
        x2_spill = nc.dram_tensor("x2spill", [T, D], F32, kind="Internal").ap()

        slotA, bufA = self.sb("slotA", [128, NT * D], F32, NT)
        slotB, bufB = self.sb("slotB", [128, NT * D], F32, NT)
        ident, bident = self.sb("ident", [128, 128], BF16)
        identf, bidentf = self.sb("identf", [128, 128], F32)
        maskf, bmaskf = self.sb("maskf", [128, 640], BF16)
        maskl, bmaskl = self.sb("maskl", [128, 128], BF16)
        onesb, bonesb = self.sb("onesb", [1, 128], BF16)
        abqkv_pp, b_abqkv = self.sb("abqkv_pp", [128, 24], F32)
        lg_pp, b_lg = self.sb("lg_pp", [128, 16], F32)
        lb_pp, b_lb = self.sb("lb_pp", [128, 16], F32)
        b1a_pp, b_b1a = self.sb("b1a_pp", [128, 2 * NH], F32)
        b1g_pp, b_b1g = self.sb("b1g_pp", [128, 2 * NH], F32)
        b2_pp, b_b2 = self.sb("b2_pp", [128, 16], F32)
        qsrow, b_qsrow = self.sb("qsrow", [1, 128], F32)
        ksrow, b_ksrow = self.sb("ksrow", [1, 128], F32)
        stat, b_stat = self.sb("stat", [128, 128], F32, 48)
        scr, b_scr = self.sb("scr", [128, 1024], F32)

        PT = es.enter_context(nc.psum_tensor("PsT", [128, 1024], BF16))
        pb = {"PT": Buf("PT")}
        self.pb = pb
        self.P = dict(PT=PT)
        self.psum_prev = []
        self._psum_uid = 0
        self.done = False

        ccol_t, b_ccol = self.sb("ccol", [128, 8], F32)
        self._ccols = {}
        def constcol(val):
            if val not in self._ccols:
                i = len(self._ccols)
                assert i < 8
                ap = ccol_t[:, i:i + 1]
                r.op("vector", lambda e, a=ap, v=float(val): e.memset(a, v),
                     writes=[b_ccol[0]])
                self._ccols[val] = ap
            return self._ccols[val]
        self.constcol = constcol
        self.b_ccol = b_ccol[0]

        def d(q, out_ap, in_ap, wbufs, rbufs=(), slow=False):
            def fn(e, o=out_ap, i=in_ap, s=slow):
                if s:
                    with nc.allow_non_contiguous_dma("small"):
                        return e.dma_start(out=o, in_=i, allow_slow_non_contiguous=True)
                return e.dma_start(out=o, in_=i)
            return r.dma(q, fn, reads=list(rbufs), writes=list(wbufs))

        ROP = r.op

        d("qs", ident[:], c_ident, [bident[0]])
        d("qs", identf[:], c_identf, [bidentf[0]])
        d("qs", maskf[:], c_maskf, [bmaskf[0]])
        d("qs", maskl[:], c_maskl, [bmaskl[0]])
        ROP("vector", lambda e: e.memset(onesb[:], 1.0), writes=[bonesb[0]])
        ROP("vector", lambda e: e.memset(b1a_pp[:], 0.0), writes=[b_b1a[0]])
        ROP("vector", lambda e: e.memset(b1g_pp[:], 0.0), writes=[b_b1g[0]])
        d("qp", abqkv_pp[:], ab_qkv.rearrange("(c p) -> p c", p=128), [b_abqkv[0]], slow=True)
        for l in range(2):
            d("qp", lg_pp[:, l * 8:(l + 1) * 8], flng[l].rearrange("(c p) -> p c", p=128), [b_lg[0]], slow=True)
            d("qp", lb_pp[:, l * 8:(l + 1) * 8], flnb[l].rearrange("(c p) -> p c", p=128), [b_lb[0]], slow=True)
            d("qp", b2_pp[:, l * 8:(l + 1) * 8], fb2[l].rearrange("(c p) -> p c", p=128), [b_b2[0]], slow=True)
            d("qp", b1a_pp[:, l * NH:l * NH + 21], fb1[l, 0:2688].rearrange("(c p) -> p c", p=128), [b_b1a[0]], slow=True)
            d("qp", b1a_pp[0:42, l * NH + 21:l * NH + 22], fb1[l, 2688:2730].unsqueeze(1), [b_b1a[0]], slow=True)
            d("qp", b1g_pp[:, l * NH:l * NH + 21], fb1[l, 2730:5418].rearrange("(c p) -> p c", p=128), [b_b1g[0]], slow=True)
            d("qp", b1g_pp[0:42, l * NH + 21:l * NH + 22], fb1[l, 5418:5460].unsqueeze(1), [b_b1g[0]], slow=True)
            d("qs", qsrow[:, l * 64:(l + 1) * 64], fqs[l].unsqueeze(0), [b_qsrow[0]])
            d("qs", ksrow[:, l * 64:(l + 1) * 64], fks[l].unsqueeze(0), [b_ksrow[0]])

        def ln_full_tile(src_ap, src_bufs, dst_ap, dst_bufs, tslot, eps=1e-5):
            sc = stat[:, tslot * 4 + 0:tslot * 4 + 1]
            qc = stat[:, tslot * 4 + 1:tslot * 4 + 2]
            nm = stat[:, tslot * 4 + 2:tslot * 4 + 3]
            rs = stat[:, tslot * 4 + 3:tslot * 4 + 4]
            sb_ = b_stat[tslot]
            ROP("scalar", lambda e: e.activation(scr[:], src_ap, AF.Square),
                reads=src_bufs, writes=[b_scr[0]])
            ROP("vector", lambda e: e.reduce_sum(qc, scr[:], axis=AX.X),
                reads=[b_scr[0]], writes=[sb_])
            ROP("vector", lambda e: e.reduce_sum(sc, src_ap, axis=AX.X),
                reads=src_bufs, writes=[sb_])
            ROP("vector", lambda e: e.tensor_scalar(nm, sc, -1.0 / D, None, ALU.mult),
                reads=[sb_], writes=[sb_])
            ROP("vector", lambda e: e.tensor_mul(rs, nm, nm), reads=[sb_], writes=[sb_])
            ROP("vector", lambda e: e.scalar_tensor_tensor(qc, qc, 1.0 / D, rs, ALU.mult, ALU.subtract),
                reads=[sb_], writes=[sb_])
            epsc = self.constcol(eps)
            nhalf = self.constcol(-0.5)
            ROP("scalar", lambda e: e.activation(sc, qc, AF.Ln, bias=epsc),
                reads=[sb_, b_ccol[0]], writes=[sb_])
            ROP("scalar", lambda e: e.activation(rs, sc, AF.Exp, scale=nhalf),
                reads=[sb_, b_ccol[0]], writes=[sb_])
            ROP("vector", lambda e: e.tensor_scalar(dst_ap, src_ap, nm, rs, ALU.add, ALU.mult),
                reads=list(src_bufs) + [sb_], writes=dst_bufs)

        self.arena = Arena(nc, nc.sbuf_base, nc.sbuf_top)
        xio, b_xio = self.arena.alloc("xio0", [128, 3, D], F32, 3)
        for t in range(NT):
            xb = b_xio[t % 3]
            xap = xio[:, t % 3, :]
            d("qs", xap, x_in[t * 128:(t + 1) * 128, :], [xb])
            ln_full_tile(xap, [xb], slotA[:, t * D:(t + 1) * D], [bufA[t]], t % 16)
        self.arena.free(xio)

        self.ctx = dict(locals())

        if st == "ln1":
            self.emit_out(slotA, bufA)
            r.finalize_and_emit()
            return

        if st in ("lin_dbg", "linq_dbg", "lins_dbg", "lins_dbg2", "lins_dbg3"):
            self.linattn(slotA, bufA, slotB, bufB)
            if st in ("linq_dbg", "lins_dbg", "lins_dbg2", "lins_dbg3"):
                self.emit_out(slotA, bufA)
            else:
                self.emit_out(slotB, bufB)
            r.finalize_and_emit()
            return

        self.ffn_call(slotA, bufA, call=0)
        if self.done:
            r.finalize_and_emit()
            return
        if st == "ffn1":
            self.emit_out(slotA, bufA)
            r.finalize_and_emit()
            return

        xio, b_xio = self.arena.alloc("xio1", [128, 3, D], F32, 3)
        for t in range(NT):
            xb = b_xio[t % 3]
            xap = xio[:, t % 3, :]
            d("qs", xap, x_in[t * 128:(t + 1) * 128, :], [xb])
            ROP("vector", lambda e, t=t, xap=xap: e.scalar_tensor_tensor(
                slotB[:, t * D:(t + 1) * D], slotA[:, t * D:(t + 1) * D], 0.5, xap,
                ALU.mult, ALU.add),
                reads=[bufA[t], xb], writes=[bufB[t]])
        self.arena.free(xio)
        if st == "res1":
            self.emit_out(slotB, bufB)
            r.finalize_and_emit()
            return

        self.linattn(slotB, bufB, slotA, bufA)
        if st == "lin":
            self.emit_out(slotA, bufA)
            r.finalize_and_emit()
            return

        for t in range(NT):
            d("qs", x2_spill[t * 128:(t + 1) * 128, :], slotA[:, t * D:(t + 1) * D],
              [Buf(f"x2sp{t}")], rbufs=[bufA[t]])
            ln_full_tile(slotA[:, t * D:(t + 1) * D], [bufA[t]],
                         slotB[:, t * D:(t + 1) * D], [bufB[t]], t % 16)

        self.ffn_call(slotB, bufB, call=1)

        xio, b_xio = self.arena.alloc("xio2", [128, 3, D], F32, 3)
        for t in range(NT):
            xb = b_xio[t % 3]
            xap = xio[:, t % 3, :]
            d("qs", xap, x2_spill[t * 128:(t + 1) * 128, :], [xb])
            ROP("vector", lambda e, t=t, xap=xap: e.scalar_tensor_tensor(
                slotA[:, t * D:(t + 1) * D], slotB[:, t * D:(t + 1) * D], 0.5, xap,
                ALU.mult, ALU.add),
                reads=[bufB[t], xb], writes=[bufA[t]])
        self.emit_out(slotA, bufA)
        r.finalize_and_emit()

    def emit_out(self, slot, bufs):
        r = self.rec
        y = self._y
        for t in range(NT):
            def fn(e, t=t):
                return e.dma_start(out=y[t * 128:(t + 1) * 128, :], in_=slot[:, t * D:(t + 1) * D])
            r.dma("qs", fn, reads=[bufs[t]], writes=[Buf(f"yo{t}")])

    def ffn_call(self, slot, bufs, call):
        free_slot = self.ctx["slotB"] if call == 0 else self.ctx["slotA"]
        free_bufs = self.ctx["bufB"] if call == 0 else self.ctx["bufA"]
        off = self.nc.lookup_mloc(free_slot).addr
        size = NT * D * 4
        self.arena.lend(off, size, free_bufs)
        for l in range(2):
            self.ffn_layer(slot, bufs, l, call)
            if self.done:
                return
        parents = self.arena.reclaim(off, size)
        for t in range(NT):
            free_bufs[t] = Buf(f"reslot{call}_{t}", parents)

    def ffn_layer(self, slot, bufs, l, call):
        c = self.ctx
        nc, r, ar = self.nc, self.rec, self.arena
        ROP = r.op
        d = c["d"]
        PT = self.P["PT"]
        pb = self.pb
        ident = c["ident"]; bident = c["bident"][0]
        maskf = c["maskf"]; bmaskf = c["bmaskf"][0]
        onesb = c["onesb"]; bonesb = c["bonesb"][0]
        fw_qkv, fwo, fw1, fb1, fw2 = c["fw_qkv"], c["fwo"], c["fw1"], c["fb1"], c["fw2"]
        fqs, fks = c["fqs"], c["fks"]
        ccsq, csxq, ccsk, csxk = c["c_csq"], c["c_sxq"], c["c_csk"], c["c_sxk"]
        lg_pp, lb_pp = c["lg_pp"], c["lb_pp"]
        b_lg, b_lb = c["b_lg"][0], c["b_lb"][0]
        b1a_pp, b1g_pp, b2_pp = c["b1a_pp"], c["b1g_pp"], c["b2_pp"]
        b_b1a, b_b1g, b_b2 = c["b_b1a"][0], c["b_b1g"][0], c["b_b2"][0]
        stat = c["stat"]; b_stat = c["b_stat"]
        constcol = self.constcol
        bcc = self.b_ccol

        wqkv_sb, b_wqkv = ar.alloc("wqkv", [128, NF, 1536], BF16)
        wo_sb, b_wo = ar.alloc("wo", [128, 4, 1024], BF16)
        cs_sb, b_cs = ar.alloc("cstag", [128, NT * 64], BF16)
        cq_sb, b_cq = ar.alloc("cqtab", [128, 4, NT * 64], BF16)
        qsb_sb, b_qsb = ar.alloc("qsb", [128, 4, 64], BF16)
        qsrow_bf, b_qsrowbf = ar.alloc("qsrowbf", [1, 128], BF16)
        xs_sb, b_xs = ar.alloc("xs", [128, 2, NF, 129], BF16, 2)
        cast_sb, b_cast = ar.alloc("cast", [128, 2, D], BF16, 2)
        qf_sb, b_qf = ar.alloc("qf", [128, 2, 512], BF16, 2)
        kf_sb, b_kf = ar.alloc("kf", [128, 6, 512], BF16, 6)
        v1_sb, b_v1 = ar.alloc("v1", [128, 6, FH * 65], BF16, 6)
        exp_sb, b_exp = ar.alloc("exps", [128, 2, 640], BF16, 2)
        at_sb, b_at = ar.alloc("atT", [128, 2, 640], BF16, 2)
        for _i in range(2):
            ROP("vector", lambda e, _i=_i: e.memset(exp_sb[:, _i, :], 0.0), writes=[b_exp[_i]])
            ROP("vector", lambda e, _i=_i: e.memset(at_sb[:, _i, :], 0.0), writes=[b_at[_i]])
        otm_sb, b_otm = ar.alloc("otm", [128, 2, 512], BF16, 2)
        ofm_sb, b_ofm = ar.alloc("ofm", [128, 2, 512], BF16, 2)
        qn_sb, b_qn = ar.alloc("qn", [128, 2, 512], F32, 2)
        c1_sb, b_c1 = ar.alloc("c1", [128, 2, 512], F32, 2)
        c2_sb, b_c2 = ar.alloc("c2", [128, 2, 512], F32, 2)
        sqs_sb, b_sqs = c["scr"], c["b_scr"]
        rn_sb, b_rn = ar.alloc("rn", [128, 4, 32], F32, 4)
        for _i in range(4):
            ROP("vector", lambda e, _i=_i: e.memset(rn_sb[:, _i, :], 0.0), writes=[b_rn[_i]])
        qpb_sb, b_qpb = ar.alloc("qpb", [128, 512], BF16)
        kpb_sb, b_kpb = ar.alloc("kpb", [128, 512], BF16)
        st2_sb, b_st2g = ar.alloc("st2", [128, 64], F32)
        self._st2, self._b_st2 = st2_sb, b_st2g

        for f in range(NF):
            d("qp", wqkv_sb[:, f, :], fw_qkv[l, f * 128:(f + 1) * 128, :], [b_wqkv[0]])
        for ci in range(4):
            d("qp", wo_sb[:, ci, :], fwo[l, ci * 128:(ci + 1) * 128, :], [b_wo[0]])
        d("qp", qsrow_bf[:, 0:64], fqs[l].unsqueeze(0), [b_qsrowbf[0]])
        d("qp", qsrow_bf[:, 64:128], fks[l].unsqueeze(0), [b_qsrowbf[0]])
        pes = ExitStack()
        PS, PB = self.psum_phase(pes, {"sA": [128, 768], "sB": [128, 768],
                                       "pq": [128, 512], "pk": [128, 512],
                                       "pv": [128, 512]})
        ROP("tensor", lambda e: e.matmul(PS["pq"][:, 0:64], onesb[:], qsrow_bf[:, 0:64], start=True, stop=True),
            reads=[bonesb, b_qsrowbf[0]], writes=[PB["pq"]])
        ROP("tensor", lambda e: e.matmul(PS["pq"][:, 64:128], onesb[:], qsrow_bf[:, 64:128], start=True, stop=True),
            reads=[bonesb, b_qsrowbf[0]], writes=[PB["pq"]])
        ROP("vector", lambda e: e.tensor_copy(qsb_sb[:, 0, :], PS["pq"][:, 0:64]),
            reads=[PB["pq"]], writes=[b_qsb[0]])
        ROP("vector", lambda e: e.tensor_copy(qsb_sb[:, 2, :], PS["pq"][:, 64:128]),
            reads=[PB["pq"]], writes=[b_qsb[0]])
        ROP("vector", lambda e: e.tensor_copy(qsb_sb[:, 1, 0:32], qsb_sb[:, 0, 32:64]),
            reads=[b_qsb[0]], writes=[b_qsb[0]])
        ROP("vector", lambda e: e.tensor_copy(qsb_sb[:, 1, 32:64], qsb_sb[:, 0, 0:32]),
            reads=[b_qsb[0]], writes=[b_qsb[0]])
        ROP("vector", lambda e: e.tensor_copy(qsb_sb[:, 3, 0:32], qsb_sb[:, 2, 32:64]),
            reads=[b_qsb[0]], writes=[b_qsb[0]])
        ROP("vector", lambda e: e.tensor_copy(qsb_sb[:, 3, 32:64], qsb_sb[:, 2, 0:32]),
            reads=[b_qsb[0]], writes=[b_qsb[0]])
        for i, cc in enumerate((ccsq, csxq, ccsk, csxk)):
            d("qs", cs_sb[:], cc, [b_cs[0]])
            def mkt(e, i=i):
                tv = cs_sb[:].rearrange("p (t dd) -> p t dd", dd=64)
                qv = qsb_sb[:, i, :].unsqueeze(1).broadcast_to([128, NT, 64])
                ov = cq_sb[:, i, :].rearrange("p (t dd) -> p t dd", dd=64)
                return e.tensor_tensor(ov, tv, qv, ALU.mult)
            ROP("vector", mkt, reads=[b_cs[0], b_qsb[0]], writes=[b_cq[0]])
        for s in range(6):
            ROP("vector", lambda e, s=s: e.memset(
                v1_sb[:, s, :].rearrange("p (h u) -> p h u", u=65)[:, :, 64:65], 1.0),
                writes=[b_v1[s]])
        ROP("vector", lambda e: e.memset(xs_sb[:, 0, :, 0:1], 0.0), writes=[b_xs[0]])

        sc_reg = [PS["sA"][:, 0:640], PS["sB"][:, 0:640]]
        sc_buf = [PB["sA"], PB["sB"]]
        qkq_reg, qkq_buf = PS["pq"][:, 0:512], PB["pq"]
        qkk_reg, qkk_buf = PS["pk"][:, 0:512], PB["pk"]
        qkv_reg, qkv_buf = PS["pv"][:, 0:512], PB["pv"]
        av_reg = [PS["sA"][:, 640:705], PS["sB"][:, 640:705]]
        av_buf = [PB["sA"], PB["sB"]]
        ptbuf = pb["PT"]

        ktiles = lambda t: list(range(max(0, t - 4), t + 1))

        ntiles = 2 if self.stage == "ffn_dbg" else NT
        def stage_load(t):
            ts = t % 2
            cast_ap = cast_sb[:, ts, :]
            ROP("scalar", lambda e, t=t, a=cast_ap: e.copy(a, slot[:, t * D:(t + 1) * D]),
                reads=[bufs[t]], writes=[b_cast[ts]])
            for fc in range(NF):
                ROP("tensor", lambda e, fc=fc, a=cast_ap: e.transpose(
                    PT[:, fc * 128:(fc + 1) * 128], a[:, fc * 128:(fc + 1) * 128], ident[:]),
                    reads=[b_cast[ts], bident], writes=[ptbuf])
            ROP("vector", lambda e, ts=ts: e.tensor_copy(
                xs_sb[:, ts, :, 1:129], PT[:].rearrange("p (c u) -> p c u", u=128)),
                reads=[ptbuf], writes=[b_xs[ts]])
            if t < NT - 1:
                ROP("vector", lambda e, ts=ts: e.tensor_copy(
                    xs_sb[:, 1 - ts, :, 0:1], xs_sb[:, ts, :, 128:129]),
                    reads=[b_xs[ts]], writes=[b_xs[1 - ts]])
            def xs_view(fc, ts=ts):
                off = 1 if fc < 4 else 0
                return xs_sb[:, ts, fc, off:off + 128]
            for gi, (reg, rbuf) in enumerate(((qkq_reg, qkq_buf), (qkk_reg, qkk_buf), (qkv_reg, qkv_buf))):
                for fc in range(NF):
                    ROP("tensor", lambda e, gi=gi, fc=fc, reg=reg, ts=ts, xv=xs_view: e.matmul(
                        reg, xv(fc, ts), wqkv_sb[:, fc, gi * 512:(gi + 1) * 512],
                        start=(fc == 0), stop=(fc == NF - 1)),
                        reads=[b_xs[ts], b_wqkv[0]], writes=[rbuf])
            vs = t % 6
            ROP("vector", lambda e, vs=vs, qr=qkv_reg: e.tensor_copy(
                v1_sb[:, vs, :].rearrange("p (h u) -> p h u", u=65)[:, :, 0:64],
                qr.rearrange("p (h u) -> p h u", u=64)),
                reads=[qkv_buf], writes=[b_v1[vs]])

        def stage_prep(t):
            ts = t % 2
            tcol = t % 4
            rnb = b_rn[tcol]
            rss = rn_sb[:, tcol, :]
            streams = [
                (qkq_reg, qkq_buf, 0, 1, qf_sb[:, ts, :], b_qf[ts]),
                (qkk_reg, qkk_buf, 2, 3, kf_sb[:, t % 6, :], b_kf[t % 6]),
            ]
            for qi, (reg, rbuf, tabc, tabs, dstf, dstfb) in enumerate(streams):
                ROP("scalar", lambda e, reg=reg, qi=qi: e.activation(
                    sqs_sb[:, qi * 512:qi * 512 + 512], reg, AF.Square),
                    reads=[rbuf], writes=[b_sqs[0]])
            for qi, (reg, rbuf, tabc, tabs, dstf, dstfb) in enumerate(streams):
                sqv = sqs_sb[:, qi * 512:qi * 512 + 512].rearrange("p (h u) -> p h u", u=64)
                ROP("vector", lambda e, qi=qi, sqv=sqv, rss=rss: e.reduce_sum(
                    rss[:, qi * 8:qi * 8 + 8].unsqueeze(2), sqv, axis=AX.X),
                    reads=[b_sqs[0]], writes=[rnb])
            tiny = constcol(1e-24)
            nhalf = constcol(-0.5)
            ROP("scalar", lambda e, rss=rss: e.activation(rss[:, 16:32], rss[:, 0:16], AF.Ln, bias=tiny),
                reads=[rnb, bcc], writes=[rnb])
            ROP("scalar", lambda e, rss=rss: e.activation(rss[:, 0:16], rss[:, 16:32], AF.Exp, scale=nhalf),
                reads=[rnb, bcc], writes=[rnb])
            for qi, (reg, rbuf, tabc, tabs, dstf, dstfb) in enumerate(streams):
                regv = reg.rearrange("p (h u) -> p h u", u=64)
                qnv = qn_sb[:, qi, :].rearrange("p (h u) -> p h u", u=64)
                rnv = rss[:, qi * 8:qi * 8 + 8].unsqueeze(2).broadcast_to([128, 8, 64])
                ROP("vector", lambda e, regv=regv, rnv=rnv, qnv=qnv: e.tensor_tensor(
                    qnv, regv, rnv, ALU.mult), reads=[rbuf, rnb], writes=[b_qn[qi]])
            for qi, (reg, rbuf, tabc, tabs, dstf, dstfb) in enumerate(streams):
                qnv = qn_sb[:, qi, :].rearrange("p (h u) -> p h u", u=64)
                c1v = c1_sb[:, qi, :].rearrange("p (h u) -> p h u", u=64)
                tc = cq_sb[:, tabc, t * 64:(t + 1) * 64].unsqueeze(1).broadcast_to([128, 8, 64])
                ROP("gpsimd", lambda e, qnv=qnv, c1v=c1v, tc=tc: e.tensor_tensor(
                    c1v, qnv, tc, ALU.mult), reads=[b_qn[qi], b_cq[0]], writes=[b_c1[qi]])
            for qi, (reg, rbuf, tabc, tabs, dstf, dstfb) in enumerate(streams):
                qnv = qn_sb[:, qi, :].rearrange("p (h u) -> p h u", u=64)
                c2v = c2_sb[:, qi, :].rearrange("p (h u) -> p h u", u=64)
                qsw0 = qnv[:, :, 32:64]
                qsw1 = qnv[:, :, 0:32]
                ts0 = cq_sb[:, tabs, t * 64:t * 64 + 32].unsqueeze(1).broadcast_to([128, 8, 32])
                ts1 = cq_sb[:, tabs, t * 64 + 32:t * 64 + 64].unsqueeze(1).broadcast_to([128, 8, 32])
                ROP("vector", lambda e, qsw0=qsw0, c2v=c2v, ts0=ts0: e.tensor_tensor(
                    c2v[:, :, 0:32], qsw0, ts0, ALU.mult), reads=[b_qn[qi], b_cq[0]], writes=[b_c2[qi]])
                ROP("vector", lambda e, qsw1=qsw1, c2v=c2v, ts1=ts1: e.tensor_tensor(
                    c2v[:, :, 32:64], qsw1, ts1, ALU.mult), reads=[b_qn[qi], b_cq[0]], writes=[b_c2[qi]])
            for qi, (reg, rbuf, tabc, tabs, dstf, dstfb) in enumerate(streams):
                pbf = (qpb_sb, kpb_sb)[qi]
                pbfb = (b_qpb, b_kpb)[qi][0]
                ROP("vector", lambda e, pbf=pbf, qi=qi: e.tensor_add(
                    pbf[:], c1_sb[:, qi, :], c2_sb[:, qi, :]),
                    reads=[b_c1[qi], b_c2[qi]], writes=[pbfb])
                for ci2 in range(4):
                    ROP("tensor", lambda e, ci2=ci2, qi=qi, pbf=pbf: e.transpose(
                        PT[:, qi * 512 + ci2 * 128: qi * 512 + (ci2 + 1) * 128],
                        pbf[:, ci2 * 128:(ci2 + 1) * 128], ident[:]),
                        reads=[pbfb, bident], writes=[ptbuf])
                ROP("vector", lambda e, qi=qi, dstf=dstf: e.tensor_copy(
                    dstf, PT[:, qi * 512:(qi + 1) * 512]), reads=[ptbuf], writes=[dstfb])


        def stage_attn(t):
            ts = t % 2
            kts = ktiles(t)
            m = len(kts)

            def head_front(h):
                hreg = h % 2
                sreg, sbuf_ = sc_reg[hreg], sc_buf[hreg]
                hp = (h % 2) * 64
                hc = h // 2
                acol = 64 + (t % 4) * 8 + (h % 4) * 2
                amax = stat[:, acol:acol + 1]
                bamax = b_stat[16 + (t % 4) * 4 + h % 4]
                for ci2, kt in enumerate(kts):
                    ROP("tensor", lambda e, ci2=ci2, kt=kt, sreg=sreg, hp=hp, hc=hc, ts=ts: e.matmul(
                        sreg[:, ci2 * 128:(ci2 + 1) * 128],
                        qf_sb[hp:hp + 64, ts, hc * 128:(hc + 1) * 128],
                        kf_sb[hp:hp + 64, kt % 6, hc * 128:(hc + 1) * 128],
                        start=True, stop=True),
                        reads=[b_qf[ts], b_kf[kt % 6]], writes=[sbuf_])
                ROP("vector", lambda e, sreg=sreg, m=m, amax=amax: e.reduce_max(
                    amax, sreg[:, 0:m * 128], axis=AX.X, negate=True),
                    reads=[sbuf_], writes=[bamax])
                es_ = h % 2
                ROP("scalar", lambda e, sreg=sreg, m=m, es_=es_, amax=amax: e.activation(
                    exp_sb[:, es_, 0:m * 128], sreg[:, 0:m * 128], AF.Exp, bias=amax),
                    reads=[sbuf_, bamax], writes=[b_exp[es_]])

            def head_back(h):
                hreg = h % 2
                sreg, sbuf_ = sc_reg[hreg], sc_buf[hreg]
                es_ = h % 2
                acol = 64 + (t % 4) * 8 + (h % 4) * 2
                rcol = stat[:, acol + 1:acol + 2]
                bamax = b_stat[16 + (t % 4) * 4 + h % 4]
                for ci2 in range(m):
                    ROP("tensor", lambda e, ci2=ci2, es_=es_: e.transpose(
                        PT[:, ci2 * 128:(ci2 + 1) * 128],
                        exp_sb[:, es_, ci2 * 128:(ci2 + 1) * 128], ident[:]),
                        reads=[b_exp[es_], bident], writes=[ptbuf])
                ROP("vector", lambda e, m=m, es_=es_: e.tensor_tensor(
                    at_sb[:, es_, 0:m * 128], PT[:, 0:m * 128],
                    maskf[:, (5 - m) * 128:640], ALU.mult),
                    reads=[ptbuf, bmaskf], writes=[b_at[es_]])
                areg, abuf_ = av_reg[hreg], av_buf[hreg]
                for ci2, kt in enumerate(kts):
                    ROP("tensor", lambda e, ci2=ci2, kt=kt, areg=areg, h=h, es_=es_, m=m: e.matmul(
                        areg, at_sb[:, es_, ci2 * 128:(ci2 + 1) * 128],
                        v1_sb[:, kt % 6, h * 65:(h + 1) * 65],
                        start=(ci2 == 0), stop=(ci2 == m - 1)),
                        reads=[b_at[es_], b_v1[kt % 6]], writes=[abuf_])
                ROP("vector", lambda e, areg=areg, rcol=rcol: e.reciprocal(rcol, areg[:, 64:65]),
                    reads=[abuf_], writes=[bamax])
                ROP("scalar", lambda e, areg=areg, h=h, ts=ts, rcol=rcol: e.activation(
                    otm_sb[:, ts, h * 64:(h + 1) * 64], areg[:, 0:64], AF.Copy, scale=rcol),
                    reads=[abuf_, bamax], writes=[b_otm[ts]])

            for h in range(FH):
                head_front(h)
                if h >= 1:
                    head_back(h - 1)
            head_back(FH - 1)
            for ci2 in range(4):
                ROP("tensor", lambda e, ci2=ci2, ts=ts: e.transpose(
                    PT[:, ci2 * 128:(ci2 + 1) * 128], otm_sb[:, ts, ci2 * 128:(ci2 + 1) * 128],
                    ident[:]), reads=[b_otm[ts], bident], writes=[ptbuf])
            ROP("vector", lambda e, ts=ts: e.tensor_copy(ofm_sb[:, ts, :], PT[:, 0:512]),
                reads=[ptbuf], writes=[b_ofm[ts]])
            for oh in range(2):
                wreg, wbuf_ = ((sc_reg[0][:, 0:512], sc_buf[0]), (sc_reg[1][:, 0:512], sc_buf[1]))[oh]
                for ci2 in range(4):
                    ROP("tensor", lambda e, ci2=ci2, oh=oh, wreg=wreg, ts=ts: e.matmul(
                        wreg, ofm_sb[:, ts, ci2 * 128:(ci2 + 1) * 128],
                        wo_sb[:, ci2, oh * 512:(oh + 1) * 512],
                        start=(ci2 == 0), stop=(ci2 == 3)),
                        reads=[b_ofm[ts], b_wo[0]], writes=[wbuf_])
                ROP("vector", lambda e, oh=oh, t=t, wreg=wreg: e.tensor_tensor(
                    slot[:, t * D + oh * 512: t * D + (oh + 1) * 512], wreg,
                    slot[:, t * D + oh * 512: t * D + (oh + 1) * 512], ALU.add),
                    reads=[wbuf_, bufs[t]], writes=[bufs[t]])
            ROP("scalar", lambda e, t=t: e.activation(
                sqs_sb[:, 0:512], slot[:, t * D:t * D + 512], AF.Square,
                accum_out=st2_sb[:, 16 + t:17 + t]),
                reads=[bufs[t]], writes=[b_sqs[0], b_st2g[0]])
            ROP("scalar", lambda e, t=t: e.activation(
                sqs_sb[:, 512:1024], slot[:, t * D + 512:(t + 1) * D], AF.Square,
                accum_out=st2_sb[:, 48 + t:49 + t]),
                reads=[bufs[t]], writes=[b_sqs[0], b_st2g[0]])
            ROP("vector", lambda e, t=t: e.reduce_sum(
                st2_sb[:, t:t + 1], slot[:, t * D:t * D + 512], axis=AX.X),
                reads=[bufs[t]], writes=[b_st2g[0]])
            ROP("vector", lambda e, t=t: e.reduce_sum(
                st2_sb[:, 32 + t:33 + t], slot[:, t * D + 512:(t + 1) * D], axis=AX.X),
                reads=[bufs[t]], writes=[b_st2g[0]])


        for t in range(ntiles + 1):
            if t < ntiles:
                stage_load(t)
            if t >= 1:
                stage_attn(t - 1)
            if t < ntiles:
                stage_prep(t)

        if self.stage == "ffn_dbg" and l == 0 and call == 0:
            dbg = {"dbg_cast1": (cast_sb[:, 1, :], b_cast[1]),
                   "dbg_qf1": (qf_sb[:, 1, :], b_qf[1]),
                   "dbg_kf0": (kf_sb[:, 0, :], b_kf[0]),
                   "dbg_kf1": (kf_sb[:, 1, :], b_kf[1]),
                   "dbg_v10": (v1_sb[:, 0, :], b_v1[0]),
                   "dbg_v11": (v1_sb[:, 1, :], b_v1[1]),
                   "dbg_otm1": (otm_sb[:, 1, :], b_otm[1]),
                   "dbg_exp1": (exp_sb[:, 1, :], b_exp[1]),
                   "dbg_at1": (at_sb[:, 1, :], b_at[1]),
                   "dbg_xs1": (xs_sb[:, 1, :, :].rearrange("p a b -> p (a b)"), b_xs[1]),
                   "dbg_cq": (cq_sb[:, 0, :], b_cq[0]),
                   }
            for nm, (ap, bb) in dbg.items():
                o = self.nc.dram_tensor(nm, [128, ap.shape[-1] if ap.ndim == 2 else int(np.prod(ap.shape[1:]))],
                                        ap.dtype, kind="ExternalOutput").ap()
                r.dma("qs", (lambda e, o=o, ap=ap: e.dma_start(out=o, in_=ap)),
                      reads=[bb], writes=[Buf("dbo_" + nm)])
            self.emit_out(slot, bufs)
            self.done = True
            return

        ar.free(wqkv_sb, wo_sb, cs_sb, qsb_sb, qsrow_bf, xs_sb, cast_sb, qn_sb,
                c1_sb, c2_sb, qpb_sb, kpb_sb)
        ar.free(cq_sb, qf_sb, kf_sb, v1_sb, exp_sb, at_sb, otm_sb, ofm_sb, rn_sb)
        pes.close()
        self.psum_phase_end(PB)

        if self.stage == f"ffn_a{l}" and call == 0 and l == 1:
            self.emit_out(slot, bufs)
            self.done = True
            return
        if self.stage == f"ffn_a{l}" and call == 0:
            self.emit_out(slot, bufs)
            self.done = True
            return

        self.ffn_mlp(slot, bufs, l, call)
        if self.stage == f"ffn_l{l}" and call == 0:
            self.emit_out(slot, bufs)
            self.done = True
            return

    def ffn_mlp(self, slot, bufs, l, call):
        c = self.ctx
        nc, r, ar = self.nc, self.rec, self.arena
        ROP = r.op
        d = c["d"]
        PT = self.P["PT"]
        pb = self.pb
        ident = c["ident"]; bident = c["bident"][0]
        identf = c["identf"]; bidentf = c["bidentf"][0]
        fw1, fb1, fw2 = c["fw1"], c["fb1"], c["fw2"]
        lg_pp, lb_pp = c["lg_pp"], c["lb_pp"]
        b_lg, b_lb = c["b_lg"][0], c["b_lb"][0]
        b1a_pp, b1g_pp, b2_pp = c["b1a_pp"], c["b1g_pp"], c["b2_pp"]
        b_b1a, b_b1g, b_b2 = c["b_b1a"][0], c["b_b1g"][0], c["b_b2"][0]
        scr = c["scr"]; b_scr = c["b_scr"][0]
        constcol = self.constcol
        bcc = self.b_ccol

        st2, b_st2 = self._st2, self._b_st2
        sh2, b_sh2 = ar.alloc("sh2", [128, 32], F32)
        cm_sb, b_cm = ar.alloc("cm", [128, 32], F32)
        mr_sb, b_mr = ar.alloc("mr", [128, 64], F32)
        ROP("vector", lambda e: e.memset(mr_sb[:], 0.0), writes=[b_mr[0]])
        nrm_sb, b_nrm = ar.alloc("nrm", [128, 2, D], BF16, 2)
        ln2_sb, b_ln2 = ar.alloc("ln2", [128, NF, 1025], BF16)
        bnd_sb, b_bnd = ar.alloc("bnd", [128, NF, 1], BF16)
        h2_sb, b_h2 = ar.alloc("h2", [128, NH, 1024], BF16)
        w1b_sb, b_w1b = ar.alloc("w1b", [128, 2, NF, 256], BF16, 2)
        abf_sb, b_abf = ar.alloc("abf", [128, 2, 512], BF16, 2)
        gbf_sb, b_gbf = ar.alloc("gbf", [128, 2, 512], BF16, 2)
        wof_sb, b_wof = ar.alloc("wof", [128, NH, 128], BF16)
        w2t_sb, b_w2t = ar.alloc("w2t", [128, 1024], F32)

        ROP("vector", lambda e: e.memset(sh2[0:1, :], 0.0), writes=[b_sh2[0]])
        d("qp", sh2[1:128, 0:16], st2[0:127, 32:48], [b_sh2[0]], rbufs=[b_st2[0]])
        d("qp", sh2[1:128, 16:32], st2[0:127, 48:64], [b_sh2[0]], rbufs=[b_st2[0]])
        d("qp", sh2[0:1, 1:16], st2[127:128, 32:47], [b_sh2[0]], rbufs=[b_st2[0]])
        d("qp", sh2[0:1, 17:32], st2[127:128, 48:63], [b_sh2[0]], rbufs=[b_st2[0]])
        ROP("vector", lambda e: e.tensor_add(cm_sb[:], st2[:, 0:32], sh2[:]),
            reads=[b_st2[0], b_sh2[0]], writes=[b_cm[0]])
        ROP("vector", lambda e: e.tensor_scalar(mr_sb[:, 0:16], cm_sb[:, 0:16], -1.0 / D, None, ALU.mult),
            reads=[b_cm[0]], writes=[b_mr[0]])
        ROP("vector", lambda e: e.tensor_mul(mr_sb[:, 16:32], mr_sb[:, 0:16], mr_sb[:, 0:16]),
            reads=[b_mr[0]], writes=[b_mr[0]])
        ROP("vector", lambda e: e.scalar_tensor_tensor(
            mr_sb[:, 32:48], cm_sb[:, 16:32], 1.0 / D, mr_sb[:, 16:32], ALU.mult, ALU.subtract),
            reads=[b_cm[0], b_mr[0]], writes=[b_mr[0]])
        eps5 = constcol(1e-5)
        nhalf = constcol(-0.5)
        ROP("scalar", lambda e: e.activation(mr_sb[:, 48:64], mr_sb[:, 32:48], AF.Ln, bias=eps5),
            reads=[b_mr[0], bcc], writes=[b_mr[0]])
        ROP("scalar", lambda e: e.activation(mr_sb[:, 16:32], mr_sb[:, 48:64], AF.Exp, scale=nhalf),
            reads=[b_mr[0], bcc], writes=[b_mr[0]])
        d("qp", mr_sb[0:127, 32:64], mr_sb[1:128, 0:32], [b_mr[0]], rbufs=[b_mr[0]])
        d("qp", mr_sb[127:128, 32:47], mr_sb[0:1, 1:16], [b_mr[0]], rbufs=[b_mr[0]])
        d("qp", mr_sb[127:128, 48:63], mr_sb[0:1, 17:32], [b_mr[0]], rbufs=[b_mr[0]])


        pes = ExitStack()
        PS, PB = self.psum_phase(pes, {"pa0": [128, 512], "pa1": [128, 512],
                                       "pg0": [128, 512], "pg1": [128, 512]})
        a_reg = [(PS["pa0"][:], PB["pa0"]), (PS["pa1"][:], PB["pa1"])]
        g_reg = [(PS["pg0"][:], PB["pg0"]), (PS["pg1"][:], PB["pg1"])]
        tb_reg = [(PS["pg0"][:, 0:128], PB["pg0"]), (PS["pg1"][:, 0:128], PB["pg1"])]
        ptbuf = pb["PT"]

        for g in range(2):
            if g == 0:
                ROP("vector", lambda e: e.memset(ln2_sb[:, :, 0:1], 0.0), writes=[b_ln2[0]])
            else:
                ROP("vector", lambda e: e.tensor_copy(ln2_sb[:, :, 0:1], bnd_sb[:]),
                    reads=[b_bnd[0]], writes=[b_ln2[0]])
            for tt in range(8):
                t = 8 * g + tt
                ns = tt % 2
                ROP("vector", lambda e, t=t, ns=ns: e.tensor_scalar(
                    nrm_sb[:, ns, 0:512], slot[:, t * D:t * D + 512],
                    mr_sb[:, t:t + 1], mr_sb[:, 16 + t:17 + t], ALU.add, ALU.mult),
                    reads=[bufs[t], b_mr[0]], writes=[b_nrm[ns]])
                ROP("vector", lambda e, t=t, ns=ns: e.tensor_scalar(
                    nrm_sb[:, ns, 512:1024], slot[:, t * D + 512:(t + 1) * D],
                    mr_sb[:, 32 + t:33 + t], mr_sb[:, 48 + t:49 + t], ALU.add, ALU.mult),
                    reads=[bufs[t], b_mr[0]], writes=[b_nrm[ns]])
                for fc in range(NF):
                    ROP("tensor", lambda e, fc=fc, ns=ns: e.transpose(
                        PT[:, fc * 128:(fc + 1) * 128], nrm_sb[:, ns, fc * 128:(fc + 1) * 128],
                        ident[:]), reads=[b_nrm[ns], bident], writes=[ptbuf])
                for fc in range(NF):
                    ROP("vector", lambda e, fc=fc, tt=tt, lc=l * 8 + fc: e.tensor_scalar(
                        ln2_sb[:, fc, 1 + tt * 128:1 + (tt + 1) * 128],
                        PT[:, fc * 128:(fc + 1) * 128],
                        lg_pp[:, lc:lc + 1], lb_pp[:, lc:lc + 1], ALU.mult, ALU.add),
                        reads=[ptbuf, b_lg, b_lb], writes=[b_ln2[0]])
            if g == 0:
                ROP("vector", lambda e: e.tensor_copy(bnd_sb[:], ln2_sb[:, :, 1024:1025]),
                    reads=[b_ln2[0]], writes=[b_bnd[0]])

            for j in range(NH):
                wj = min(128, HID - 128 * j)
                jp = j % 2
                d("qp", w1b_sb[:, jp, :, 0:wj],
                  fw1[l, :, 128 * j:128 * j + wj].rearrange("(a p) h -> p a h", p=128),
                  [b_w1b[jp]])
                d("qp", w1b_sb[:, jp, :, 128:128 + wj],
                  fw1[l, :, HID + 128 * j:HID + 128 * j + wj].rearrange("(a p) h -> p a h", p=128),
                  [b_w1b[jp]])
                for th in range(2):
                    (pa, pab), (pg, pgb) = a_reg[th], g_reg[th]
                    for fc in range(NF):
                        off = (1 if fc < 4 else 0) + th * 512
                        ROP("tensor", lambda e, fc=fc, jp=jp, wj=wj, pa=pa, off=off: e.matmul(
                            pa[0:wj, :], w1b_sb[:, jp, fc, 0:wj], ln2_sb[:, fc, off:off + 512],
                            start=(fc == 0), stop=(fc == NF - 1)),
                            reads=[b_w1b[jp], b_ln2[0]], writes=[pab])
                    for fc in range(NF):
                        off = (1 if fc < 4 else 0) + th * 512
                        ROP("tensor", lambda e, fc=fc, jp=jp, wj=wj, pg=pg, off=off: e.matmul(
                            pg[0:wj, :], w1b_sb[:, jp, fc, 128:128 + wj], ln2_sb[:, fc, off:off + 512],
                            start=(fc == 0), stop=(fc == NF - 1)),
                            reads=[b_w1b[jp], b_ln2[0]], writes=[pgb])
                    ROP("vector", lambda e, th=th, j=j, pa=pa: e.tensor_scalar(
                        abf_sb[:, th, :], pa, b1a_pp[:, l * NH + j:l * NH + j + 1], None, ALU.add),
                        reads=[pab, b_b1a], writes=[b_abf[th]])
                    ROP("scalar", lambda e, th=th, j=j, pg=pg: e.activation(
                        gbf_sb[:, th, :], pg, AF.Gelu, bias=b1g_pp[:, l * NH + j:l * NH + j + 1]),
                        reads=[pgb, b_b1g], writes=[b_gbf[th]])
                    ROP("vector", lambda e, th=th, j=j: e.tensor_mul(
                        h2_sb[:, j, th * 512:(th + 1) * 512], abf_sb[:, th, :], gbf_sb[:, th, :]),
                        reads=[b_abf[th], b_gbf[th]], writes=[b_h2[0]])

            for of in range(8):
                d("qp", wof_sb[:, 0:21, :],
                  fw2[l, 0:2688, of * 128:(of + 1) * 128].rearrange("(j p) o -> p j o", p=128),
                  [b_wof[0]])
                d("qp", wof_sb[0:42, 21, :], fw2[l, 2688:HID, of * 128:(of + 1) * 128],
                  [b_wof[0]])
                for th in range(2):
                    pa, pab = a_reg[th]
                    for j in range(NH):
                        wj = min(128, HID - 128 * j)
                        ROP("tensor", lambda e, j=j, wj=wj, th=th, pa=pa: e.matmul(
                            pa, wof_sb[0:wj, j, :], h2_sb[0:wj, j, th * 512:(th + 1) * 512],
                            start=(j == 0), stop=(j == NH - 1)),
                            reads=[b_wof[0], b_h2[0]], writes=[pab])
                    ROP("vector", lambda e, th=th, of=of, pa=pa: e.tensor_scalar(
                        w2t_sb[:, th * 512:(th + 1) * 512], pa,
                        b2_pp[:, l * 8 + of:l * 8 + of + 1], None, ALU.add),
                        reads=[pab, b_b2], writes=[b_w2t[0]])
                for tt in range(8):
                    t = 8 * g + tt
                    (tr, trb) = tb_reg[tt % 2]
                    ROP("tensor", lambda e, tt=tt, tr=tr: e.transpose(
                        tr[:, 0:128], w2t_sb[:, tt * 128:(tt + 1) * 128], identf[:]),
                        reads=[b_w2t[0], bidentf], writes=[trb])
                    ROP("vector", lambda e, t=t, of=of, tr=tr: e.tensor_tensor(
                        slot[:, t * D + of * 128:t * D + (of + 1) * 128], tr[:, 0:128],
                        slot[:, t * D + of * 128:t * D + (of + 1) * 128], ALU.add),
                        reads=[trb, bufs[t]], writes=[bufs[t]])

        ar.free(st2, sh2, cm_sb, mr_sb, nrm_sb, ln2_sb, bnd_sb, h2_sb, w1b_sb,
                abf_sb, gbf_sb, wof_sb, w2t_sb)
        self._st2 = None
        pes.close()
        self.psum_phase_end(PB)

    def linattn(self, slotX, bufX, slotO, bufO):
        c = self.ctx
        nc, r, ar = self.nc, self.rec, self.arena
        ROP = r.op
        d = c["d"]
        PT = self.P["PT"]
        pb = self.pb
        ident = c["ident"]; bident = c["bident"][0]
        maskl = c["maskl"]; bmaskl = c["bmaskl"][0]
        onesb = c["onesb"]; bonesb = c["bonesb"][0]
        aw_qkv, aw_out, ab_qkv, ab_out = c["aw_qkv"], c["aw_out"], c["ab_qkv"], c["ab_out"]
        ln_full_tile = c["ln_full_tile"]
        QN = 4

        abrow, b_abrow = ar.alloc("abrow", [1, 3072], BF16)
        aorow, b_aorow = ar.alloc("aorow", [1, 1024], BF16)
        S_sb, b_S = ar.alloc("S", [128, 2, 520], F32, 2)
        Sb_sb, b_Sb = ar.alloc("Sbf", [128, 2, 520], BF16, 2)
        nrc_sb, b_nrc = ar.alloc("nrc", [128, 4], F32, 2)
        qtm_sb, b_qtm = ar.alloc("qtm", [128, QN, 1024], BF16, QN)
        ktm_sb, b_ktm = ar.alloc("ktm", [128, QN, 1024], BF16, QN)
        v1_sb, b_v1 = ar.alloc("v1l", [128, QN, LH * 65], BF16, QN)
        d("qp", abrow[:], ab_qkv.unsqueeze(0), [b_abrow[0]])
        d("qp", aorow[:], ab_out.unsqueeze(0), [b_aorow[0]])
        ROP("vector", lambda e: e.memset(S_sb[:, 1, :], 0.0), writes=[b_S[1]])
        ROP("vector", lambda e: e.memset(Sb_sb[:, 1, :], 0.0), writes=[b_Sb[1]])

        ptbuf = pb["PT"]

        def _quarter(qr):
            tiles = list(range(QN * qr, QN * qr + QN))
            ln1_sb, b_ln1 = ar.alloc("ln1f", [128, NF, QN * 128], BF16)
            lncast_sb, b_lncast = ar.alloc("lncast", [128, 2, D], BF16, 2)
            wst_sb, b_wst = ar.alloc("wst", [128, 1024], BF16)
            wt_sb, b_wt = ar.alloc("wtog", [128, NF, 512], BF16)
            emin_sb, b_emin = ar.alloc("emin", [128, 512], F32)
            eexp_sb, b_eexp = ar.alloc("eexp", [128, 512], F32)
            for s in range(QN):
                ROP("vector", lambda e, s=s: e.memset(
                    v1_sb[:, s, :].rearrange("p (h u) -> p h u", u=65)[:, :, 64:65], 1.0),
                    writes=[b_v1[s]])
            for tt, t in enumerate(tiles):
                ns = tt % 2
                ln_full_tile(slotX[:, t * D:(t + 1) * D], [bufX[t]],
                             lncast_sb[:, ns, :], [b_lncast[ns]], t % 16)
                for fc in range(NF):
                    ROP("tensor", lambda e, fc=fc, ns=ns: e.transpose(
                        PT[:, fc * 128:(fc + 1) * 128],
                        lncast_sb[:, ns, fc * 128:(fc + 1) * 128], ident[:]),
                        reads=[b_lncast[ns], bident], writes=[ptbuf])
                ROP("vector", lambda e, tt=tt: e.tensor_copy(
                    ln1_sb[:, :, tt * 128:(tt + 1) * 128],
                    PT[:].rearrange("p (cc u) -> p cc u", u=128)),
                    reads=[ptbuf], writes=[b_ln1[0]])
            pesq = ExitStack()
            PSq, PBq = self.psum_phase(pesq, {"l0": [128, 512], "l1": [128, 512],
                                              "l2": [128, 512]})
            qkrot = [(PSq["l0"][:], PBq["l0"]), (PSq["l1"][:], PBq["l1"]),
                     (PSq["l2"][:], PBq["l2"])]
            if self.stage in ("lins_dbg", "lins_dbg2"):
                def _mkcb(tag, names):
                    def _cb(sim, tag=tag, names=names):
                        import numpy as np
                        msg = [tag]
                        for nm in names:
                            try:
                                a = np.array(sim.tensor(nm)).astype(np.float32)
                                fin = np.isfinite(a)
                                msg.append(f"{nm}:max={np.abs(a[fin]).max() if fin.any() else -1:.4f},nan={int((~fin).sum())}")
                            except Exception as ex:
                                msg.append(f"{nm}:ERR")
                        print("[CB]", *msg)
                    return _cb
                def _fn0(e):
                    from concourse.bass_interp import add_callback
                    return add_callback(e, _mkcb("post-ln1", [ln1_sb.name]))
                r.op("vector", _fn0, reads=[b_ln1[0]])
            for og in range(6):
                for i in range(4):
                    d("qp", wst_sb[:], aw_qkv[og * 512 + i * 128: og * 512 + (i + 1) * 128, :],
                      [b_wst[0]])
                    for fc in range(NF):
                        ROP("tensor", lambda e, fc=fc, w=wst_sb: e.transpose(
                            PT[:, fc * 128:(fc + 1) * 128], w[:, fc * 128:(fc + 1) * 128],
                            ident[:]), reads=[b_wst[0], bident], writes=[ptbuf])
                    if self.stage in ("lins_dbg", "lins_dbg2", "linq_dbg") and og == 0 and i == 0:
                        def _fnq(e):
                            from concourse.bass_interp import add_callback2
                            def _cbq(sim, inst):
                                import numpy as np
                                from concourse.bass_interp import Direction
                                v = sim.instruction_executor.view_arg(
                                    inst.ins[0], Direction.READ, inst, reg_snapshot=None)
                                a = np.asarray(v).astype(np.float32)
                                fin = np.isfinite(a)
                                print("[CBQ] PT view:", a.shape,
                                      "max", np.abs(a[fin]).max() if fin.any() else -1,
                                      "nan", int((~fin).sum()))
                                try:
                                    sems = {k: v for k, v in sim.semaphores.items()} if hasattr(sim, "semaphores") else None
                                    print("[CBQ] sems:", sems)
                                except Exception as ex:
                                    print("[CBQ] semerr", ex)
                                st = [x for x in dir(sim) if "sem" in x.lower() or "engine" in x.lower()]
                                print("[CBQ] attrs:", st[:12])
                            return add_callback2(e, _cbq, [PT[:]])
                        r.op("vector", _fnq, reads=[ptbuf])
                        _lasttok = ptbuf.w
                        def _fnp(e, _lt=_lasttok):
                            from concourse.bass_interp import add_callback
                            def _cbp(sim, _lt=_lt):
                                import numpy as np
                                for nm2 in ("PsT", "PsT_set", "pp1_l0"):
                                    try:
                                        a = np.array(sim.tensor(nm2)).astype(np.float32)
                                        fin = np.isfinite(a)
                                        print("[CB]", nm2, "shape", a.shape, "max",
                                              np.abs(a[fin]).max() if fin.any() else -1,
                                              "nan", int((~fin).sum()))
                                    except Exception as ex:
                                        print("[CB]", nm2, "ERR", str(ex)[:80])
                            return add_callback(e, _cbp)
                        r.op("vector", _fnp, reads=[ptbuf])
                    ROP("vector", lambda e, i=i: e.tensor_copy(
                        wt_sb[:, :, i * 128:(i + 1) * 128],
                        PT[:].rearrange("p (cc u) -> p cc u", u=128)),
                        reads=[ptbuf], writes=[b_wt[0]])
                for tt in range(QN):
                    ps, psb = qkrot[tt % 3]
                    for fc in range(NF):
                        ROP("tensor", lambda e, fc=fc, tt=tt, ps=ps: e.matmul(
                            ps, ln1_sb[:, fc, tt * 128:(tt + 1) * 128], wt_sb[:, fc, :],
                            start=(fc == 0), stop=False),
                            reads=[b_ln1[0], b_wt[0]], writes=[psb])
                    ROP("tensor", lambda e, og=og, ps=ps: e.matmul(
                        ps, onesb[:], abrow[0:1, og * 512:(og + 1) * 512],
                        start=False, stop=True),
                        reads=[bonesb, b_abrow[0]], writes=[psb])
                    if og < 4:
                        dst = qtm_sb if og < 2 else ktm_sb
                        dbuf = (b_qtm if og < 2 else b_ktm)[tt]
                        dslice = (og % 2) * 512
                        if self.stage == "lins_dbg3" and og == 0 and tt == 0 and qr == 0:
                            psnap, b_psnap = ar.alloc("psnap", [128, 512], F32)
                            self._psnap = psnap
                            ROP("vector", lambda e, ps=ps: e.tensor_copy(psnap[:], ps),
                                reads=[psb], writes=[b_psnap[0]])
                        ROP("vector", lambda e, ps=ps: e.tensor_scalar_min(emin_sb[:], ps, 0.0),
                            reads=[psb], writes=[b_emin[0]])
                        ROP("scalar", lambda e: e.activation(eexp_sb[:], emin_sb[:], AF.Exp),
                            reads=[b_emin[0]], writes=[b_eexp[0]])
                        ROP("vector", lambda e, ps=ps, dst=dst, tt=tt, dslice=dslice: e.scalar_tensor_tensor(
                            dst[:, tt, dslice:dslice + 512], ps, 0.0, eexp_sb[:],
                            ALU.max, ALU.add),
                            reads=[psb, b_eexp[0]], writes=[dbuf])
                    else:
                        hof = 8 * (og - 4)
                        ROP("vector", lambda e, ps=ps, tt=tt, hof=hof: e.tensor_copy(
                            v1_sb[:, tt, hof * 65:(hof + 8) * 65].rearrange(
                                "p (h u) -> p h u", u=65)[:, :, 0:64],
                            ps.rearrange("p (h u) -> p h u", u=64)),
                            reads=[psb], writes=[b_v1[tt]])
                if self.stage in ("lins_dbg", "lins_dbg2"):
                    def _fn1(e, og=og):
                        from concourse.bass_interp import add_callback
                        return add_callback(e, _mkcb(f"post-og{og}",
                            [wt_sb.name, qtm_sb.name, v1_sb.name]))
                    r.op("vector", _fn1, reads=[b_wt[0], b_qtm[0], b_v1[0]])
            if self.stage == "linq_dbg":
                self.done = True
                return
            if self.stage == "lins_dbg":
                def _cb(sim, qtm_t=qtm_sb):
                    import numpy as np
                    a = np.array(sim.tensors.mems.get("ar7_qtm_53", [[0]])) if hasattr(sim, "tensors") else None
                    try:
                        a = np.array(sim.tensor("ar7_qtm_53")).astype(np.float32)
                        print("[CB] post-qkv qtm absmax:", np.abs(a).max())
                    except Exception as ex:
                        print("[CB] err", ex)
                def _fn(e):
                    from concourse.bass_interp import add_callback
                    return add_callback(e, _cb)
                r.op("vector", _fn, reads=[b_qtm[0], b_qtm[1], b_qtm[2], b_qtm[3]])
            ar.free(ln1_sb, lncast_sb, wst_sb, wt_sb, emin_sb, eexp_sb)
            pesq.close()
            self.psum_phase_end(PBq)

            pess = ExitStack()
            PSs, PBs = self.psum_phase(pess, {
                "at0": [128, 512], "at1": [128, 512],
                "dsA": [128, 512], "dsB": [128, 512],
                "avA": [128, 512], "avB": [128, 512]})
            woT_sb, b_woT = ar.alloc("woT", [128, NF, 1024], BF16)
            wos_sb, b_wos = ar.alloc("wst2", [128, 1024], BF16)
            qf_sb, b_qf = ar.alloc("qfl", [128, 2, 1024], BF16, 2)
            kf_sb, b_kf = ar.alloc("kfl", [128, 2, 1024], BF16, 2)
            at_sb, b_at = ar.alloc("atl", [128, 2, 512], BF16, 2)
            otm_sb, b_otm = ar.alloc("otml", [128, 1, 1024], BF16, 1)
            ofm_sb, b_ofm = ar.alloc("ofml", [128, 1, 1024], BF16, 1)
            for i in range(NF):
                d("qp", wos_sb[:], aw_out[i * 128:(i + 1) * 128, :], [b_wos[0]])
                for fc in range(NF):
                    ROP("tensor", lambda e, fc=fc, w=wos_sb: e.transpose(
                        PT[:, fc * 128:(fc + 1) * 128], w[:, fc * 128:(fc + 1) * 128],
                        ident[:]), reads=[b_wos[0], bident], writes=[ptbuf])
                ROP("vector", lambda e, i=i: e.tensor_copy(
                    woT_sb[:, :, i * 128:(i + 1) * 128],
                    PT[:].rearrange("p (cc u) -> p cc u", u=128)),
                    reads=[ptbuf], writes=[b_woT[0]])
            for tt in range(QN):
                tg = QN * qr + tt
                qs_ = tt % 2
                for fc in range(NF):
                    ROP("tensor", lambda e, fc=fc, tt=tt: e.transpose(
                        PT[:, fc * 128:(fc + 1) * 128],
                        qtm_sb[:, tt, fc * 128:(fc + 1) * 128], ident[:]),
                        reads=[b_qtm[tt], bident], writes=[ptbuf])
                ROP("vector", lambda e, qs_=qs_: e.tensor_copy(qf_sb[:, qs_, :], PT[:]),
                    reads=[ptbuf], writes=[b_qf[qs_]])
                for fc in range(NF):
                    ROP("tensor", lambda e, fc=fc, tt=tt: e.transpose(
                        PT[:, fc * 128:(fc + 1) * 128],
                        ktm_sb[:, tt, fc * 128:(fc + 1) * 128], ident[:]),
                        reads=[b_ktm[tt], bident], writes=[ptbuf])
                ROP("vector", lambda e, qs_=qs_: e.tensor_copy(kf_sb[:, qs_, :], PT[:]),
                    reads=[ptbuf], writes=[b_kf[qs_]])
                atrot = [(PSs["at0"][:], PBs["at0"]), (PSs["at1"][:], PBs["at1"])]
                avrot = [(PSs["avA"][:, 0:260], PBs["avA"]), (PSs["avB"][:, 0:260], PBs["avB"])]
                Sold = (tg + 1) % 2

                def grp_at(g):
                    for j in range(4):
                        h = 4 * g + j
                        hp, hc = 64 * (h % 2), h // 2
                        areg, abuf_ = atrot[j % 2]
                        cb = (g % 2) * 256 + (j // 2) * 128
                        ROP("tensor", lambda e, areg=areg, cb=cb, hp=hp, hc=hc, qs_=qs_: e.matmul(
                            areg[:, cb:cb + 128],
                            kf_sb[hp:hp + 64, qs_, hc * 128:(hc + 1) * 128],
                            qf_sb[hp:hp + 64, qs_, hc * 128:(hc + 1) * 128],
                            start=True, stop=True),
                            reads=[b_kf[qs_], b_qf[qs_]], writes=[abuf_])

                def grp_evac(g):
                    gs = g % 2
                    for j in range(4):
                        areg, abuf_ = atrot[j % 2]
                        cb = (g % 2) * 256 + (j // 2) * 128
                        ROP("vector", lambda e, areg=areg, cb=cb, gs=gs, j=j: e.tensor_tensor(
                            at_sb[:, gs, j * 128:(j + 1) * 128],
                            areg[:, cb:cb + 128], maskl[:], ALU.mult),
                            reads=[abuf_, bmaskl], writes=[b_at[gs]])

                def grp_av_ds(g):
                    vreg, vbuf_ = avrot[g % 2]
                    gs = g % 2
                    for j in range(4):
                        h = 4 * g + j
                        hp, hc = 64 * (h % 2), h // 2
                        ROP("tensor", lambda e, vreg=vreg, gs=gs, j=j, tt=tt, h=h: e.matmul(
                            vreg[:, j * 65:j * 65 + 65], at_sb[:, gs, j * 128:(j + 1) * 128],
                            v1_sb[:, tt, h * 65:(h + 1) * 65],
                            start=True, stop=False),
                            reads=[b_at[gs], b_v1[tt]], writes=[vbuf_])
                        ROP("tensor", lambda e, vreg=vreg, j=j, hp=hp, hc=hc, qs_=qs_, So=Sold: e.matmul(
                            vreg[:, j * 65:j * 65 + 65],
                            qf_sb[hp:hp + 64, qs_, hc * 128:(hc + 1) * 128],
                            Sb_sb[hp:hp + 64, So, hc * 65:(hc + 1) * 65],
                            start=False, stop=True),
                            reads=[b_qf[qs_], b_Sb[Sold]], writes=[vbuf_])
                        dreg_b = PBs["dsA"] if hc < 4 else PBs["dsB"]
                        dap = (PSs["dsA"][hp:hp + 64, 65 * hc:65 * hc + 65] if hc < 4 else
                               PSs["dsB"][hp:hp + 64, 65 * (hc - 4):65 * (hc - 4) + 65])
                        ROP("tensor", lambda e, dap=dap, tt=tt, h=h: e.matmul(
                            dap, ktm_sb[:, tt, h * 64:(h + 1) * 64],
                            v1_sb[:, tt, h * 65:(h + 1) * 65],
                            start=True, stop=True),
                            reads=[b_ktm[tt], b_v1[tt]], writes=[dreg_b])

                def grp_norm(g):
                    vreg, vbuf_ = avrot[g % 2]
                    for j in range(4):
                        h = 4 * g + j
                        nq = j % 2
                        ROP("vector", lambda e, vreg=vreg, j=j, nq=nq: e.tensor_scalar(
                            nrc_sb[:, nq * 2:nq * 2 + 1],
                            vreg[:, j * 65 + 64:j * 65 + 65], 1e-6, None, ALU.add),
                            reads=[vbuf_], writes=[b_nrc[nq]])
                        ROP("vector", lambda e, nq=nq: e.reciprocal(
                            nrc_sb[:, nq * 2 + 1:nq * 2 + 2], nrc_sb[:, nq * 2:nq * 2 + 1]),
                            reads=[b_nrc[nq]], writes=[b_nrc[nq]])
                        ROP("scalar", lambda e, vreg=vreg, j=j, h=h, nq=nq: e.activation(
                            otm_sb[:, 0, h * 64:(h + 1) * 64], vreg[:, j * 65:j * 65 + 64],
                            AF.Copy, scale=nrc_sb[:, nq * 2 + 1:nq * 2 + 2]),
                            reads=[vbuf_, b_nrc[nq]], writes=[b_otm[0]])

                for g in range(4):
                    grp_at(g)
                    if g >= 1:
                        grp_av_ds(g - 1)
                    grp_evac(g)
                    if g >= 1:
                        grp_norm(g - 1)
                grp_av_ds(3)
                grp_norm(3)
                Snew = tg % 2
                ROP("vector", lambda e, Snew=Snew, So=Sold, dsA=PSs["dsA"]: e.tensor_tensor(
                    S_sb[:, Snew, 0:260], dsA[:, 0:260], S_sb[:, So, 0:260], ALU.add),
                    reads=[PBs["dsA"], b_S[Sold]], writes=[b_S[Snew]])
                ROP("vector", lambda e, Snew=Snew, So=Sold, dsB=PSs["dsB"]: e.tensor_tensor(
                    S_sb[:, Snew, 260:520], dsB[:, 0:260], S_sb[:, So, 260:520], ALU.add),
                    reads=[PBs["dsB"], b_S[Sold]], writes=[b_S[Snew]])
                ROP("vector", lambda e, Snew=Snew: e.tensor_copy(
                    Sb_sb[:, Snew, 0:260], S_sb[:, Snew, 0:260]),
                    reads=[b_S[Snew]], writes=[b_Sb[Snew]])
                ROP("vector", lambda e, Snew=Snew: e.tensor_copy(
                    Sb_sb[:, Snew, 260:520], S_sb[:, Snew, 260:520]),
                    reads=[b_S[Snew]], writes=[b_Sb[Snew]])
                for fc in range(NF):
                    ROP("tensor", lambda e, fc=fc: e.transpose(
                        PT[:, fc * 128:(fc + 1) * 128],
                        otm_sb[:, 0, fc * 128:(fc + 1) * 128], ident[:]),
                        reads=[b_otm[0], bident], writes=[ptbuf])
                ROP("vector", lambda e: e.tensor_copy(ofm_sb[:, 0, :], PT[:]),
                    reads=[ptbuf], writes=[b_ofm[0]])
                for oh in range(2):
                    oreg, orb = ((PSs["dsA"][:, 0:512], PBs["dsA"]),
                                 (PSs["dsB"][:, 0:512], PBs["dsB"]))[oh]
                    for fc in range(NF):
                        ROP("tensor", lambda e, fc=fc, oh=oh, oreg=oreg: e.matmul(
                            oreg, ofm_sb[:, 0, fc * 128:(fc + 1) * 128],
                            woT_sb[:, fc, oh * 512:(oh + 1) * 512],
                            start=(fc == 0), stop=False),
                            reads=[b_ofm[0], b_woT[0]], writes=[orb])
                    ROP("tensor", lambda e, oh=oh, oreg=oreg: e.matmul(
                        oreg, onesb[:], aorow[0:1, oh * 512:(oh + 1) * 512],
                        start=False, stop=True),
                        reads=[bonesb, b_aorow[0]], writes=[orb])
                    ROP("vector", lambda e, oh=oh, tg=tg, oreg=oreg: e.tensor_tensor(
                        slotO[:, tg * D + oh * 512:tg * D + (oh + 1) * 512], oreg,
                        slotX[:, tg * D + oh * 512:tg * D + (oh + 1) * 512], ALU.add),
                        reads=[orb, bufX[tg]], writes=[bufO[tg]])
            if self.stage in ("lins_dbg", "lins_dbg2", "lins_dbg3"):
                self.done = True
                return
            ar.free(woT_sb, wos_sb, qf_sb, kf_sb, at_sb, otm_sb, ofm_sb)
            pess.close()
            self.psum_phase_end(PBs)

        for qr in range(4):
            _quarter(qr)
            if self.done:
                return

        ar.free(abrow, aorow, S_sb, Sb_sb, nrc_sb, qtm_sb, ktm_sb, v1_sb)


_CACHE = {}

def _get_mk():
    if STAGE not in _CACHE:
        _CACHE[STAGE] = MK(STAGE)
    return _CACHE[STAGE]

def kernel(L0_attn_x, L0_attn_w_qkv, L0_attn_b_qkv, L0_attn_w_out, L0_attn_b_out,
           L0_ffn_wqkv, L0_ffn_q_scale, L0_ffn_k_scale, L0_ffn_wo, L0_ffn_ln_g,
           L0_ffn_ln_b, L0_ffn_w1, L0_ffn_b1, L0_ffn_w2, L0_ffn_b2):
    mk = _get_mk()
    csq, sxq, csk, sxk = rot_tables()
    consts = {
        "c_ident": np.eye(128, dtype=np.float32).astype(BF),
        "c_identf": np.eye(128, dtype=np.float32),
        "c_maskf": mask_t_ffn(),
        "c_maskl": mask_t_lin(),
        "c_csq": csq, "c_sxq": sxq, "c_csk": csk, "c_sxk": sxk,
    }
    shared = {
        "aw_qkv": np.ascontiguousarray(L0_attn_w_qkv, np.float32),
        "ab_qkv": np.ascontiguousarray(L0_attn_b_qkv, np.float32),
        "aw_out": np.ascontiguousarray(L0_attn_w_out, np.float32),
        "ab_out": np.ascontiguousarray(L0_attn_b_out, np.float32),
        "fw_qkv": np.ascontiguousarray(L0_ffn_wqkv, np.float32),
        "fqs": np.ascontiguousarray(L0_ffn_q_scale, np.float32),
        "fks": np.ascontiguousarray(L0_ffn_k_scale, np.float32),
        "fwo": np.ascontiguousarray(L0_ffn_wo, np.float32),
        "flng": np.ascontiguousarray(L0_ffn_ln_g, np.float32),
        "flnb": np.ascontiguousarray(L0_ffn_ln_b, np.float32),
        "fw1": np.ascontiguousarray(L0_ffn_w1, np.float32),
        "fb1": np.ascontiguousarray(L0_ffn_b1, np.float32),
        "fw2": np.ascontiguousarray(L0_ffn_w2, np.float32),
        "fb2": np.ascontiguousarray(L0_ffn_b2, np.float32),
    }
    shared.update(consts)
    x = np.ascontiguousarray(L0_attn_x, np.float32)
    in_maps = []
    for c in range(8):
        m = dict(shared)
        m["x"] = np.ascontiguousarray(x[c])
        in_maps.append(m)
    res = run_bass_kernel_spmd(mk.nc, in_maps, core_ids=list(range(8)))
    if STAGE == "ffn_dbg":
        return res.results[0]
    return np.stack([res.results[c]["y"] for c in range(8)], axis=0)
